# version 02
# baseline (speedup 1.0000x reference)
from contextlib import ExitStack

import numpy as np
import concourse.bass as bass
import concourse.mybir as mybir
from concourse.bass_utils import run_bass_kernel_spmd

F32 = mybir.dt.float32
F32R = mybir.dt.float32r
AF = mybir.ActivationFunctionType
ALU = mybir.AluOpType

D = 2048
DC = 16
TS = 1024
HIDDEN = 8192
HC = 64
NKB = {0: 8, 1: 16}
KBLIST = {0: [0, 1, 2, 3, 8, 9, 10, 11], 1: list(range(16))}
LN_EPS = 1e-5
RMS_EPS = float(np.finfo(np.float32).eps)
FWHT_SCALE = float(1.0 / np.sqrt(HIDDEN))

ENGINES = ("TE", "ACT", "DVE", "POOL", "SP")

STAGES = {"p1": True, "ffn": True, "fwht": True, "w2": True, "attn": True,
          "debug": False}


def ceil32(x):
    return (x + 31) & ~31


class Builder:
    def __init__(self, stages=STAGES):
        self.st = dict(stages)
        nc = bass.Bass("TRN2", target_bir_lowering=False)
        self.nc = nc
        self.ops = {e: [] for e in ENGINES}
        self.cnt = {}
        self.sem = {}
        self.ctx = ExitStack()
        for name in ("mm", "act", "dve", "pool", "sp"):
            self.sem[name] = self.ctx.enter_context(nc.semaphore("s_" + name))
            self.cnt[name] = 0
        self.qsem, self.qcnt, self.qeng = {}, {}, {}
        self._declare_dram()
        self._declare_sbuf()
        self._declare_psum()

    def inc(self, name, n=1):
        self.cnt[name] += n
        return self.cnt[name]

    def emit(self, eng, fn):
        self.ops[eng].append(fn)

    def wait(self, eng, sem, val):
        if val <= 0:
            return
        s = self.sem[sem]
        self.emit(eng, lambda e, s=s, v=val: e.wait_ge(s, v))

    def dma(self, eng, stream, out, in_):
        if stream not in self.qsem:
            self.qsem[stream] = self.ctx.enter_context(self.nc.semaphore("q_" + stream))
            self.qcnt[stream] = 0
            self.qeng[stream] = eng
        assert self.qeng[stream] == eng, (stream, eng)
        self.qcnt[stream] += 16
        v = self.qcnt[stream]
        s = self.qsem[stream]
        self.emit(eng, lambda e, o=out, i=in_, s=s: e.dma_start(out=o, in_=i).then_inc(s, 16))
        return (stream, v)

    def qwait(self, eng, tok):
        if tok is None:
            return
        stream, v = tok
        if v <= 0:
            return
        s = self.qsem[stream]
        self.emit(eng, lambda e, s=s, v=v: e.wait_ge(s, v))

    def qtotal(self, stream):
        return (stream, self.qcnt.get(stream, 0))

    @staticmethod
    def fm(ap2d):
        return ap2d.rearrange("(c p) t -> p c t", p=128)

    @staticmethod
    def fs(sb_ap, t):
        return sb_ap.rearrange("p (c t) -> p c t", t=t)

    def op(self, eng, esem, fn, inc=True):
        if inc:
            v = self.inc(esem)
            s = self.sem[esem]
            self.emit(eng, lambda e, f=fn, s=s: f(e).then_inc(s, 1))
            return v
        self.emit(eng, lambda e, f=fn: f(e))
        return self.cnt[esem]

    def fence(self, tag=""):
        vals = {}
        for e in ENGINES:
            for stream, owner in self.qeng.items():
                if owner == e:
                    self.qwait(e, self.qtotal(stream))
            es = {"TE": "mm", "ACT": "act", "DVE": "dve", "POOL": "pool", "SP": "sp"}[e]
            self.wait(e, es, self.cnt[es])
            v = self.inc(es)
            s = self.sem[es]
            self.emit(e, lambda en, s=s: en.nop().then_inc(s, 1))
            vals[e] = (es, v)
        for e in ENGINES:
            for o in ENGINES:
                if o != e:
                    sname, v = vals[o]
                    self.wait(e, sname, v)

    def _declare_dram(self):
        nc = self.nc
        t = nc.dram_tensor
        self.i_x = t("xT", [D, 2 * TS], F32, kind="ExternalInput").ap()
        self.i_w = {}
        for nm in ("wq", "wk", "wv", "wg", "wp"):
            self.i_w[nm] = t(nm + "T", [D, D], F32, kind="ExternalInput").ap()
        self.i_w["w1"] = t("w1T", [D, HIDDEN], F32, kind="ExternalInput").ap()
        self.i_w["w3"] = t("w3T", [D, HIDDEN], F32, kind="ExternalInput").ap()
        self.i_w["w2"] = t("w2T", [HIDDEN, D], F32, kind="ExternalInput").ap()
        self.i_masks = t("masks", [24, 128, 512], F32, kind="ExternalInput").ap()
        self.i_h128 = t("had128", [128, 128], F32, kind="ExternalInput").ap()

        kind = "ExternalOutput" if self.st["debug"] else "Internal"
        if self.st["debug"]:
            self.d_x2r = t("x2r_d", [D, TS], F32R, kind=kind).ap()
            self.d_x2e = t("x2e_d", [D, TS], F32R, kind=kind).ap()
            self.d_y1dbg = t("y1dbg_d", [128, HC * 256], F32, kind=kind).ap()
            self.d_ybf = t("ybf_d", [128, HC * 256], F32, kind=kind).ap()
        self.d_kr = t("kr_d", [D, 2 * TS], F32R, kind=kind).ap()
        self.d_ke = t("ke_d", [D, 2 * TS], F32R, kind=kind).ap()
        self.d_vr = t("vr_d", [2 * TS, D], F32R, kind=kind).ap()
        self.d_ve = t("ve_d", [2 * TS, D], F32R, kind=kind).ap()
        self.d_qr = t("qr_d", [D, TS], F32R, kind=kind).ap()
        self.d_qe = t("qe_d", [D, TS], F32R, kind=kind).ap()
        self.d_g = t("g_d", [D, TS], F32, kind=kind).ap()
        self.d_xn = t("xn_d", [D, TS], F32, kind=kind).ap()
        self.d_hr = t("hr_d", [HIDDEN, TS], F32R, kind=kind).ap()
        self.d_he = t("he_d", [HIDDEN, TS], F32R, kind=kind).ap()
        self.d_yr = t("yr_d", [HIDDEN, TS], F32R, kind=kind).ap()
        self.d_ye = t("ye_d", [HIDDEN, TS], F32R, kind=kind).ap()
        self.d_g2 = t("g2_d", [D, TS], F32, kind=kind).ap()
        self.o_out = t("outT", [D, TS], F32, kind="ExternalOutput").ap()

    def _declare_sbuf(self):
        nc = self.nc
        at = nc.alloc_sbuf_tensor_at
        base0 = nc.sbuf_base
        M = ceil32(base0)
        self.M = M
        self.A = M + 18432
        self.B = M + 38912
        self.C = M + 53248
        self.BIG = M + 77824
        end = self.BIG + 131072
        nc.alloc_sbuf_tensor("manual_map_reserved",
                             [128, end - base0], mybir.dt.uint8)
        self.ones_col = at("ones_col", [128, 1], F32, offset=M).ap()
        self.ones_row = at("ones_row", [1, 128], F32, offset=M + 32).ap()
        self.h128r = at("h128r", [128, 128], F32R, offset=M + 544).ap()
        self.h128f = at("h128f", [128, 128], F32, offset=M + 1056).ap()
        self.stat = [at(f"stat{i}", [1, 1024], F32, offset=M + 1600 + 4096 * i).ap()
                     for i in range(4)]
        self._tens = {}

    def sb(self, name, shape, dtype, off):
        key = (name, tuple(shape), dtype, off)
        if key not in self._tens:
            self._tens[key] = self.nc.alloc_sbuf_tensor_at(
                f"{name}_{off}", shape, dtype, offset=off).ap()
        return self._tens[key]

    def _declare_psum(self):
        self.P = [self.ctx.enter_context(
            self.nc.psum_tensor(f"pb{i}", [128, 512], F32)) for i in range(8)]

    def ph0(self):
        self.op("DVE", "dve", lambda e: e.memset(self.ones_col[:, :], 1.0), inc=False)
        self.op("DVE", "dve", lambda e: e.memset(self.ones_row[:, :], 1.0))
        v = self.dma("POOL", "ini", self.h128f[:, :], self.i_h128[:, :])
        self.qwait("POOL", v)
        vc = self.dma("POOL", "ini", self.h128r[:, :], self.h128f[:, :])
        self.qwait("TE", vc)
        self.wait("TE", "dve", self.cnt["dve"])

    def ph1_slab(self, s):
        c0 = s * TS
        own = (s == 0)
        BIG = self.BIG
        xn = self.sb("xn", [128, DC * TS], F32, BIG)
        xn2r = self.sb("xn2r", [128, DC * TS], F32R, BIG + 65536)
        xn2e = self.sb("xn2e", [128, DC * TS], F32R, BIG)
        sq = [self.sb("sqb", [128, 1024], F32, self.B + 4096 * i) for i in range(2)]
        bnc = [self.sb("lnb", [128, 1024], F32, self.B + 8192 + 4096 * i) for i in range(2)]

        src = self.fm(self.i_x[:, c0:c0 + TS])
        vx = self.dma("ACT", "xs", self.fs(xn[:, :], TS), src)

        self.qwait("TE", vx)
        self.qwait("ACT", vx)
        mm_done = []
        sq_done = [0, 0]
        for dc in range(DC):
            xc = xn[:, dc * TS:(dc + 1) * TS]
            sqb = sq[dc % 2]
            if dc >= 2:
                self.wait("ACT", "mm", sq_done[dc % 2])
            va = self.op("ACT", "act", lambda e, i=xc, o=sqb: e.activation(
                out=o[:, :], in_=i, func=AF.Square))
            for t in range(2):
                last = (dc == DC - 1)
                self.op("TE", "mm", lambda e, i=xc, t=t, dc=dc, last=last: e.matmul(
                    self.P[t][0:1, :], self.ones_col[:, :], i[:, t * 512:(t + 1) * 512],
                    start=(dc == 0), stop=last), inc=last)
            self.wait("TE", "act", va)
            for t in range(2):
                last = (dc == DC - 1)
                v = self.op("TE", "mm", lambda e, i=sqb, t=t, dc=dc, last=last: e.matmul(
                    self.P[2 + t][0:1, :], self.ones_col[:, :], i[:, t * 512:(t + 1) * 512],
                    start=(dc == 0), stop=last), inc=True)
                sq_done[dc % 2] = v
        v_stats = self.cnt["mm"]

        st0, st1, st2, st3 = self.stat
        self.wait("DVE", "mm", v_stats)

        def R(fn):
            self.wait("DVE", "dve", self.cnt["dve"])
            return self.op("DVE", "dve", fn, inc=True)
        for t in range(2):
            R(lambda e, t=t: e.tensor_scalar_mul(st0[:, t * 512:(t + 1) * 512],
                                                 self.P[t][0:1, :], 1.0 / D))
            R(lambda e, t=t: e.tensor_scalar_mul(st1[:, t * 512:(t + 1) * 512],
                                                 self.P[2 + t][0:1, :], 1.0 / D))
        R(lambda e: e.tensor_tensor(st2[:, :], st0[:, :], st0[:, :], ALU.mult))
        R(lambda e: e.tensor_sub(st1[:, :], st1[:, :], st2[:, :]))
        self._rsqrt_row(st1, st2, st3, LN_EPS)
        self.wait("DVE", "dve", self.cnt["dve"])
        v_rows = self.op("DVE", "dve", lambda e: e.tensor_tensor(
            st3[:, :], st0[:, :], st2[:, :], ALU.mult))

        self.wait("TE", "dve", v_rows)
        for t in range(2):
            self.op("TE", "mm", lambda e, t=t: e.matmul(
                self.P[4 + t][:, :], self.ones_row[:, :], st2[:, t * 512:(t + 1) * 512],
                start=True, stop=True), inc=False)
            self.op("TE", "mm", lambda e, t=t: e.matmul(
                self.P[6 + t][:, :], self.ones_row[:, :], st3[:, t * 512:(t + 1) * 512],
                start=True, stop=True), inc=(t == 1))
        v_bcast = self.cnt["mm"]

        self.wait("DVE", "mm", v_bcast)
        xn_chunk_done = []
        for dc in range(DC):
            xc = xn[:, dc * TS:(dc + 1) * TS]
            b = bnc[dc % 2]
            for t in range(2):
                R(lambda e, i=xc, b=b, t=t: e.tensor_tensor(
                    b[:, t * 512:(t + 1) * 512], i[:, t * 512:(t + 1) * 512],
                    self.P[4 + t][:, :], ALU.mult))
            self.wait("DVE", "dve", self.cnt["dve"])
            v = self.op("DVE", "dve", lambda e, i=xc, b=b: e.tensor_tensor(
                i[:, 0:512], b[:, 0:512], self.P[6][:, :], ALU.subtract), inc=True)
            self.wait("DVE", "dve", self.cnt["dve"])
            v = self.op("DVE", "dve", lambda e, i=xc, b=b: e.tensor_tensor(
                i[:, 512:1024], b[:, 512:1024], self.P[7][:, :], ALU.subtract))
            xn_chunk_done.append(v)
        if own:
            self.wait("POOL", "dve", xn_chunk_done[-1])
            self.v_xn_dma = self.dma("POOL", "xnst", self.fm(self.d_xn[:, :]),
                                     self.fs(xn[:, :], TS))

        sq_done = [0, 0]
        for dc in range(DC):
            xc = xn[:, dc * TS:(dc + 1) * TS]
            sqb = sq[dc % 2]
            self.wait("ACT", "dve", xn_chunk_done[dc])
            if dc >= 2:
                self.wait("ACT", "mm", sq_done[dc % 2])
            va = self.op("ACT", "act", lambda e, i=xc, o=sqb: e.activation(
                out=o[:, :], in_=i, func=AF.Square))
            self.wait("TE", "act", va)
            for t in range(2):
                last = (dc == DC - 1)
                v = self.op("TE", "mm", lambda e, i=sqb, t=t, dc=dc, last=last: e.matmul(
                    self.P[t][0:1, :], self.ones_col[:, :], i[:, t * 512:(t + 1) * 512],
                    start=(dc == 0), stop=last), inc=True)
                sq_done[dc % 2] = v
        v_stats2 = self.cnt["mm"]
        self.wait("DVE", "mm", v_stats2)
        for t in range(2):
            R(lambda e, t=t: e.tensor_scalar_mul(st0[:, t * 512:(t + 1) * 512],
                                                 self.P[t][0:1, :], 1.0 / D))
        self._rsqrt_row(st0, st2, st3, RMS_EPS)
        v_rows2 = self.cnt["dve"]
        self.wait("TE", "dve", v_rows2)
        for t in range(2):
            self.op("TE", "mm", lambda e, t=t: e.matmul(
                self.P[4 + t][:, :], self.ones_row[:, :], st2[:, t * 512:(t + 1) * 512],
                start=True, stop=True), inc=(t == 1))
        v_bc2 = self.cnt["mm"]

        self.wait("DVE", "mm", v_bc2)
        for dc in range(DC):
            xc = xn[:, dc * TS:(dc + 1) * TS]
            b = bnc[dc % 2]
            if dc >= 2:
                self.qwait("DVE", self._xn2cast[dc % 2])
            for t in range(2):
                R(lambda e, i=xc, b=b, t=t: e.tensor_tensor(
                    b[:, t * 512:(t + 1) * 512], i[:, t * 512:(t + 1) * 512],
                    self.P[4 + t][:, :], ALU.mult))
            vb = self.cnt["dve"]
            self.wait("POOL", "dve", vb)
            if own and dc == 0:
                pass
            vc = self.dma("POOL", "xc", xn2r[:, dc * TS:(dc + 1) * TS], b[:, :])
            if not hasattr(self, "_xn2cast"):
                self._xn2cast = [None, None]
            self._xn2cast[dc % 2] = vc
            self.qwait("DVE", vc)
            if own:
                self.qwait("DVE", self.v_xn_dma)
            self.op("DVE", "dve", lambda e, b=b, dc=dc: e.tensor_sub(
                xn2e[:, dc * TS:(dc + 1) * TS], b[:, :],
                xn2r[:, dc * TS:(dc + 1) * TS].bitcast(F32)))
        v_split = self.cnt["dve"]

        if self.st["debug"] and own:
            self.wait("POOL", "dve", v_split)
            self.dma("POOL", "dbg", self.fm(self.d_x2r[:, :]), self.fs(xn2r[:, :], TS))
            self.dma("POOL", "dbg", self.fm(self.d_x2e[:, :]), self.fs(xn2e[:, :], TS))

        self.fence()
        self.proj_featmaj("wk", self.i_w["wk"], AF.Silu, self.d_kr, self.d_ke,
                          xn2r, xn2e, out_cols=(c0, TS), v_ready=v_split)
        self.fence()
        self.proj_tokmaj(xn2r, xn2e, slab_off=c0, v_ready=v_split)
        if own:
            self.fence()
            self.proj_featmaj("wq", self.i_w["wq"], AF.Silu, self.d_qr, self.d_qe,
                              xn2r, xn2e, out_cols=(0, TS), v_ready=v_split)
            self.fence()
            self.proj_featmaj("wg", self.i_w["wg"], AF.Sigmoid, self.d_g, None,
                              xn2r, xn2e, out_cols=(0, TS), v_ready=v_split)

    def _rsqrt_row(self, vrow, out, tmp, eps):
        def R(fn):
            self.wait("DVE", "dve", self.cnt["dve"])
            return self.op("DVE", "dve", fn, inc=True)

        def A(fn):
            self.wait("ACT", "act", self.cnt["act"])
            return self.op("ACT", "act", fn, inc=True)
        R(lambda e: e.tensor_scalar_add(vrow[:, :], vrow[:, :], eps))
        self.wait("ACT", "dve", self.cnt["dve"])
        va = self.op("ACT", "act", lambda e: e.activation(out=tmp[:, :], in_=vrow[:, :],
                                                          func=AF.Sqrt))
        self.wait("DVE", "act", va)
        R(lambda e: e.reciprocal(out[:, :], tmp[:, :]))
        for i in range(2):
            R(lambda e: e.tensor_tensor(tmp[:, :], out[:, :], out[:, :], ALU.mult))
            R(lambda e: e.tensor_tensor(tmp[:, :], tmp[:, :], vrow[:, :], ALU.mult))
            R(lambda e: e.tensor_scalar(tmp[:, :], tmp[:, :], -0.5, 1.5,
                                        ALU.mult, ALU.add))
            self.wait("DVE", "dve", self.cnt["dve"])
            self.op("DVE", "dve", lambda e: e.tensor_tensor(
                out[:, :], out[:, :], tmp[:, :], ALU.mult), inc=True)

    def proj_featmaj(self, nm, w, act, dr, de, xr, xe, out_cols, v_ready,
                    n_out=D):
        c0, TS_ = out_cols
        wtmp = [self.sb(nm + "wt", [128, 256], F32, self.A + 1024 * i) for i in range(3)]
        wr = [self.sb(nm + "wr", [128, 256], F32R, self.A + 3072 + 1024 * i) for i in range(3)]
        we = [self.sb(nm + "we", [128, 256], F32R, self.A + 6144 + 1024 * i) for i in range(3)]
        eb = [self.sb(nm + "eb", [128, 512], F32, self.B + 2048 * i) for i in range(2)]
        rb = [self.sb(nm + "rb", [128, 512], F32R, self.B + 4096 + 2048 * i) for i in range(2)]
        n_u = n_out // 256
        self.wait("TE", "dve", v_ready)
        mmrec = {}
        evrec = [None, None]
        evguard = {}
        it = 0
        for u in range(n_u):
            half = u % 2
            for dc in range(DC):
                r = it % 3
                if it >= 3:
                    self.wait("SP", "mm", mmrec[(it - 3)])
                else:
                    self.wait("SP", "mm", self.cnt["mm"])
                vload = self.dma("SP", f"w{r}", wtmp[r][:, :],
                                 w[dc * 128:(dc + 1) * 128, u * 256:(u + 1) * 256])
                self.qwait("POOL", vload)
                if it >= 3:
                    self.wait("POOL", "mm", mmrec[(it - 3)])
                vcast = self.dma("POOL", f"wc{r}", wr[r][:, :], wtmp[r][:, :])
                self.qwait("DVE", vcast)
                vsub = self.op("DVE", "dve", lambda e, r=r: e.tensor_sub(
                    we[r][:, :], wtmp[r][:, :], wr[r][:, :].bitcast(F32)))
                self.wait("TE", "dve", vsub)
                if u >= 2 and dc == 0:
                    self.wait("TE", "act", evguard[u - 2])
                for oc in range(2):
                    for t in range(2):
                        pb = self.P[half * 4 + oc * 2 + t]
                        mv_r = xr[:, dc * TS_ + t * 512: dc * TS_ + t * 512 + 512]
                        mv_e = xe[:, dc * TS_ + t * 512: dc * TS_ + t * 512 + 512]
                        stt = (dc == 0)
                        stp = (dc == DC - 1)
                        lr = wr[r][:, oc * 128:(oc + 1) * 128]
                        le = we[r][:, oc * 128:(oc + 1) * 128]
                        self.op("TE", "mm", lambda e, pb=pb, lr=lr, mv=mv_r, s=stt: e.matmul(
                            pb[:, :], lr, mv, start=s, stop=False), inc=False)
                        self.op("TE", "mm", lambda e, pb=pb, lr=lr, mv=mv_e: e.matmul(
                            pb[:, :], lr, mv, start=False, stop=False), inc=False)
                        last = oc == 1 and t == 1
                        self.op("TE", "mm", lambda e, pb=pb, le=le, mv=mv_r, p=stp: e.matmul(
                            pb[:, :], le, mv, start=False, stop=p), inc=last)
                mmrec[it] = self.cnt["mm"]
                it += 1
            v_grp = self.cnt["mm"]
            self.wait("ACT", "mm", v_grp)
            for oc in range(2):
                for t in range(2):
                    pb = self.P[half * 4 + oc * 2 + t]
                    par = (oc * 2 + t) % 2
                    b = eb[par]
                    rbb = rb[par]
                    if u * 4 + oc * 2 + t >= 2:
                        self.qwait("ACT", evrec[par])
                    va = self.op("ACT", "act", lambda e, pb=pb, b=b, f=act: e.activation(
                        out=b[:, :], in_=pb[:, :], func=f))
                    row0 = (u * 2 + oc) * 128
                    dst_cols = (c0 + t * 512, c0 + t * 512 + 512)
                    self.wait("POOL", "act", va)
                    if de is None:
                        vo = self.dma("POOL", f"ps{par}",
                                      dr[row0:row0 + 128, dst_cols[0]:dst_cols[1]], b[:, :])
                        evrec[par] = vo
                    else:
                        vc = self.dma("POOL", f"pc{par}", rbb[:, :], b[:, :])
                        self.qwait("DVE", vc)
                        vs = self.op("DVE", "dve", lambda e, b=b, rbb=rbb: e.tensor_sub(
                            b[:, :], b[:, :], rbb[:, :].bitcast(F32)))
                        self.wait("POOL", "dve", vs)
                        self.dma("POOL", f"ps{par}",
                                 dr[row0:row0 + 128, dst_cols[0]:dst_cols[1]], rbb[:, :])
                        vo = self.dma("POOL", f"ps{par}",
                                      de[row0:row0 + 128, dst_cols[0]:dst_cols[1]],
                                      b[:, :].bitcast(F32R))
                        evrec[par] = vo
            evguard[u] = self.cnt["act"]

    def proj_tokmaj(self, xr, xe, slab_off, v_ready):
        nm = "wv"
        w = self.i_w["wv"]
        wtmp = [self.sb(nm + "wt", [128, 512], F32, self.A + 2048 * i) for i in range(3)]
        wr = [self.sb(nm + "wr", [128, 512], F32R, self.A + 6144 + 2048 * i) for i in range(3)]
        we = [self.sb(nm + "we", [128, 512], F32R, self.A + 12288 + 2048 * i) for i in range(3)]
        eb = [self.sb(nm + "eb", [128, 512], F32, self.B + 2048 * i) for i in range(2)]
        rb = [self.sb(nm + "rb", [128, 512], F32R, self.B + 4096 + 2048 * i) for i in range(2)]
        self.wait("TE", "dve", v_ready)
        it = 0
        mmrec = {}
        evrec = [None, None]
        evac_v = 0
        for dt in range(4):
            for dc in range(DC):
                r = it % 3
                if it >= 3:
                    self.wait("SP", "mm", mmrec[it - 3])
                else:
                    self.wait("SP", "mm", self.cnt["mm"])
                vload = self.dma("SP", f"w{r}", wtmp[r][:, :],
                                 w[dc * 128:(dc + 1) * 128, dt * 512:(dt + 1) * 512])
                self.qwait("POOL", vload)
                if it >= 3:
                    self.wait("POOL", "mm", mmrec[it - 3])
                vcast = self.dma("POOL", f"wc{r}", wr[r][:, :], wtmp[r][:, :])
                self.qwait("DVE", vcast)
                vsub = self.op("DVE", "dve", lambda e, r=r: e.tensor_sub(
                    we[r][:, :], wtmp[r][:, :], wr[r][:, :].bitcast(F32)))
                self.wait("TE", "dve", vsub)
                if dt > 0 and dc == 0:
                    self.wait("TE", "act", evac_v)
                for tb in range(8):
                    pb = self.P[tb]
                    st_r = xr[:, dc * TS + tb * 128: dc * TS + tb * 128 + 128]
                    st_e = xe[:, dc * TS + tb * 128: dc * TS + tb * 128 + 128]
                    stt = (dc == 0)
                    stp = (dc == DC - 1)
                    last = tb == 7
                    self.op("TE", "mm", lambda e, pb=pb, l=st_r, m=wr[r], s=stt: e.matmul(
                        pb[:, :], l, m[:, :], start=s, stop=False), inc=False)
                    self.op("TE", "mm", lambda e, pb=pb, l=st_r, m=we[r]: e.matmul(
                        pb[:, :], l, m[:, :], start=False, stop=False), inc=False)
                    self.op("TE", "mm", lambda e, pb=pb, l=st_e, m=wr[r], p=stp: e.matmul(
                        pb[:, :], l, m[:, :], start=False, stop=p), inc=last)
                mmrec[it] = self.cnt["mm"]
                it += 1
            v_grp = self.cnt["mm"]
            self.wait("ACT", "mm", v_grp)
            for tb in range(8):
                pb = self.P[tb]
                b = eb[tb % 2]
                rbb = rb[tb % 2]
                if dt * 8 + tb >= 2:
                    self.qwait("ACT", evrec[tb % 2])
                va = self.op("ACT", "act", lambda e, pb=pb, b=b: e.activation(
                    out=b[:, :], in_=pb[:, :], func=AF.Silu))
                self.wait("POOL", "act", va)
                vc = self.dma("POOL", f"pc{tb % 2}", rbb[:, :], b[:, :])
                self.qwait("DVE", vc)
                vs = self.op("DVE", "dve", lambda e, b=b, rbb=rbb: e.tensor_sub(
                    b[:, :], b[:, :], rbb[:, :].bitcast(F32)))
                row0 = slab_off + tb * 128
                self.wait("POOL", "dve", vs)
                self.dma("POOL", f"ps{tb % 2}",
                         self.d_vr[row0:row0 + 128, dt * 512:(dt + 1) * 512], rbb[:, :])
                vo = self.dma("POOL", f"ps{tb % 2}",
                              self.d_ve[row0:row0 + 128, dt * 512:(dt + 1) * 512],
                              b[:, :].bitcast(F32R))
                evrec[tb % 2] = vo
            evac_v = self.cnt["act"]

    def ph_ffn(self):
        BIG = self.BIG
        xn = self.sb("fxn", [128, DC * TS], F32, BIG)
        xnr = self.sb("fxnr", [128, DC * TS], F32R, BIG + 65536)
        xne = self.sb("fxne", [128, DC * TS], F32R, BIG)

        src = self.fm(self.d_xn[:, :])
        vx = self.dma("POOL", "xnst", self.fs(xn[:, :], TS), src)
        self.qwait("POOL", vx)
        cast_v = []
        for dc in range(DC):
            vc = self.dma("POOL", "fc", xnr[:, dc * TS:(dc + 1) * TS],
                          xn[:, dc * TS:(dc + 1) * TS])
            cast_v.append(vc)
        self.qwait("DVE", cast_v[-1])
        for dc in range(DC):
            self.op("DVE", "dve", lambda e, dc=dc: e.tensor_sub(
                xne[:, dc * TS:(dc + 1) * TS], xn[:, dc * TS:(dc + 1) * TS],
                xnr[:, dc * TS:(dc + 1) * TS].bitcast(F32)))
        v_split = self.cnt["dve"]

        w1, w3 = self.i_w["w1"], self.i_w["w3"]
        wt1 = [self.sb("f1t", [128, 256], F32, self.A + 1024 * i) for i in range(3)]
        wr1 = [self.sb("f1r", [128, 256], F32R, self.A + 3072 + 1024 * i) for i in range(3)]
        we1 = [self.sb("f1e", [128, 256], F32R, self.A + 6144 + 1024 * i) for i in range(3)]
        ug = self.sb("fug", [128, 4 * 512], F32, self.C)
        hb = [self.sb("fhb", [128, 512], F32, self.B + 2048 * i) for i in range(2)]
        rb = [self.sb("fhr", [128, 512], F32R, self.B + 4096 + 2048 * i) for i in range(2)]
        self.wait("TE", "dve", v_split)
        it = 0
        mmrec = {}
        h_ev = [None, None]
        ug_free = 0
        ug_done = 0
        w1_bank_free = 0
        w3_bank_free = 0
        for hu in range(32):
            for wi, w in ((0, w1), (1, w3)):
                for dc in range(DC):
                    r = it % 3
                    if it >= 3:
                        self.wait("SP", "mm", mmrec[it - 3])
                    else:
                        self.wait("SP", "mm", self.cnt["mm"])
                    vload = self.dma("SP", f"w{r}", wt1[r][:, :],
                                     w[dc * 128:(dc + 1) * 128, hu * 256:(hu + 1) * 256])
                    self.qwait("POOL", vload)
                    if it >= 3:
                        self.wait("POOL", "mm", mmrec[it - 3])
                    vcast = self.dma("POOL", f"wc{r}", wr1[r][:, :], wt1[r][:, :])
                    self.qwait("DVE", vcast)
                    vsub = self.op("DVE", "dve", lambda e, r=r: e.tensor_sub(
                        we1[r][:, :], wt1[r][:, :], wr1[r][:, :].bitcast(F32)))
                    self.wait("TE", "dve", vsub)
                    if dc == 0 and hu > 0:
                        if wi == 0:
                            self.wait("TE", "act", w1_bank_free)
                        else:
                            self.wait("TE", "dve", w3_bank_free)
                    for hc in range(2):
                        for t in range(2):
                            pb = self.P[wi * 4 + hc * 2 + t]
                            mv_r = xnr[:, dc * TS + t * 512: dc * TS + t * 512 + 512]
                            mv_e = xne[:, dc * TS + t * 512: dc * TS + t * 512 + 512]
                            lr = wr1[r][:, hc * 128:(hc + 1) * 128]
                            le = we1[r][:, hc * 128:(hc + 1) * 128]
                            stt = (dc == 0)
                            stp = (dc == DC - 1)
                            last = hc == 1 and t == 1
                            self.op("TE", "mm", lambda e, pb=pb, l=lr, m=mv_r, s=stt: e.matmul(
                                pb[:, :], l, m, start=s, stop=False), inc=False)
                            self.op("TE", "mm", lambda e, pb=pb, l=lr, m=mv_e: e.matmul(
                                pb[:, :], l, m, start=False, stop=False), inc=False)
                            self.op("TE", "mm", lambda e, pb=pb, l=le, m=mv_r, p=stp: e.matmul(
                                pb[:, :], l, m, start=False, stop=p), inc=last)
                    mmrec[it] = self.cnt["mm"]
                    it += 1
                v_grp = self.cnt["mm"]
                if wi == 0:
                    self.wait("ACT", "mm", v_grp)
                    if hu > 0:
                        self.wait("ACT", "dve", ug_free)
                    for j in range(4):
                        self.op("ACT", "act", lambda e, pb=self.P[j], j=j: e.activation(
                            out=ug[:, j * 512:(j + 1) * 512], in_=pb[:, :], func=AF.Silu),
                            inc=(j == 3))
                    ug_done = self.cnt["act"]
                    w1_bank_free = ug_done
                else:
                    self.wait("DVE", "mm", v_grp)
                    self.wait("DVE", "act", ug_done)
                    for j in range(4):
                        hc, t = divmod(j, 2)
                        b = hb[j % 2]
                        rbb = rb[j % 2]
                        if hu * 4 + j >= 2:
                            self.qwait("DVE", h_ev[j % 2])
                        vm = self.op("DVE", "dve", lambda e, pb=self.P[4 + j], j=j, b=b: e.tensor_tensor(
                            b[:, :], ug[:, j * 512:(j + 1) * 512], pb[:, :], ALU.mult),
                            inc=True)
                        self.wait("POOL", "dve", vm)
                        vc = self.dma("POOL", f"pc{j % 2}", rbb[:, :], b[:, :])
                        self.qwait("DVE", vc)
                        vs = self.op("DVE", "dve", lambda e, b=b, rbb=rbb: e.tensor_sub(
                            b[:, :], b[:, :], rbb[:, :].bitcast(F32)))
                        row0 = (hu * 2 + hc) * 128
                        self.wait("POOL", "dve", vs)
                        self.dma("POOL", f"ps{j % 2}",
                                 self.d_hr[row0:row0 + 128, t * 512:(t + 1) * 512], rbb[:, :])
                        vo = self.dma("POOL", f"ps{j % 2}",
                                      self.d_he[row0:row0 + 128, t * 512:(t + 1) * 512],
                                      b[:, :].bitcast(F32R))
                        h_ev[j % 2] = vo
                    ug_free = self.cnt["dve"]
                    w3_bank_free = self.cnt["dve"]

    def ph_fwht(self):
        TT = 256
        for tt in range(4):
            self._fwht_tile(tt, TT)

    def _fwht_tile(self, tt, TT):
        BIG = self.BIG
        a = self.sb("wa", [128, HC * TT], F32, BIG)
        bq = self.sb("wb", [128, HC * TT], F32, BIG + 65536)
        hcr = [self.sb("whr", [128, TT], F32R, self.A + 1024 * i) for i in range(4)]
        hce = [self.sb("whe", [128, TT], F32R, self.A + 4096 + 1024 * i) for i in range(4)]
        rbnc = self.sb("wrb", [128, 8 * TT], F32R, self.C)
        ebnc = self.sb("web", [128, 8 * TT], F32, self.C + 8192)

        cols = (tt * TT, (tt + 1) * TT)
        self.qwait("ACT", self.qtotal("ys"))
        self.qwait("ACT", self.qtotal("dbg"))
        self.wait("TE", "act", self.cnt["act"])
        ev = {}
        ld = {}
        for c in range(HC):
            r = c % 4
            if c >= 4:
                self.wait("SP", "mm", ld[c - 4])
            else:
                self.wait("SP", "mm", self.cnt["mm"])
            rows = slice(c * 128, (c + 1) * 128)
            v1 = self.dma("SP", f"f{r}", hcr[r][:, :],
                          self.d_hr[rows, cols[0]:cols[1]])
            v2 = self.dma("SP", f"f{r}", hce[r][:, :],
                          self.d_he[rows, cols[0]:cols[1]])
            pb = self.P[c % 8]
            if c >= 8:
                self.wait("TE", "act", ev[c - 8])
            self.qwait("TE", v2)
            self.op("TE", "mm", lambda e, pb=pb, r=r: e.matmul(
                pb[:, 0:TT], self.h128r[:, :], hcr[r][:, :],
                start=True, stop=False), inc=False)
            v = self.op("TE", "mm", lambda e, pb=pb, r=r: e.matmul(
                pb[:, 0:TT], self.h128r[:, :], hce[r][:, :],
                start=False, stop=True))
            ld[c] = self.cnt["mm"]
            self.wait("ACT", "mm", v)
            ev[c] = self.op("ACT", "act", lambda e, pb=pb, c=c: e.activation(
                out=a[:, c * TT:(c + 1) * TT], in_=pb[:, 0:TT], func=AF.Copy,
                scale=FWHT_SCALE))
        v_y = self.cnt["act"]
        if self.st["debug"] and tt == 0:
            self.wait("POOL", "act", v_y)
            vdump = self.dma("POOL", "dbg", self.d_y1dbg[:, :], a[:, :])
            self.qwait("POOL", vdump)
            self.qwait("DVE", vdump)
        self.wait("DVE", "act", v_y)
        self.wait("POOL", "act", v_y)
        src, dst = a, bq
        for lev in range(6):
            st = 1 << lev
            npair = HC // (2 * st)
            def mk(sgn, half, src=src, dst=dst, st=st, npair=npair):
                def f(e):
                    o = dst[:, :].rearrange("p (g two s) -> p g two s", two=2,
                                            s=st * TT)
                    i0 = src[:, :].rearrange("p (g two s) -> p g two s", two=2,
                                             s=st * TT)
                    out = o[:, :, half, :]
                    return e.tensor_tensor(
                        out, i0[:, :, 0, :], i0[:, :, 1, :],
                        ALU.add if sgn > 0 else ALU.subtract)
                return f
            self.wait("DVE", "dve", self.cnt["dve"])
            self.wait("POOL", "pool", self.cnt["pool"])
            v0 = self.op("DVE", "dve", mk(+1, 0))
            v1 = self.op("POOL", "pool", mk(-1, 1))
            self.wait("DVE", "pool", v1)
            self.wait("POOL", "dve", v0)
            src, dst = dst, src
        v_res_dve = self.cnt["dve"]
        v_res_pool = self.cnt["pool"]
        self.wait("POOL", "dve", v_res_dve)
        self.wait("POOL", "pool", v_res_pool)
        if self.st["debug"] and tt == 0:
            vd = self.dma("POOL", "dbg", self.d_ybf[:, :], src[:, :])
            self.qwait("POOL", vd)
        prev = None
        for g in range(8):
            gs = slice(g * 8 * TT, (g + 1) * 8 * TT)
            if prev is not None:
                self.qwait("POOL", prev)
            vc = self.dma("POOL", "ys", rbnc[:, :], src[:, gs])
            self.qwait("DVE", vc)
            self.wait("DVE", "pool", v_res_pool)
            if prev is not None:
                self.qwait("DVE", prev)
            vs = self.op("DVE", "dve", lambda e, gs=gs: e.tensor_sub(
                ebnc[:, :], src[:, gs], rbnc[:, :].bitcast(F32)))
            self.wait("POOL", "dve", vs)
            rows = slice(g * 1024, (g + 1) * 1024)
            dst_r = self.fm(self.d_yr[rows, cols[0]:cols[1]])
            dst_e = self.fm(self.d_ye[rows, cols[0]:cols[1]])
            self.dma("POOL", "ys", dst_r, self.fs(rbnc[:, :], TT))
            vo = self.dma("POOL", "ys", dst_e, self.fs(ebnc[:, :].bitcast(F32R), TT))
            prev = vo

    def ph_w2(self):
        w2 = self.i_w["w2"]
        wt = [self.sb("w2t", [128, 512], F32, self.C + 2048 * i) for i in range(2)]
        wr = [self.sb("w2r", [128, 512], F32R, self.C + 4096 + 2048 * i) for i in range(2)]
        we = [self.sb("w2e", [128, 512], F32R, self.C + 8192 + 2048 * i) for i in range(2)]
        yr = [self.sb("w2yr", [128, 512], F32R, self.A + 2048 * i) for i in range(2)]
        ye = [self.sb("w2ye", [128, 512], F32R, self.A + 4096 + 2048 * i) for i in range(2)]
        eb = [self.sb("w2b", [128, 512], F32, self.B + 2048 * i) for i in range(2)]
        mmrec = {}
        it = 0
        evguard = {}
        ev_dma = [None, None]
        u = 0
        for t in range(2):
            for og in range(4):
                half = u % 2
                for hc in range(HC):
                    r = it % 2
                    if it >= 2:
                        self.wait("SP", "mm", mmrec[it - 2])
                    else:
                        self.wait("SP", "mm", self.cnt["mm"])
                    vl = self.dma("SP", f"w{r}", wt[r][:, :],
                                  w2[hc * 128:(hc + 1) * 128, og * 512:(og + 1) * 512])
                    vy1 = self.dma("SP", f"w{r}", yr[r][:, :],
                                   self.d_yr[hc * 128:(hc + 1) * 128, t * 512:(t + 1) * 512])
                    vy2 = self.dma("SP", f"w{r}", ye[r][:, :],
                                   self.d_ye[hc * 128:(hc + 1) * 128, t * 512:(t + 1) * 512])
                    self.qwait("POOL", vy2)
                    if it >= 2:
                        self.wait("POOL", "mm", mmrec[it - 2])
                    vcast = self.dma("POOL", f"wc{r}", wr[r][:, :], wt[r][:, :])
                    self.qwait("DVE", vcast)
                    vsub = self.op("DVE", "dve", lambda e, r=r: e.tensor_sub(
                        we[r][:, :], wt[r][:, :], wr[r][:, :].bitcast(F32)))
                    self.wait("TE", "dve", vsub)
                    self.qwait("TE", vy2)
                    if hc == 0 and u >= 2:
                        self.wait("TE", "act", evguard[u - 2])
                    for oc in range(4):
                        pb = self.P[half * 4 + oc]
                        lr = wr[r][:, oc * 128:(oc + 1) * 128]
                        le = we[r][:, oc * 128:(oc + 1) * 128]
                        stt = (hc == 0)
                        stp = (hc == HC - 1)
                        last = oc == 3
                        self.op("TE", "mm", lambda e, pb=pb, l=lr, m=yr[r], s=stt: e.matmul(
                            pb[:, :], l, m[:, :], start=s, stop=False), inc=False)
                        self.op("TE", "mm", lambda e, pb=pb, l=lr, m=ye[r]: e.matmul(
                            pb[:, :], l, m[:, :], start=False, stop=False), inc=False)
                        self.op("TE", "mm", lambda e, pb=pb, l=le, m=yr[r], p=stp: e.matmul(
                            pb[:, :], l, m[:, :], start=False, stop=p), inc=last)
                    mmrec[it] = self.cnt["mm"]
                    it += 1
                v_grp = self.cnt["mm"]
                self.wait("ACT", "mm", v_grp)
                for oc in range(4):
                    b = eb[oc % 2]
                    if u * 4 + oc >= 2:
                        self.qwait("ACT", ev_dma[oc % 2])
                    va = self.op("ACT", "act", lambda e, pb=self.P[half * 4 + oc], b=b: e.activation(
                        out=b[:, :], in_=pb[:, :], func=AF.Silu))
                    self.wait("POOL", "act", va)
                    row0 = (og * 4 + oc) * 128
                    vo = self.dma("POOL", f"gs{oc % 2}",
                                  self.d_g2[row0:row0 + 128, t * 512:(t + 1) * 512], b[:, :])
                    ev_dma[oc % 2] = vo
                evguard[u] = self.cnt["act"]
                u += 1

    def ph_attn(self, qi):
        BIG = self.BIG
        nkb = NKB[qi]
        qr = self.sb("aqr", [128, DC * 512], F32R, BIG)
        qe = self.sb("aqe", [128, DC * 512], F32R, BIG + 32768)
        stor = self.sb("astr", [128, 16 * 512], F32R, BIG + 65536)
        stoe = self.sb("aste", [128, 16 * 512], F32R, BIG + 98304)
        kbr = [self.sb("akbr", [128, DC * 128], F32R, self.A + 8192 * i) for i in range(2)]
        kbe = [self.sb("akbe", [128, DC * 128], F32R, self.C + 8192 * i) for i in range(2)]
        mbuf = [self.sb("amsk", [128, 512], F32, self.M + 1600 + 2048 * i) for i in range(2)]
        sbnc = [self.sb("asb", [128, 512], F32, self.B + 2048 * i) for i in range(2)]
        srb = [self.sb("asr", [128, 512], F32R, self.B + 4096 + 2048 * i) for i in range(2)]

        cq = qi * 512
        v1 = self.dma("SP", "qq", self.fs(qr[:, :], 512),
                      self.fm(self.d_qr[:, cq:cq + 512]))
        v2 = self.dma("SP", "qq", self.fs(qe[:, :], 512),
                      self.fm(self.d_qe[:, cq:cq + 512]))
        self.qwait("TE", v2)
        kload = {}
        mload = {}
        st_done = {}
        for j, cb in enumerate(KBLIST[qi]):
            r = j % 2
            if j >= 2:
                self.wait("SP", "mm", st_done[j - 2][0])
            else:
                self.wait("SP", "mm", self.cnt["mm"])
            lk1 = self.dma("SP", f"kb{r}", self.fs(kbr[r][:, :], 128),
                           self.fm(self.d_kr[:, cb * 128:(cb + 1) * 128]))
            lk2 = self.dma("SP", f"kb{r}", self.fs(kbe[r][:, :], 128),
                           self.fm(self.d_ke[:, cb * 128:(cb + 1) * 128]))
            mi = (8 if qi else 0) + j if qi == 0 else 8 + j
            mi = j if qi == 0 else 8 + j
            if j >= 2:
                self.wait("ACT", "dve", st_done[j - 2][1])
            else:
                self.wait("ACT", "mm", self.cnt["mm"])
            lm = self.dma("ACT", f"mk{r}", mbuf[r][:, :], self.i_masks[mi, :, :])
            self.qwait("TE", lk2)
            pb = self.P[j % 2]
            if j >= 2:
                self.wait("TE", "dve", st_done[j - 2][1])
            for dc in range(DC):
                lr = kbr[r][:, dc * 128:(dc + 1) * 128]
                le = kbe[r][:, dc * 128:(dc + 1) * 128]
                mr = qr[:, dc * 512:(dc + 1) * 512]
                me = qe[:, dc * 512:(dc + 1) * 512]
                stt = (dc == 0)
                stp = (dc == DC - 1)
                self.op("TE", "mm", lambda e, pb=pb, l=lr, m=mr, s=stt: e.matmul(
                    pb[:, :], l, m, start=s, stop=False), inc=False)
                self.op("TE", "mm", lambda e, pb=pb, l=lr, m=me: e.matmul(
                    pb[:, :], l, m, start=False, stop=False), inc=False)
                self.op("TE", "mm", lambda e, pb=pb, l=le, m=mr, p=stp: e.matmul(
                    pb[:, :], l, m, start=False, stop=p), inc=stp)
            v_s = self.cnt["mm"]
            self.wait("DVE", "mm", v_s)
            self.qwait("DVE", lm)
            b = sbnc[r]
            rbb = srb[r]
            if j >= 2:
                self.qwait("DVE", st_done[j - 2][2])
            vm = self.op("DVE", "dve", lambda e, pb=pb, b=b, m=mbuf[r]: e.tensor_tensor(
                b[:, :], pb[:, :], m[:, :], ALU.mult), inc=True)
            self.wait("POOL", "dve", vm)
            vc = self.dma("POOL", "ogc", stor[:, j * 512:(j + 1) * 512], b[:, :])
            self.qwait("DVE", vc)
            vs = self.op("DVE", "dve", lambda e, b=b, j=j: e.tensor_sub(
                stoe[:, j * 512:(j + 1) * 512], b[:, :],
                stor[:, j * 512:(j + 1) * 512].bitcast(F32)))
            st_done[j] = (v_s, vs, vc)
        v_scores = self.cnt["dve"]

        ogr = self.sb("aogr", [128, DC * 512], F32R, BIG)
        oge = self.sb("aoge", [128, DC * 512], F32R, BIG + 32768)
        vb = [self.sb("avb", [128, 512], F32R, self.A + 2048 * i) for i in range(4)]
        gb = [self.sb("agb", [128, 512], F32, self.A + 8192 + 2048 * i) for i in range(2)]
        ob = [self.sb("aob", [128, 512], F32, self.B + 2048 * i) for i in range(2)]
        orb = [self.sb("aor", [128, 512], F32R, self.B + 4096 + 2048 * i) for i in range(2)]
        self.wait("SP", "mm", self.cnt["mm"])
        self.wait("ACT", "mm", self.cnt["mm"])
        self.wait("TE", "dve", v_scores)
        vload = {}
        og_evs = {}
        o_ev = [None, None]
        guse = [0, 0]
        for og in range(4):
            for j, cb in enumerate(KBLIST[qi]):
                r = j % 2
                row0 = cb * 128
                if (og * nkb + j) >= 2:
                    self.wait("SP", "mm", vload[og * nkb + j - 2])
                else:
                    self.wait("SP", "mm", self.cnt["mm"])
                l1 = self.dma("SP", f"vb{r}", vb[r * 2][:, :],
                              self.d_vr[row0:row0 + 128, og * 512:(og + 1) * 512])
                l2 = self.dma("SP", f"vb{r}", vb[r * 2 + 1][:, :],
                              self.d_ve[row0:row0 + 128, og * 512:(og + 1) * 512])
                self.qwait("TE", l2)
                if j == 0 and og >= 2:
                    self.wait("TE", "dve", og_evs[og - 2])
                for oc in range(4):
                    pb = self.P[4 * (og % 2) + oc]
                    lr = vb[r * 2][:, oc * 128:(oc + 1) * 128]
                    le = vb[r * 2 + 1][:, oc * 128:(oc + 1) * 128]
                    mr = stor[:, j * 512:(j + 1) * 512]
                    me = stoe[:, j * 512:(j + 1) * 512]
                    stt = (j == 0)
                    stp = (j == nkb - 1)
                    last = oc == 3
                    self.op("TE", "mm", lambda e, pb=pb, l=lr, m=mr, s=stt: e.matmul(
                        pb[:, :], l, m, start=s, stop=False), inc=False)
                    self.op("TE", "mm", lambda e, pb=pb, l=le, m=mr: e.matmul(
                        pb[:, :], l, m, start=False, stop=False), inc=False)
                    self.op("TE", "mm", lambda e, pb=pb, l=lr, m=me, p=stp: e.matmul(
                        pb[:, :], l, m, start=False, stop=p), inc=last)
                vload[og * nkb + j] = self.cnt["mm"]
            v_og = self.cnt["mm"]
            self.wait("DVE", "mm", v_og)
            for oc in range(4):
                ch = og * 4 + oc
                b = ob[oc % 2]
                rbb = orb[oc % 2]
                row0 = ch * 128
                if ch >= 2:
                    self.wait("ACT", "dve", guse[oc % 2])
                else:
                    self.wait("ACT", "mm", self.cnt["mm"])
                lg = self.dma("ACT", f"gl{oc % 2}", gb[oc % 2][:, :],
                              self.d_g[row0:row0 + 128, cq:cq + 512])
                self.qwait("DVE", lg)
                if ch >= 2:
                    self.qwait("DVE", o_ev[oc % 2])
                vm = self.op("DVE", "dve", lambda e, pb=self.P[4 * (og % 2) + oc], b=b,
                        g=gb[oc % 2]: e.tensor_tensor(b[:, :], pb[:, :], g[:, :],
                                                      ALU.mult), inc=True)
                guse[oc % 2] = vm
                self.wait("POOL", "dve", vm)
                vc = self.dma("POOL", "ogc", ogr[:, ch * 512:(ch + 1) * 512], b[:, :])
                o_ev[oc % 2] = vc
                self.qwait("DVE", vc)
                self.op("DVE", "dve", lambda e, b=b, ch=ch: e.tensor_sub(
                    oge[:, ch * 512:(ch + 1) * 512], b[:, :],
                    ogr[:, ch * 512:(ch + 1) * 512].bitcast(F32)))
            og_evs[og] = self.cnt["dve"]
        v_sv = self.cnt["dve"]

        self.wait("SP", "mm", self.cnt["mm"])
        wp = self.i_w["wp"]
        wt = [self.sb("apt", [128, 512], F32, self.C + 2048 * i) for i in range(2)]
        wr = [self.sb("apr", [128, 512], F32R, self.C + 4096 + 2048 * i) for i in range(2)]
        we = [self.sb("ape", [128, 512], F32R, self.C + 8192 + 2048 * i) for i in range(2)]
        xg = self.sb("axg", [128, DC * 512], F32, BIG + 65536)
        g2t = self.sb("ag2", [128, DC * 512], F32, BIG + 98304)
        obo = [self.sb("aout", [128, 512], F32, self.B + 2048 * i) for i in range(2)]
        self.wait("ACT", "mm", self.cnt["mm"])
        vxg = self.dma("ACT", "xg", self.fs(xg[:, :], 512),
                       self.fm(self.i_x[:, cq:cq + 512]))
        vg2 = self.dma("ACT", "xg", self.fs(g2t[:, :], 512),
                       self.fm(self.d_g2[:, cq:cq + 512]))
        self.wait("TE", "dve", v_sv)
        mmrec = {}
        it = 0
        evguard = {}
        out_dma = [None, None]
        for og2 in range(4):
            half = og2 % 2
            for dc in range(DC):
                r = it % 2
                if it >= 2:
                    self.wait("SP", "mm", mmrec[it - 2])
                else:
                    self.wait("SP", "mm", self.cnt["mm"])
                vl = self.dma("SP", f"w{r}", wt[r][:, :],
                              wp[dc * 128:(dc + 1) * 128, og2 * 512:(og2 + 1) * 512])
                self.qwait("POOL", vl)
                if it >= 2:
                    self.wait("POOL", "mm", mmrec[it - 2])
                vcast = self.dma("POOL", f"wc{r}", wr[r][:, :], wt[r][:, :])
                self.qwait("DVE", vcast)
                vsub = self.op("DVE", "dve", lambda e, r=r: e.tensor_sub(
                    we[r][:, :], wt[r][:, :], wr[r][:, :].bitcast(F32)))
                self.wait("TE", "dve", vsub)
                if dc == 0 and og2 >= 2:
                    self.wait("TE", "dve", evguard[og2 - 2])
                for oc in range(4):
                    pb = self.P[half * 4 + oc]
                    lr = wr[r][:, oc * 128:(oc + 1) * 128]
                    le = we[r][:, oc * 128:(oc + 1) * 128]
                    mr = ogr[:, dc * 512:(dc + 1) * 512]
                    me = oge[:, dc * 512:(dc + 1) * 512]
                    stt = (dc == 0)
                    stp = (dc == DC - 1)
                    last = oc == 3
                    self.op("TE", "mm", lambda e, pb=pb, l=lr, m=mr, s=stt: e.matmul(
                        pb[:, :], l, m, start=s, stop=False), inc=False)
                    self.op("TE", "mm", lambda e, pb=pb, l=lr, m=me: e.matmul(
                        pb[:, :], l, m, start=False, stop=False), inc=False)
                    self.op("TE", "mm", lambda e, pb=pb, l=le, m=mr, p=stp: e.matmul(
                        pb[:, :], l, m, start=False, stop=p), inc=last)
                mmrec[it] = self.cnt["mm"]
                it += 1
            v_grp = self.cnt["mm"]
            self.wait("DVE", "mm", v_grp)
            self.qwait("DVE", vg2)
            for oc in range(4):
                ch = og2 * 4 + oc
                b = obo[oc % 2]
                if ch >= 2:
                    self.qwait("DVE", out_dma[oc % 2])
                vmul = self.op("DVE", "dve", lambda e, pb=self.P[half * 4 + oc], b=b, ch=ch: e.tensor_tensor(
                    b[:, :], pb[:, :], g2t[:, ch * 512:(ch + 1) * 512], ALU.mult),
                    inc=True)
                self.wait("DVE", "dve", vmul)
                vm = self.op("DVE", "dve", lambda e, b=b, ch=ch: e.tensor_tensor(
                    b[:, :], b[:, :], xg[:, ch * 512:(ch + 1) * 512], ALU.add))
                self.wait("ACT", "dve", vm)
                row0 = ch * 128
                vo = self.dma("ACT", f"o{oc % 2}",
                              self.o_out[row0:row0 + 128, cq:cq + 512], b[:, :])
                out_dma[oc % 2] = vo
            evguard[og2] = self.cnt["dve"]

    def build(self):
        self.ph0()
        if self.st["p1"]:
            self.fence()
            self.ph1_slab(1)
            self.fence()
            self.ph1_slab(0)
        if self.st["ffn"]:
            self.fence()
            self.ph_ffn()
        if self.st["fwht"]:
            self.fence()
            self.ph_fwht()
        if self.st["w2"]:
            self.fence()
            self.ph_w2()
        if self.st["attn"]:
            self.fence()
            self.ph_attn(0)
            self.fence()
            self.ph_attn(1)
        self.fence()
        self.qwait("ACT", self.qtotal("o0"))
        self.qwait("ACT", self.qtotal("o1"))
        self.qwait("POOL", self.qtotal("o0"))
        self.qwait("POOL", self.qtotal("o1"))
        with self.nc.Block() as block:
            @block.tensor
            def _(e):
                for f in self.ops["TE"]:
                    f(e)

            @block.scalar
            def _(e):
                for f in self.ops["ACT"]:
                    f(e)

            @block.vector
            def _(e):
                for f in self.ops["DVE"]:
                    f(e)

            @block.gpsimd
            def _(e):
                for f in self.ops["POOL"]:
                    f(e)

            @block.sync
            def _(e):
                for f in self.ops["SP"]:
                    f(e)
        self.ctx.close()
        return self.nc


def _hadamard(n):
    h = np.array([[1.0]], dtype=np.float32)
    while h.shape[0] < n:
        h = np.block([[h, h], [h, -h]])
    return h.astype(np.float32)


def _staging(core, x):
    b, h = core // 2, core % 2
    own = np.concatenate([np.arange((2 * m + h) * 128, (2 * m + h) * 128 + 128)
                          for m in range(8)])
    par = np.concatenate([np.arange((2 * m + 1 - h) * 128, (2 * m + 1 - h) * 128 + 128)
                          for m in range(8)])
    perm = np.concatenate([own, par])
    xT = np.ascontiguousarray(x[b][perm, :].T)
    gidx = perm
    masks = np.zeros((24, 128, 512), dtype=np.float32)
    row = 0
    for qi in (0, 1):
        gq = gidx[qi * 512: qi * 512 + 512]
        for cb in KBLIST[qi]:
            gk = gidx[cb * 128:(cb + 1) * 128]
            masks[row] = (gk[:, None] <= gq[None, :]).astype(np.float32)
            row += 1
    return xT, masks, own


def kernel(L0_attn_x, L0_attn_Wq, L0_attn_Wk, L0_attn_Wv, L0_attn_Wg, L0_attn_Wp,
           L0_ffn_W1, L0_ffn_W2, L0_ffn_W3):
    x = np.ascontiguousarray(np.asarray(L0_attn_x, dtype=np.float32))
    ws = {
        "wqT": np.ascontiguousarray(np.asarray(L0_attn_Wq).T),
        "wkT": np.ascontiguousarray(np.asarray(L0_attn_Wk).T),
        "wvT": np.ascontiguousarray(np.asarray(L0_attn_Wv).T),
        "wgT": np.ascontiguousarray(np.asarray(L0_attn_Wg).T),
        "wpT": np.ascontiguousarray(np.asarray(L0_attn_Wp).T),
        "w1T": np.ascontiguousarray(np.asarray(L0_ffn_W1).T),
        "w3T": np.ascontiguousarray(np.asarray(L0_ffn_W3).T),
        "w2T": np.ascontiguousarray(np.asarray(L0_ffn_W2).T),
    }
    had = _hadamard(128)
    in_maps = []
    owns = []
    for c in range(8):
        xT, masks, own = _staging(c, x)
        owns.append(own)
        m = {"xT": xT, "masks": masks, "had128": had}
        m.update(ws)
        in_maps.append(m)

    nc = Builder().build()
    res = run_bass_kernel_spmd(nc, in_maps, core_ids=list(range(8)))
    out = np.empty((4, 2048, 2048), dtype=np.float32)
    for c in range(8):
        b = c // 2
        out[b][owns[c], :] = res.results[c]["outT"].T
    return out


# version 03
# speedup vs baseline: 1.1296x; 1.1296x over previous
from contextlib import ExitStack

import numpy as np
import concourse.bass as bass
import concourse.mybir as mybir
from concourse.bass_utils import run_bass_kernel_spmd

F32 = mybir.dt.float32
F32R = mybir.dt.float32r
AF = mybir.ActivationFunctionType
ALU = mybir.AluOpType

D = 2048
DC = 16
TS = 1024
HIDDEN = 8192
HC = 64
NKB = {0: 8, 1: 16}
KBLIST = {0: [0, 1, 2, 3, 8, 9, 10, 11], 1: list(range(16))}
LN_EPS = 1e-5
RMS_EPS = float(np.finfo(np.float32).eps)
FWHT_SCALE = float(1.0 / np.sqrt(HIDDEN))

ENGINES = ("TE", "ACT", "DVE", "POOL", "SP")

STAGES = {"p1": True, "ffn": True, "fwht": True, "w2": True, "attn": True,
          "debug": False}


def ceil32(x):
    return (x + 31) & ~31


class Builder:
    def __init__(self, stages=STAGES):
        self.st = dict(stages)
        nc = bass.Bass("TRN2", target_bir_lowering=False)
        self.nc = nc
        self.ops = {e: [] for e in ENGINES}
        self.cnt = {}
        self.sem = {}
        self.ctx = ExitStack()
        for name in ("mm", "act", "dve", "pool", "sp"):
            self.sem[name] = self.ctx.enter_context(nc.semaphore("s_" + name))
            self.cnt[name] = 0
        self.qsem, self.qcnt, self.qeng = {}, {}, {}
        self._declare_dram()
        self._declare_sbuf()
        self._declare_psum()

    def inc(self, name, n=1):
        self.cnt[name] += n
        return self.cnt[name]

    def emit(self, eng, fn):
        self.ops[eng].append(fn)

    def wait(self, eng, sem, val):
        if val <= 0:
            return
        s = self.sem[sem]
        self.emit(eng, lambda e, s=s, v=val: e.wait_ge(s, v))

    def dma(self, eng, stream, out, in_):
        if stream not in self.qsem:
            self.qsem[stream] = self.ctx.enter_context(self.nc.semaphore("q_" + stream))
            self.qcnt[stream] = 0
            self.qeng[stream] = eng
        assert self.qeng[stream] == eng, (stream, eng)
        self.qcnt[stream] += 16
        v = self.qcnt[stream]
        s = self.qsem[stream]
        self.emit(eng, lambda e, o=out, i=in_, s=s: e.dma_start(out=o, in_=i).then_inc(s, 16))
        return (stream, v)

    def qwait(self, eng, tok):
        if tok is None:
            return
        stream, v = tok
        if v <= 0:
            return
        s = self.qsem[stream]
        self.emit(eng, lambda e, s=s, v=v: e.wait_ge(s, v))

    def qtotal(self, stream):
        return (stream, self.qcnt.get(stream, 0))

    @staticmethod
    def fm(ap2d):
        return ap2d.rearrange("(c p) t -> p c t", p=128)

    @staticmethod
    def fs(sb_ap, t):
        return sb_ap.rearrange("p (c t) -> p c t", t=t)

    def op(self, eng, esem, fn, inc=True):
        if inc:
            v = self.inc(esem)
            s = self.sem[esem]
            self.emit(eng, lambda e, f=fn, s=s: f(e).then_inc(s, 1))
            return v
        self.emit(eng, lambda e, f=fn: f(e))
        return self.cnt[esem]

    def fence(self, tag=""):
        vals = {}
        for e in ENGINES:
            for stream, owner in self.qeng.items():
                if owner == e:
                    self.qwait(e, self.qtotal(stream))
            es = {"TE": "mm", "ACT": "act", "DVE": "dve", "POOL": "pool", "SP": "sp"}[e]
            self.wait(e, es, self.cnt[es])
            v = self.inc(es)
            s = self.sem[es]
            self.emit(e, lambda en, s=s: en.nop().then_inc(s, 1))
            vals[e] = (es, v)
        for e in ENGINES:
            for o in ENGINES:
                if o != e:
                    sname, v = vals[o]
                    self.wait(e, sname, v)

    def _declare_dram(self):
        nc = self.nc
        t = nc.dram_tensor
        self.i_x = t("xT", [D, 2 * TS], F32, kind="ExternalInput").ap()
        self.i_w = {}
        for nm in ("wq", "wk", "wv", "wg", "wp"):
            self.i_w[nm] = t(nm + "T", [D, D], F32, kind="ExternalInput").ap()
        self.i_w["w1"] = t("w1T", [D, HIDDEN], F32, kind="ExternalInput").ap()
        self.i_w["w3"] = t("w3T", [D, HIDDEN], F32, kind="ExternalInput").ap()
        self.i_w["w2"] = t("w2T", [HIDDEN, D], F32, kind="ExternalInput").ap()
        self.i_masks = t("masks", [24, 128, 512], F32, kind="ExternalInput").ap()
        self.i_h128 = t("had128", [128, 128], F32, kind="ExternalInput").ap()

        kind = "ExternalOutput" if self.st["debug"] else "Internal"
        if self.st["debug"]:
            self.d_x2r = t("x2r_d", [D, TS], F32R, kind=kind).ap()
            self.d_x2e = t("x2e_d", [D, TS], F32R, kind=kind).ap()
            self.d_y1dbg = t("y1dbg_d", [128, HC * 256], F32, kind=kind).ap()
            self.d_ybf = t("ybf_d", [128, HC * 256], F32, kind=kind).ap()
        self.d_kr = t("kr_d", [D, 2 * TS], F32R, kind=kind).ap()
        self.d_ke = t("ke_d", [D, 2 * TS], F32R, kind=kind).ap()
        self.d_vr = t("vr_d", [2 * TS, D], F32R, kind=kind).ap()
        self.d_ve = t("ve_d", [2 * TS, D], F32R, kind=kind).ap()
        self.d_qr = t("qr_d", [D, TS], F32R, kind=kind).ap()
        self.d_qe = t("qe_d", [D, TS], F32R, kind=kind).ap()
        self.d_g = t("g_d", [D, TS], F32, kind=kind).ap()
        self.d_xn = t("xn_d", [D, TS], F32, kind=kind).ap()
        self.d_hr = t("hr_d", [HIDDEN, TS], F32R, kind=kind).ap()
        self.d_he = t("he_d", [HIDDEN, TS], F32R, kind=kind).ap()
        self.d_yr = t("yr_d", [HIDDEN, TS], F32R, kind=kind).ap()
        self.d_ye = t("ye_d", [HIDDEN, TS], F32R, kind=kind).ap()
        self.d_g2 = t("g2_d", [D, TS], F32, kind=kind).ap()
        self.o_out = t("outT", [D, TS], F32, kind="ExternalOutput").ap()

    def _declare_sbuf(self):
        nc = self.nc
        at = nc.alloc_sbuf_tensor_at
        base0 = nc.sbuf_base
        M = ceil32(base0)
        self.M = M
        self.A = M + 18432
        self.B = M + 38912
        self.C = M + 53248
        self.BIG = M + 77824
        end = self.BIG + 131072
        nc.alloc_sbuf_tensor("manual_map_reserved",
                             [128, end - base0], mybir.dt.uint8)
        self.ones_col = at("ones_col", [128, 1], F32, offset=M).ap()
        self.ones_row = at("ones_row", [1, 128], F32, offset=M + 32).ap()
        self.h128r = at("h128r", [128, 128], F32R, offset=M + 544).ap()
        self.h128f = at("h128f", [128, 128], F32, offset=M + 1056).ap()
        self.stat = [at(f"stat{i}", [1, 1024], F32, offset=M + 1600 + 4096 * i).ap()
                     for i in range(4)]
        self._tens = {}

    def sb(self, name, shape, dtype, off):
        key = (name, tuple(shape), dtype, off)
        if key not in self._tens:
            self._tens[key] = self.nc.alloc_sbuf_tensor_at(
                f"{name}_{off}", shape, dtype, offset=off).ap()
        return self._tens[key]

    def _declare_psum(self):
        self.P = [self.ctx.enter_context(
            self.nc.psum_tensor(f"pb{i}", [128, 512], F32)) for i in range(8)]

    def ph0(self):
        self.op("DVE", "dve", lambda e: e.memset(self.ones_col[:, :], 1.0), inc=False)
        self.op("DVE", "dve", lambda e: e.memset(self.ones_row[:, :], 1.0))
        v = self.dma("POOL", "ini", self.h128f[:, :], self.i_h128[:, :])
        self.qwait("POOL", v)
        vc = self.dma("POOL", "ini", self.h128r[:, :], self.h128f[:, :])
        self.qwait("TE", vc)
        self.wait("TE", "dve", self.cnt["dve"])

    def ph1_slab(self, s):
        c0 = s * TS
        own = (s == 0)
        BIG = self.BIG
        xn = self.sb("xn", [128, DC * TS], F32, BIG)
        xn2r = self.sb("xn2r", [128, DC * TS], F32R, BIG + 65536)
        xn2e = self.sb("xn2e", [128, DC * TS], F32R, BIG)
        sq = [self.sb("sqb", [128, 1024], F32, self.B + 4096 * i) for i in range(2)]
        bnc = [self.sb("lnb", [128, 1024], F32, self.B + 8192 + 4096 * i) for i in range(2)]

        src = self.fm(self.i_x[:, c0:c0 + TS])
        vx = self.dma("ACT", "xs", self.fs(xn[:, :], TS), src)

        self.qwait("TE", vx)
        self.qwait("ACT", vx)
        mm_done = []
        sq_done = [0, 0]
        for dc in range(DC):
            xc = xn[:, dc * TS:(dc + 1) * TS]
            sqb = sq[dc % 2]
            if dc >= 2:
                self.wait("ACT", "mm", sq_done[dc % 2])
            va = self.op("ACT", "act", lambda e, i=xc, o=sqb: e.activation(
                out=o[:, :], in_=i, func=AF.Square))
            for t in range(2):
                last = (dc == DC - 1)
                self.op("TE", "mm", lambda e, i=xc, t=t, dc=dc, last=last: e.matmul(
                    self.P[t][0:1, :], self.ones_col[:, :], i[:, t * 512:(t + 1) * 512],
                    start=(dc == 0), stop=last), inc=last)
            self.wait("TE", "act", va)
            for t in range(2):
                last = (dc == DC - 1)
                v = self.op("TE", "mm", lambda e, i=sqb, t=t, dc=dc, last=last: e.matmul(
                    self.P[2 + t][0:1, :], self.ones_col[:, :], i[:, t * 512:(t + 1) * 512],
                    start=(dc == 0), stop=last), inc=True)
                sq_done[dc % 2] = v
        v_stats = self.cnt["mm"]

        st0, st1, st2, st3 = self.stat
        self.wait("DVE", "mm", v_stats)

        def R(fn):
            self.wait("DVE", "dve", self.cnt["dve"])
            return self.op("DVE", "dve", fn, inc=True)
        for t in range(2):
            R(lambda e, t=t: e.tensor_scalar_mul(st0[:, t * 512:(t + 1) * 512],
                                                 self.P[t][0:1, :], 1.0 / D))
            R(lambda e, t=t: e.tensor_scalar_mul(st1[:, t * 512:(t + 1) * 512],
                                                 self.P[2 + t][0:1, :], 1.0 / D))
        R(lambda e: e.tensor_tensor(st2[:, :], st0[:, :], st0[:, :], ALU.mult))
        R(lambda e: e.tensor_sub(st1[:, :], st1[:, :], st2[:, :]))
        self._rsqrt_row(st1, st2, st3, LN_EPS)
        self.wait("DVE", "dve", self.cnt["dve"])
        v_rows = self.op("DVE", "dve", lambda e: e.tensor_tensor(
            st3[:, :], st0[:, :], st2[:, :], ALU.mult))

        self.wait("TE", "dve", v_rows)
        for t in range(2):
            self.op("TE", "mm", lambda e, t=t: e.matmul(
                self.P[4 + t][:, :], self.ones_row[:, :], st2[:, t * 512:(t + 1) * 512],
                start=True, stop=True), inc=False)
            self.op("TE", "mm", lambda e, t=t: e.matmul(
                self.P[6 + t][:, :], self.ones_row[:, :], st3[:, t * 512:(t + 1) * 512],
                start=True, stop=True), inc=(t == 1))
        v_bcast = self.cnt["mm"]

        self.wait("DVE", "mm", v_bcast)
        xn_chunk_done = []
        for dc in range(DC):
            xc = xn[:, dc * TS:(dc + 1) * TS]
            b = bnc[dc % 2]
            for t in range(2):
                R(lambda e, i=xc, b=b, t=t: e.tensor_tensor(
                    b[:, t * 512:(t + 1) * 512], i[:, t * 512:(t + 1) * 512],
                    self.P[4 + t][:, :], ALU.mult))
            self.wait("DVE", "dve", self.cnt["dve"])
            v = self.op("DVE", "dve", lambda e, i=xc, b=b: e.tensor_tensor(
                i[:, 0:512], b[:, 0:512], self.P[6][:, :], ALU.subtract), inc=True)
            self.wait("DVE", "dve", self.cnt["dve"])
            v = self.op("DVE", "dve", lambda e, i=xc, b=b: e.tensor_tensor(
                i[:, 512:1024], b[:, 512:1024], self.P[7][:, :], ALU.subtract))
            xn_chunk_done.append(v)
        if own:
            self.wait("POOL", "dve", xn_chunk_done[-1])
            self.v_xn_dma = self.dma("POOL", "xnst", self.fm(self.d_xn[:, :]),
                                     self.fs(xn[:, :], TS))

        sq_done = [0, 0]
        for dc in range(DC):
            xc = xn[:, dc * TS:(dc + 1) * TS]
            sqb = sq[dc % 2]
            self.wait("ACT", "dve", xn_chunk_done[dc])
            if dc >= 2:
                self.wait("ACT", "mm", sq_done[dc % 2])
            va = self.op("ACT", "act", lambda e, i=xc, o=sqb: e.activation(
                out=o[:, :], in_=i, func=AF.Square))
            self.wait("TE", "act", va)
            for t in range(2):
                last = (dc == DC - 1)
                v = self.op("TE", "mm", lambda e, i=sqb, t=t, dc=dc, last=last: e.matmul(
                    self.P[t][0:1, :], self.ones_col[:, :], i[:, t * 512:(t + 1) * 512],
                    start=(dc == 0), stop=last), inc=True)
                sq_done[dc % 2] = v
        v_stats2 = self.cnt["mm"]
        self.wait("DVE", "mm", v_stats2)
        for t in range(2):
            R(lambda e, t=t: e.tensor_scalar_mul(st0[:, t * 512:(t + 1) * 512],
                                                 self.P[t][0:1, :], 1.0 / D))
        self._rsqrt_row(st0, st2, st3, RMS_EPS)
        v_rows2 = self.cnt["dve"]
        self.wait("TE", "dve", v_rows2)
        for t in range(2):
            self.op("TE", "mm", lambda e, t=t: e.matmul(
                self.P[4 + t][:, :], self.ones_row[:, :], st2[:, t * 512:(t + 1) * 512],
                start=True, stop=True), inc=(t == 1))
        v_bc2 = self.cnt["mm"]

        self.wait("DVE", "mm", v_bc2)
        for dc in range(DC):
            xc = xn[:, dc * TS:(dc + 1) * TS]
            b = bnc[dc % 2]
            if dc >= 2:
                self.qwait("DVE", self._xn2cast[dc % 2])
            for t in range(2):
                R(lambda e, i=xc, b=b, t=t: e.tensor_tensor(
                    b[:, t * 512:(t + 1) * 512], i[:, t * 512:(t + 1) * 512],
                    self.P[4 + t][:, :], ALU.mult))
            vb = self.cnt["dve"]
            self.wait("POOL", "dve", vb)
            if own and dc == 0:
                pass
            vc = self.dma("POOL", "xc", xn2r[:, dc * TS:(dc + 1) * TS], b[:, :])
            if not hasattr(self, "_xn2cast"):
                self._xn2cast = [None, None]
            self._xn2cast[dc % 2] = vc
            self.qwait("DVE", vc)
            if own:
                self.qwait("DVE", self.v_xn_dma)
            self.op("DVE", "dve", lambda e, b=b, dc=dc: e.tensor_sub(
                xn2e[:, dc * TS:(dc + 1) * TS], b[:, :],
                xn2r[:, dc * TS:(dc + 1) * TS].bitcast(F32)))
        v_split = self.cnt["dve"]

        if self.st["debug"] and own:
            self.wait("POOL", "dve", v_split)
            self.dma("POOL", "dbg", self.fm(self.d_x2r[:, :]), self.fs(xn2r[:, :], TS))
            self.dma("POOL", "dbg", self.fm(self.d_x2e[:, :]), self.fs(xn2e[:, :], TS))

        self.fence()
        self.proj_featmaj("wk", self.i_w["wk"], AF.Silu, self.d_kr, self.d_ke,
                          xn2r, xn2e, out_cols=(c0, TS), v_ready=v_split)
        self.fence()
        self.proj_tokmaj(xn2r, xn2e, slab_off=c0, v_ready=v_split)
        if own:
            self.fence()
            self.proj_featmaj("wq", self.i_w["wq"], AF.Silu, self.d_qr, self.d_qe,
                              xn2r, xn2e, out_cols=(0, TS), v_ready=v_split)
            self.fence()
            self.proj_featmaj("wg", self.i_w["wg"], AF.Sigmoid, self.d_g, None,
                              xn2r, xn2e, out_cols=(0, TS), v_ready=v_split)

    def _rsqrt_row(self, vrow, out, tmp, eps):
        def R(fn):
            self.wait("DVE", "dve", self.cnt["dve"])
            return self.op("DVE", "dve", fn, inc=True)

        def A(fn):
            self.wait("ACT", "act", self.cnt["act"])
            return self.op("ACT", "act", fn, inc=True)
        R(lambda e: e.tensor_scalar_add(vrow[:, :], vrow[:, :], eps))
        self.wait("ACT", "dve", self.cnt["dve"])
        va = self.op("ACT", "act", lambda e: e.activation(out=tmp[:, :], in_=vrow[:, :],
                                                          func=AF.Sqrt))
        self.wait("DVE", "act", va)
        R(lambda e: e.reciprocal(out[:, :], tmp[:, :]))
        for i in range(2):
            R(lambda e: e.tensor_tensor(tmp[:, :], out[:, :], out[:, :], ALU.mult))
            R(lambda e: e.tensor_tensor(tmp[:, :], tmp[:, :], vrow[:, :], ALU.mult))
            R(lambda e: e.tensor_scalar(tmp[:, :], tmp[:, :], -0.5, 1.5,
                                        ALU.mult, ALU.add))
            self.wait("DVE", "dve", self.cnt["dve"])
            self.op("DVE", "dve", lambda e: e.tensor_tensor(
                out[:, :], out[:, :], tmp[:, :], ALU.mult), inc=True)

    def proj_featmaj(self, nm, w, act, dr, de, xr, xe, out_cols, v_ready,
                    n_out=D):
        c0, TS_ = out_cols
        wtmp = [self.sb(nm + "wt", [128, 512], F32, self.A + 2048 * i) for i in range(3)]
        wr = [self.sb(nm + "wr", [128, 512], F32R, self.A + 6144 + 2048 * i) for i in range(3)]
        we = [self.sb(nm + "we", [128, 512], F32R, self.A + 12288 + 2048 * i) for i in range(3)]
        rb = [self.sb(nm + "rb", [128, 512], F32R, self.B + 2048 * i) for i in range(2)]
        ebw = self.sb(nm + "ebw", [128, 8 * 512], F32, self.C + 2048)
        n_og = n_out // 512
        self.wait("TE", "dve", v_ready)
        mmrec = {}
        evrec = [None]
        it = 0
        evac_v = 0
        for og in range(n_og):
            for dc in range(DC):
                r = it % 3
                if it >= 3:
                    self.wait("SP", "mm", mmrec[(it - 3)])
                else:
                    self.wait("SP", "mm", self.cnt["mm"])
                vload = self.dma("SP", f"w{r}", wtmp[r][:, :],
                                 w[dc * 128:(dc + 1) * 128, og * 512:(og + 1) * 512])
                self.qwait("POOL", vload)
                if it >= 3:
                    self.wait("POOL", "mm", mmrec[(it - 3)])
                vcast = self.dma("POOL", f"wc{r}", wr[r][:, :], wtmp[r][:, :])
                self.qwait("DVE", vcast)
                vsub = self.op("DVE", "dve", lambda e, r=r: e.tensor_sub(
                    we[r][:, :], wtmp[r][:, :], wr[r][:, :].bitcast(F32)))
                self.wait("TE", "dve", vsub)
                if og > 0 and dc == 0:
                    self.wait("TE", "act", evac_v)
                for oc in range(4):
                    for t in range(2):
                        pb = self.P[oc * 2 + t]
                        mv_r = xr[:, dc * TS_ + t * 512: dc * TS_ + t * 512 + 512]
                        mv_e = xe[:, dc * TS_ + t * 512: dc * TS_ + t * 512 + 512]
                        stt = (dc == 0)
                        stp = (dc == DC - 1)
                        lr = wr[r][:, oc * 128:(oc + 1) * 128]
                        le = we[r][:, oc * 128:(oc + 1) * 128]
                        self.op("TE", "mm", lambda e, pb=pb, lr=lr, mv=mv_r, s=stt: e.matmul(
                            pb[:, :], lr, mv, start=s, stop=False), inc=False)
                        self.op("TE", "mm", lambda e, pb=pb, lr=lr, mv=mv_e: e.matmul(
                            pb[:, :], lr, mv, start=False, stop=False), inc=False)
                        last = oc == 3 and t == 1
                        self.op("TE", "mm", lambda e, pb=pb, le=le, mv=mv_r, p=stp: e.matmul(
                            pb[:, :], le, mv, start=False, stop=p), inc=last)
                mmrec[it] = self.cnt["mm"]
                it += 1
            v_grp = self.cnt["mm"]
            self.wait("ACT", "mm", v_grp)
            if og > 0:
                self.qwait("ACT", evrec[0])
            for j in range(8):
                oc, t = divmod(j, 2)
                self.op("ACT", "act", lambda e, pb=self.P[j], j=j, f=act: e.activation(
                    out=ebw[:, j * 512:(j + 1) * 512], in_=pb[:, :], func=f),
                    inc=(j == 7))
            evac_v = self.cnt["act"]
            va = evac_v
            prev_sub = None
            for j in range(8):
                oc, t = divmod(j, 2)
                bsl = ebw[:, j * 512:(j + 1) * 512]
                rbb = rb[j % 2]
                row0 = (og * 4 + oc) * 128
                dst_cols = (c0 + t * 512, c0 + t * 512 + 512)
                self.wait("POOL", "act", va)
                if de is None:
                    vo = self.dma("POOL", "ps0",
                                  dr[row0:row0 + 128, dst_cols[0]:dst_cols[1]], bsl)
                    evrec[0] = vo
                else:
                    if prev_sub is not None:
                        self.wait("POOL", "dve", prev_sub)
                    if j >= 2:
                        self.qwait("POOL", evrec[0])
                    vc = self.dma("POOL", "pc0", rbb[:, :], bsl)
                    self.qwait("DVE", vc)
                    vs = self.op("DVE", "dve", lambda e, b=bsl, rbb=rbb: e.tensor_sub(
                        b, b, rbb[:, :].bitcast(F32)))
                    prev_sub = vs
                    self.wait("POOL", "dve", vs)
                    self.dma("POOL", "ps0",
                             dr[row0:row0 + 128, dst_cols[0]:dst_cols[1]], rbb[:, :])
                    vo = self.dma("POOL", "ps0",
                                  de[row0:row0 + 128, dst_cols[0]:dst_cols[1]],
                                  bsl.bitcast(F32R))
                    evrec[0] = vo

    def proj_tokmaj(self, xr, xe, slab_off, v_ready):
        nm = "wv"
        w = self.i_w["wv"]
        wtmp = [self.sb(nm + "wt", [128, 512], F32, self.A + 2048 * i) for i in range(3)]
        wr = [self.sb(nm + "wr", [128, 512], F32R, self.A + 6144 + 2048 * i) for i in range(3)]
        we = [self.sb(nm + "we", [128, 512], F32R, self.A + 12288 + 2048 * i) for i in range(3)]
        rb = [self.sb(nm + "rb", [128, 512], F32R, self.B + 2048 * i) for i in range(2)]
        ebw = self.sb(nm + "ebw", [128, 8 * 512], F32, self.C + 2048)
        self.wait("TE", "dve", v_ready)
        it = 0
        mmrec = {}
        evrec = [None]
        evac_v = 0
        for dt in range(4):
            for dc in range(DC):
                r = it % 3
                if it >= 3:
                    self.wait("SP", "mm", mmrec[it - 3])
                else:
                    self.wait("SP", "mm", self.cnt["mm"])
                vload = self.dma("SP", f"w{r}", wtmp[r][:, :],
                                 w[dc * 128:(dc + 1) * 128, dt * 512:(dt + 1) * 512])
                self.qwait("POOL", vload)
                if it >= 3:
                    self.wait("POOL", "mm", mmrec[it - 3])
                vcast = self.dma("POOL", f"wc{r}", wr[r][:, :], wtmp[r][:, :])
                self.qwait("DVE", vcast)
                vsub = self.op("DVE", "dve", lambda e, r=r: e.tensor_sub(
                    we[r][:, :], wtmp[r][:, :], wr[r][:, :].bitcast(F32)))
                self.wait("TE", "dve", vsub)
                if dt > 0 and dc == 0:
                    self.wait("TE", "act", evac_v)
                for tb in range(8):
                    pb = self.P[tb]
                    st_r = xr[:, dc * TS + tb * 128: dc * TS + tb * 128 + 128]
                    st_e = xe[:, dc * TS + tb * 128: dc * TS + tb * 128 + 128]
                    stt = (dc == 0)
                    stp = (dc == DC - 1)
                    last = tb == 7
                    self.op("TE", "mm", lambda e, pb=pb, l=st_r, m=wr[r], s=stt: e.matmul(
                        pb[:, :], l, m[:, :], start=s, stop=False), inc=False)
                    self.op("TE", "mm", lambda e, pb=pb, l=st_r, m=we[r]: e.matmul(
                        pb[:, :], l, m[:, :], start=False, stop=False), inc=False)
                    self.op("TE", "mm", lambda e, pb=pb, l=st_e, m=wr[r], p=stp: e.matmul(
                        pb[:, :], l, m[:, :], start=False, stop=p), inc=last)
                mmrec[it] = self.cnt["mm"]
                it += 1
            v_grp = self.cnt["mm"]
            self.wait("ACT", "mm", v_grp)
            if dt > 0:
                self.qwait("ACT", evrec[0])
            for tb in range(8):
                self.op("ACT", "act", lambda e, pb=self.P[tb], tb=tb: e.activation(
                    out=ebw[:, tb * 512:(tb + 1) * 512], in_=pb[:, :], func=AF.Silu),
                    inc=(tb == 7))
            evac_v = self.cnt["act"]
            va = evac_v
            prev_sub = None
            for tb in range(8):
                bsl = ebw[:, tb * 512:(tb + 1) * 512]
                rbb = rb[tb % 2]
                row0 = slab_off + tb * 128
                self.wait("POOL", "act", va)
                if prev_sub is not None:
                    self.wait("POOL", "dve", prev_sub)
                if tb >= 2:
                    self.qwait("POOL", evrec[0])
                vc = self.dma("POOL", "pc0", rbb[:, :], bsl)
                self.qwait("DVE", vc)
                vs = self.op("DVE", "dve", lambda e, b=bsl, rbb=rbb: e.tensor_sub(
                    b, b, rbb[:, :].bitcast(F32)))
                prev_sub = vs
                self.wait("POOL", "dve", vs)
                self.dma("POOL", "ps0",
                         self.d_vr[row0:row0 + 128, dt * 512:(dt + 1) * 512], rbb[:, :])
                vo = self.dma("POOL", "ps0",
                              self.d_ve[row0:row0 + 128, dt * 512:(dt + 1) * 512],
                              bsl.bitcast(F32R))
                evrec[0] = vo

    def ph_ffn(self):
        BIG = self.BIG
        xn = self.sb("fxn", [128, DC * TS], F32, BIG)
        xnr = self.sb("fxnr", [128, DC * TS], F32R, BIG + 65536)
        xne = self.sb("fxne", [128, DC * TS], F32R, BIG)

        src = self.fm(self.d_xn[:, :])
        vx = self.dma("POOL", "xnst", self.fs(xn[:, :], TS), src)
        self.qwait("POOL", vx)
        cast_v = []
        for dc in range(DC):
            vc = self.dma("POOL", "fc", xnr[:, dc * TS:(dc + 1) * TS],
                          xn[:, dc * TS:(dc + 1) * TS])
            cast_v.append(vc)
        self.qwait("DVE", cast_v[-1])
        for dc in range(DC):
            self.op("DVE", "dve", lambda e, dc=dc: e.tensor_sub(
                xne[:, dc * TS:(dc + 1) * TS], xn[:, dc * TS:(dc + 1) * TS],
                xnr[:, dc * TS:(dc + 1) * TS].bitcast(F32)))
        v_split = self.cnt["dve"]

        w1, w3 = self.i_w["w1"], self.i_w["w3"]
        wt1 = [self.sb("f1t", [128, 512], F32, self.A + 2048 * i) for i in range(3)]
        wr1 = [self.sb("f1r", [128, 512], F32R, self.A + 6144 + 2048 * i) for i in range(3)]
        we1 = [self.sb("f1e", [128, 512], F32R, self.A + 12288 + 2048 * i) for i in range(3)]
        ug = self.sb("fug", [128, 8 * 512], F32, self.C)
        hb = [self.sb("fhb", [128, 512], F32, self.B + 2048 * i) for i in range(4)]
        rb = [self.sb("fhr", [128, 512], F32R, self.C + 16384 + 2048 * i) for i in range(2)]
        re = [self.sb("fhe", [128, 512], F32R, self.C + 20480 + 2048 * i) for i in range(2)]
        self.wait("TE", "dve", v_split)
        it = 0
        mmrec = {}
        evac_sem, evac_v = "act", 0
        hsub = {}
        h_latest = None
        for hg in range(16):
            for wi, w in ((0, w1), (1, w3)):
                for dc in range(DC):
                    r = it % 3
                    if it >= 3:
                        self.wait("SP", "mm", mmrec[it - 3])
                    else:
                        self.wait("SP", "mm", self.cnt["mm"])
                    vload = self.dma("SP", f"w{r}", wt1[r][:, :],
                                     w[dc * 128:(dc + 1) * 128, hg * 512:(hg + 1) * 512])
                    self.qwait("POOL", vload)
                    if it >= 3:
                        self.wait("POOL", "mm", mmrec[it - 3])
                    vcast = self.dma("POOL", f"wc{r}", wr1[r][:, :], wt1[r][:, :])
                    self.qwait("DVE", vcast)
                    vsub = self.op("DVE", "dve", lambda e, r=r: e.tensor_sub(
                        we1[r][:, :], wt1[r][:, :], wr1[r][:, :].bitcast(F32)))
                    self.wait("TE", "dve", vsub)
                    if dc == 0 and (hg > 0 or wi > 0):
                        self.wait("TE", evac_sem, evac_v)
                    for hc in range(4):
                        for t in range(2):
                            pb = self.P[hc * 2 + t]
                            mv_r = xnr[:, dc * TS + t * 512: dc * TS + t * 512 + 512]
                            mv_e = xne[:, dc * TS + t * 512: dc * TS + t * 512 + 512]
                            lr = wr1[r][:, hc * 128:(hc + 1) * 128]
                            le = we1[r][:, hc * 128:(hc + 1) * 128]
                            stt = (dc == 0)
                            stp = (dc == DC - 1)
                            last = hc == 3 and t == 1
                            self.op("TE", "mm", lambda e, pb=pb, l=lr, m=mv_r, s=stt: e.matmul(
                                pb[:, :], l, m, start=s, stop=False), inc=False)
                            self.op("TE", "mm", lambda e, pb=pb, l=lr, m=mv_e: e.matmul(
                                pb[:, :], l, m, start=False, stop=False), inc=False)
                            self.op("TE", "mm", lambda e, pb=pb, l=le, m=mv_r, p=stp: e.matmul(
                                pb[:, :], l, m, start=False, stop=p), inc=last)
                    mmrec[it] = self.cnt["mm"]
                    it += 1
                v_grp = self.cnt["mm"]
                if wi == 0:
                    self.wait("ACT", "mm", v_grp)
                    if hg > 0:
                        self.wait("ACT", "dve", h_done)
                    for j in range(8):
                        self.op("ACT", "act", lambda e, pb=self.P[j], j=j: e.activation(
                            out=ug[:, j * 512:(j + 1) * 512], in_=pb[:, :], func=AF.Silu),
                            inc=(j == 7))
                    evac_sem, evac_v = "act", self.cnt["act"]
                else:
                    self.wait("DVE", "mm", v_grp)
                    self.wait("DVE", "act", evac_v)
                    prev_sub = None
                    for j in range(8):
                        hc, t = divmod(j, 2)
                        gj = hg * 8 + j
                        b = hb[j % 4]
                        rbb = rb[j % 2]
                        reb = re[j % 2]
                        if gj >= 4:
                            self.wait("DVE", "dve", hsub[gj - 4])
                        vm = self.op("DVE", "dve", lambda e, pb=self.P[j], j=j, b=b: e.tensor_tensor(
                            b[:, :], ug[:, j * 512:(j + 1) * 512], pb[:, :], ALU.mult),
                            inc=True)
                        self.wait("POOL", "dve", vm)
                        if prev_sub is not None:
                            self.wait("POOL", "dve", prev_sub)
                        if h_latest is not None:
                            self.qwait("POOL", h_latest)
                        vc = self.dma("POOL", "pc0", rbb[:, :], b[:, :])
                        self.qwait("DVE", vc)
                        if h_latest is not None:
                            self.qwait("DVE", h_latest)
                        vs = self.op("DVE", "dve", lambda e, b=b, rbb=rbb, reb=reb: e.tensor_sub(
                            reb[:, :], b[:, :], rbb[:, :].bitcast(F32)))
                        hsub[gj] = vs
                        prev_sub = vs
                        row0 = (hg * 4 + hc) * 128
                        self.wait("POOL", "dve", vs)
                        self.dma("POOL", "ps0",
                                 self.d_hr[row0:row0 + 128, t * 512:(t + 1) * 512], rbb[:, :])
                        vo = self.dma("POOL", "ps0",
                                      self.d_he[row0:row0 + 128, t * 512:(t + 1) * 512],
                                      reb[:, :])
                        h_latest = vo
                    evac_sem, evac_v = "dve", self.cnt["dve"]
                    h_done = self.cnt["dve"]

    def ph_fwht(self):
        TT = 256
        for tt in range(4):
            self._fwht_tile(tt, TT)

    def _fwht_tile(self, tt, TT):
        BIG = self.BIG
        a = self.sb("wa", [128, HC * TT], F32, BIG)
        bq = self.sb("wb", [128, HC * TT], F32, BIG + 65536)
        hcr = [self.sb("whr", [128, TT], F32R, self.A + 1024 * i) for i in range(4)]
        hce = [self.sb("whe", [128, TT], F32R, self.A + 4096 + 1024 * i) for i in range(4)]
        rbnc = self.sb("wrb", [128, 8 * TT], F32R, self.C)
        ebnc = self.sb("web", [128, 8 * TT], F32, self.C + 8192)

        cols = (tt * TT, (tt + 1) * TT)
        self.qwait("ACT", self.qtotal("ys"))
        self.qwait("ACT", self.qtotal("dbg"))
        self.wait("TE", "act", self.cnt["act"])
        ev = {}
        ld = {}
        for c in range(HC):
            r = c % 4
            if c >= 4:
                self.wait("SP", "mm", ld[c - 4])
            else:
                self.wait("SP", "mm", self.cnt["mm"])
            rows = slice(c * 128, (c + 1) * 128)
            v1 = self.dma("SP", f"f{r}", hcr[r][:, :],
                          self.d_hr[rows, cols[0]:cols[1]])
            v2 = self.dma("SP", f"f{r}", hce[r][:, :],
                          self.d_he[rows, cols[0]:cols[1]])
            pb = self.P[c % 8]
            if c >= 8:
                self.wait("TE", "act", ev[c - 8])
            self.qwait("TE", v2)
            self.op("TE", "mm", lambda e, pb=pb, r=r: e.matmul(
                pb[:, 0:TT], self.h128r[:, :], hcr[r][:, :],
                start=True, stop=False), inc=False)
            v = self.op("TE", "mm", lambda e, pb=pb, r=r: e.matmul(
                pb[:, 0:TT], self.h128r[:, :], hce[r][:, :],
                start=False, stop=True))
            ld[c] = self.cnt["mm"]
            self.wait("ACT", "mm", v)
            ev[c] = self.op("ACT", "act", lambda e, pb=pb, c=c: e.activation(
                out=a[:, c * TT:(c + 1) * TT], in_=pb[:, 0:TT], func=AF.Copy,
                scale=FWHT_SCALE))
        v_y = self.cnt["act"]
        if self.st["debug"] and tt == 0:
            self.wait("POOL", "act", v_y)
            vdump = self.dma("POOL", "dbg", self.d_y1dbg[:, :], a[:, :])
            self.qwait("POOL", vdump)
            self.qwait("DVE", vdump)
        self.wait("DVE", "act", v_y)
        self.wait("POOL", "act", v_y)
        src, dst = a, bq
        for lev in range(6):
            st = 1 << lev
            npair = HC // (2 * st)
            def mk(sgn, half, src=src, dst=dst, st=st, npair=npair):
                def f(e):
                    o = dst[:, :].rearrange("p (g two s) -> p g two s", two=2,
                                            s=st * TT)
                    i0 = src[:, :].rearrange("p (g two s) -> p g two s", two=2,
                                             s=st * TT)
                    out = o[:, :, half, :]
                    return e.tensor_tensor(
                        out, i0[:, :, 0, :], i0[:, :, 1, :],
                        ALU.add if sgn > 0 else ALU.subtract)
                return f
            self.wait("DVE", "dve", self.cnt["dve"])
            self.wait("POOL", "pool", self.cnt["pool"])
            v0 = self.op("DVE", "dve", mk(+1, 0))
            v1 = self.op("POOL", "pool", mk(-1, 1))
            self.wait("DVE", "pool", v1)
            self.wait("POOL", "dve", v0)
            src, dst = dst, src
        v_res_dve = self.cnt["dve"]
        v_res_pool = self.cnt["pool"]
        self.wait("POOL", "dve", v_res_dve)
        self.wait("POOL", "pool", v_res_pool)
        if self.st["debug"] and tt == 0:
            vd = self.dma("POOL", "dbg", self.d_ybf[:, :], src[:, :])
            self.qwait("POOL", vd)
        prev = None
        for g in range(8):
            gs = slice(g * 8 * TT, (g + 1) * 8 * TT)
            if prev is not None:
                self.qwait("POOL", prev)
            vc = self.dma("POOL", "ys", rbnc[:, :], src[:, gs])
            self.qwait("DVE", vc)
            self.wait("DVE", "pool", v_res_pool)
            if prev is not None:
                self.qwait("DVE", prev)
            vs = self.op("DVE", "dve", lambda e, gs=gs: e.tensor_sub(
                ebnc[:, :], src[:, gs], rbnc[:, :].bitcast(F32)))
            self.wait("POOL", "dve", vs)
            rows = slice(g * 1024, (g + 1) * 1024)
            dst_r = self.fm(self.d_yr[rows, cols[0]:cols[1]])
            dst_e = self.fm(self.d_ye[rows, cols[0]:cols[1]])
            self.dma("POOL", "ys", dst_r, self.fs(rbnc[:, :], TT))
            vo = self.dma("POOL", "ys", dst_e, self.fs(ebnc[:, :].bitcast(F32R), TT))
            prev = vo

    def ph_w2(self):
        w2 = self.i_w["w2"]
        wt = [self.sb("w2t", [128, 1024], F32, self.C + 4096 * i) for i in range(2)]
        wr = [self.sb("w2r", [128, 1024], F32R, self.C + 8192 + 4096 * i) for i in range(2)]
        we = [self.sb("w2e", [128, 1024], F32R, self.C + 16384 + 4096 * i) for i in range(2)]
        yr = [self.sb("w2yr", [128, 512], F32R, self.A + 2048 * i) for i in range(2)]
        ye = [self.sb("w2ye", [128, 512], F32R, self.A + 4096 + 2048 * i) for i in range(2)]
        eb = [self.sb("w2b", [128, 512], F32, self.B + 2048 * i) for i in range(4)]
        mmrec = {}
        it = 0
        evac_v = 0
        ev_dma = [None, None, None, None]
        for t in range(2):
            for og in range(2):
                for hc in range(HC):
                    r = it % 2
                    if it >= 2:
                        self.wait("SP", "mm", mmrec[it - 2])
                    else:
                        self.wait("SP", "mm", self.cnt["mm"])
                    vl = self.dma("SP", f"w{r}", wt[r][:, :],
                                  w2[hc * 128:(hc + 1) * 128, og * 1024:(og + 1) * 1024])
                    vy1 = self.dma("SP", f"w{r}", yr[r][:, :],
                                   self.d_yr[hc * 128:(hc + 1) * 128, t * 512:(t + 1) * 512])
                    vy2 = self.dma("SP", f"w{r}", ye[r][:, :],
                                   self.d_ye[hc * 128:(hc + 1) * 128, t * 512:(t + 1) * 512])
                    self.qwait("POOL", vy2)
                    if it >= 2:
                        self.wait("POOL", "mm", mmrec[it - 2])
                    vcast = self.dma("POOL", f"wc{r}", wr[r][:, :], wt[r][:, :])
                    self.qwait("DVE", vcast)
                    vsub = self.op("DVE", "dve", lambda e, r=r: e.tensor_sub(
                        we[r][:, :], wt[r][:, :], wr[r][:, :].bitcast(F32)))
                    self.wait("TE", "dve", vsub)
                    self.qwait("TE", vy2)
                    if hc == 0 and it > 0:
                        self.wait("TE", "act", evac_v)
                    for oc in range(8):
                        pb = self.P[oc]
                        lr = wr[r][:, oc * 128:(oc + 1) * 128]
                        le = we[r][:, oc * 128:(oc + 1) * 128]
                        stt = (hc == 0)
                        stp = (hc == HC - 1)
                        last = oc == 7
                        self.op("TE", "mm", lambda e, pb=pb, l=lr, m=yr[r], s=stt: e.matmul(
                            pb[:, :], l, m[:, :], start=s, stop=False), inc=False)
                        self.op("TE", "mm", lambda e, pb=pb, l=lr, m=ye[r]: e.matmul(
                            pb[:, :], l, m[:, :], start=False, stop=False), inc=False)
                        self.op("TE", "mm", lambda e, pb=pb, l=le, m=yr[r], p=stp: e.matmul(
                            pb[:, :], l, m[:, :], start=False, stop=p), inc=last)
                    mmrec[it] = self.cnt["mm"]
                    it += 1
                v_grp = self.cnt["mm"]
                self.wait("ACT", "mm", v_grp)
                for oc in range(8):
                    b = eb[oc % 4]
                    if og * 8 + oc >= 4 or t > 0:
                        self.qwait("ACT", ev_dma[oc % 4])
                    va = self.op("ACT", "act", lambda e, pb=self.P[oc], b=b: e.activation(
                        out=b[:, :], in_=pb[:, :], func=AF.Silu))
                    self.wait("POOL", "act", va)
                    row0 = (og * 8 + oc) * 128
                    vo = self.dma("POOL", f"gs{oc % 4}",
                                  self.d_g2[row0:row0 + 128, t * 512:(t + 1) * 512], b[:, :])
                    ev_dma[oc % 4] = vo
                evac_v = self.cnt["act"]

    def ph_attn(self, qi):
        BIG = self.BIG
        nkb = NKB[qi]
        qr = self.sb("aqr", [128, DC * 512], F32R, BIG)
        qe = self.sb("aqe", [128, DC * 512], F32R, BIG + 32768)
        stor = self.sb("astr", [128, 16 * 512], F32R, BIG + 65536)
        stoe = self.sb("aste", [128, 16 * 512], F32R, BIG + 98304)
        kbr = [self.sb("akbr", [128, DC * 128], F32R, self.A + 8192 * i) for i in range(2)]
        kbe = [self.sb("akbe", [128, DC * 128], F32R, self.C + 8192 * i) for i in range(2)]
        mbuf = [self.sb("amsk", [128, 512], F32, self.M + 1600 + 2048 * i) for i in range(2)]
        sbnc = [self.sb("asb", [128, 512], F32, self.B + 2048 * i) for i in range(2)]
        srb = [self.sb("asr", [128, 512], F32R, self.B + 4096 + 2048 * i) for i in range(2)]

        cq = qi * 512
        v1 = self.dma("SP", "qq", self.fs(qr[:, :], 512),
                      self.fm(self.d_qr[:, cq:cq + 512]))
        v2 = self.dma("SP", "qq", self.fs(qe[:, :], 512),
                      self.fm(self.d_qe[:, cq:cq + 512]))
        self.qwait("TE", v2)
        kload = {}
        mload = {}
        st_done = {}
        for j, cb in enumerate(KBLIST[qi]):
            r = j % 2
            if j >= 2:
                self.wait("SP", "mm", st_done[j - 2][0])
            else:
                self.wait("SP", "mm", self.cnt["mm"])
            lk1 = self.dma("SP", f"kb{r}", self.fs(kbr[r][:, :], 128),
                           self.fm(self.d_kr[:, cb * 128:(cb + 1) * 128]))
            lk2 = self.dma("SP", f"kb{r}", self.fs(kbe[r][:, :], 128),
                           self.fm(self.d_ke[:, cb * 128:(cb + 1) * 128]))
            mi = (8 if qi else 0) + j if qi == 0 else 8 + j
            mi = j if qi == 0 else 8 + j
            if j >= 2:
                self.wait("ACT", "dve", st_done[j - 2][1])
            else:
                self.wait("ACT", "mm", self.cnt["mm"])
            lm = self.dma("ACT", f"mk{r}", mbuf[r][:, :], self.i_masks[mi, :, :])
            self.qwait("TE", lk2)
            pb = self.P[j % 2]
            if j >= 2:
                self.wait("TE", "dve", st_done[j - 2][1])
            for dc in range(DC):
                lr = kbr[r][:, dc * 128:(dc + 1) * 128]
                le = kbe[r][:, dc * 128:(dc + 1) * 128]
                mr = qr[:, dc * 512:(dc + 1) * 512]
                me = qe[:, dc * 512:(dc + 1) * 512]
                stt = (dc == 0)
                stp = (dc == DC - 1)
                self.op("TE", "mm", lambda e, pb=pb, l=lr, m=mr, s=stt: e.matmul(
                    pb[:, :], l, m, start=s, stop=False), inc=False)
                self.op("TE", "mm", lambda e, pb=pb, l=lr, m=me: e.matmul(
                    pb[:, :], l, m, start=False, stop=False), inc=False)
                self.op("TE", "mm", lambda e, pb=pb, l=le, m=mr, p=stp: e.matmul(
                    pb[:, :], l, m, start=False, stop=p), inc=stp)
            v_s = self.cnt["mm"]
            self.wait("DVE", "mm", v_s)
            self.qwait("DVE", lm)
            b = sbnc[r]
            rbb = srb[r]
            if j >= 2:
                self.qwait("DVE", st_done[j - 2][2])
            vm = self.op("DVE", "dve", lambda e, pb=pb, b=b, m=mbuf[r]: e.tensor_tensor(
                b[:, :], pb[:, :], m[:, :], ALU.mult), inc=True)
            self.wait("POOL", "dve", vm)
            vc = self.dma("POOL", "ogc", stor[:, j * 512:(j + 1) * 512], b[:, :])
            self.qwait("DVE", vc)
            vs = self.op("DVE", "dve", lambda e, b=b, j=j: e.tensor_sub(
                stoe[:, j * 512:(j + 1) * 512], b[:, :],
                stor[:, j * 512:(j + 1) * 512].bitcast(F32)))
            st_done[j] = (v_s, vs, vc)
        v_scores = self.cnt["dve"]

        ogr = self.sb("aogr", [128, DC * 512], F32R, BIG)
        oge = self.sb("aoge", [128, DC * 512], F32R, BIG + 32768)
        vb = [self.sb("avb", [128, 512], F32R, self.A + 2048 * i) for i in range(4)]
        gb = [self.sb("agb", [128, 512], F32, self.A + 8192 + 2048 * i) for i in range(2)]
        ob = [self.sb("aob", [128, 512], F32, self.B + 2048 * i) for i in range(2)]
        orb = [self.sb("aor", [128, 512], F32R, self.B + 4096 + 2048 * i) for i in range(2)]
        self.wait("SP", "mm", self.cnt["mm"])
        self.wait("ACT", "mm", self.cnt["mm"])
        self.wait("TE", "dve", v_scores)
        vload = {}
        og_ev = 0
        o_ev = [None, None]
        guse = [0, 0]
        for og in range(4):
            for j, cb in enumerate(KBLIST[qi]):
                r = j % 2
                row0 = cb * 128
                if (og * nkb + j) >= 2:
                    self.wait("SP", "mm", vload[og * nkb + j - 2])
                else:
                    self.wait("SP", "mm", self.cnt["mm"])
                l1 = self.dma("SP", f"vb{r}", vb[r * 2][:, :],
                              self.d_vr[row0:row0 + 128, og * 512:(og + 1) * 512])
                l2 = self.dma("SP", f"vb{r}", vb[r * 2 + 1][:, :],
                              self.d_ve[row0:row0 + 128, og * 512:(og + 1) * 512])
                self.qwait("TE", l2)
                if j == 0 and og > 0:
                    self.wait("TE", "dve", og_ev)
                for oc in range(4):
                    pb = self.P[4 * (og % 2) + oc]
                    lr = vb[r * 2][:, oc * 128:(oc + 1) * 128]
                    le = vb[r * 2 + 1][:, oc * 128:(oc + 1) * 128]
                    mr = stor[:, j * 512:(j + 1) * 512]
                    me = stoe[:, j * 512:(j + 1) * 512]
                    stt = (j == 0)
                    stp = (j == nkb - 1)
                    last = oc == 3
                    self.op("TE", "mm", lambda e, pb=pb, l=lr, m=mr, s=stt: e.matmul(
                        pb[:, :], l, m, start=s, stop=False), inc=False)
                    self.op("TE", "mm", lambda e, pb=pb, l=le, m=mr: e.matmul(
                        pb[:, :], l, m, start=False, stop=False), inc=False)
                    self.op("TE", "mm", lambda e, pb=pb, l=lr, m=me, p=stp: e.matmul(
                        pb[:, :], l, m, start=False, stop=p), inc=last)
                vload[og * nkb + j] = self.cnt["mm"]
            v_og = self.cnt["mm"]
            self.wait("DVE", "mm", v_og)
            for oc in range(4):
                ch = og * 4 + oc
                b = ob[oc % 2]
                rbb = orb[oc % 2]
                row0 = ch * 128
                if ch >= 2:
                    self.wait("ACT", "dve", guse[oc % 2])
                else:
                    self.wait("ACT", "mm", self.cnt["mm"])
                lg = self.dma("ACT", f"gl{oc % 2}", gb[oc % 2][:, :],
                              self.d_g[row0:row0 + 128, cq:cq + 512])
                self.qwait("DVE", lg)
                if ch >= 2:
                    self.qwait("DVE", o_ev[oc % 2])
                vm = self.op("DVE", "dve", lambda e, pb=self.P[4 * (og % 2) + oc], b=b,
                        g=gb[oc % 2]: e.tensor_tensor(b[:, :], pb[:, :], g[:, :],
                                                      ALU.mult), inc=True)
                guse[oc % 2] = vm
                self.wait("POOL", "dve", vm)
                vc = self.dma("POOL", "ogc", ogr[:, ch * 512:(ch + 1) * 512], b[:, :])
                o_ev[oc % 2] = vc
                self.qwait("DVE", vc)
                self.op("DVE", "dve", lambda e, b=b, ch=ch: e.tensor_sub(
                    oge[:, ch * 512:(ch + 1) * 512], b[:, :],
                    ogr[:, ch * 512:(ch + 1) * 512].bitcast(F32)))
            og_ev = self.cnt["dve"]
        v_sv = self.cnt["dve"]

        self.wait("SP", "mm", self.cnt["mm"])
        wp = self.i_w["wp"]
        wt = [self.sb("apt", [128, 1024], F32, self.C + 4096 * i) for i in range(2)]
        wr = [self.sb("apr", [128, 1024], F32R, self.C + 8192 + 4096 * i) for i in range(2)]
        we = [self.sb("ape", [128, 1024], F32R, self.C + 16384 + 4096 * i) for i in range(2)]
        xg = self.sb("axg", [128, DC * 512], F32, BIG + 65536)
        g2t = self.sb("ag2", [128, DC * 512], F32, BIG + 98304)
        obo = [self.sb("aout", [128, 512], F32, self.B + 2048 * i) for i in range(4)]
        self.wait("ACT", "mm", self.cnt["mm"])
        vxg = self.dma("ACT", "xg", self.fs(xg[:, :], 512),
                       self.fm(self.i_x[:, cq:cq + 512]))
        vg2 = self.dma("ACT", "xg", self.fs(g2t[:, :], 512),
                       self.fm(self.d_g2[:, cq:cq + 512]))
        self.wait("TE", "dve", v_sv)
        mmrec = {}
        it = 0
        evac_v = 0
        out_dma = [None, None, None, None]
        for og2 in range(2):
            for dc in range(DC):
                r = it % 2
                if it >= 2:
                    self.wait("SP", "mm", mmrec[it - 2])
                else:
                    self.wait("SP", "mm", self.cnt["mm"])
                vl = self.dma("SP", f"w{r}", wt[r][:, :],
                              wp[dc * 128:(dc + 1) * 128, og2 * 1024:(og2 + 1) * 1024])
                self.qwait("POOL", vl)
                if it >= 2:
                    self.wait("POOL", "mm", mmrec[it - 2])
                vcast = self.dma("POOL", f"wc{r}", wr[r][:, :], wt[r][:, :])
                self.qwait("DVE", vcast)
                vsub = self.op("DVE", "dve", lambda e, r=r: e.tensor_sub(
                    we[r][:, :], wt[r][:, :], wr[r][:, :].bitcast(F32)))
                self.wait("TE", "dve", vsub)
                if dc == 0 and og2 > 0:
                    self.wait("TE", "dve", evac_v)
                for oc in range(8):
                    pb = self.P[oc]
                    lr = wr[r][:, oc * 128:(oc + 1) * 128]
                    le = we[r][:, oc * 128:(oc + 1) * 128]
                    mr = ogr[:, dc * 512:(dc + 1) * 512]
                    me = oge[:, dc * 512:(dc + 1) * 512]
                    stt = (dc == 0)
                    stp = (dc == DC - 1)
                    last = oc == 7
                    self.op("TE", "mm", lambda e, pb=pb, l=lr, m=mr, s=stt: e.matmul(
                        pb[:, :], l, m, start=s, stop=False), inc=False)
                    self.op("TE", "mm", lambda e, pb=pb, l=lr, m=me: e.matmul(
                        pb[:, :], l, m, start=False, stop=False), inc=False)
                    self.op("TE", "mm", lambda e, pb=pb, l=le, m=mr, p=stp: e.matmul(
                        pb[:, :], l, m, start=False, stop=p), inc=last)
                mmrec[it] = self.cnt["mm"]
                it += 1
            v_grp = self.cnt["mm"]
            self.wait("DVE", "mm", v_grp)
            self.qwait("DVE", vg2)
            for oc in range(8):
                ch = og2 * 8 + oc
                b = obo[oc % 4]
                if ch >= 4:
                    self.qwait("DVE", out_dma[oc % 4])
                vmul = self.op("DVE", "dve", lambda e, pb=self.P[oc], b=b, ch=ch: e.tensor_tensor(
                    b[:, :], pb[:, :], g2t[:, ch * 512:(ch + 1) * 512], ALU.mult),
                    inc=True)
                self.wait("DVE", "dve", vmul)
                vm = self.op("DVE", "dve", lambda e, b=b, ch=ch: e.tensor_tensor(
                    b[:, :], b[:, :], xg[:, ch * 512:(ch + 1) * 512], ALU.add))
                self.wait("ACT", "dve", vm)
                row0 = ch * 128
                vo = self.dma("ACT", f"o{oc % 4}",
                              self.o_out[row0:row0 + 128, cq:cq + 512], b[:, :])
                out_dma[oc % 4] = vo
            evac_v = self.cnt["dve"]

    def build(self):
        self.ph0()
        if self.st["p1"]:
            self.fence()
            self.ph1_slab(1)
            self.fence()
            self.ph1_slab(0)
        if self.st["ffn"]:
            self.fence()
            self.ph_ffn()
        if self.st["fwht"]:
            self.fence()
            self.ph_fwht()
        if self.st["w2"]:
            self.fence()
            self.ph_w2()
        if self.st["attn"]:
            self.fence()
            self.ph_attn(0)
            self.fence()
            self.ph_attn(1)
        self.fence()
        for s in ("o0", "o1", "o2", "o3"):
            self.qwait("ACT", self.qtotal(s))
            self.qwait("POOL", self.qtotal(s))
        with self.nc.Block() as block:
            @block.tensor
            def _(e):
                for f in self.ops["TE"]:
                    f(e)

            @block.scalar
            def _(e):
                for f in self.ops["ACT"]:
                    f(e)

            @block.vector
            def _(e):
                for f in self.ops["DVE"]:
                    f(e)

            @block.gpsimd
            def _(e):
                for f in self.ops["POOL"]:
                    f(e)

            @block.sync
            def _(e):
                for f in self.ops["SP"]:
                    f(e)
        self.ctx.close()
        return self.nc


def _hadamard(n):
    h = np.array([[1.0]], dtype=np.float32)
    while h.shape[0] < n:
        h = np.block([[h, h], [h, -h]])
    return h.astype(np.float32)


def _staging(core, x):
    b, h = core // 2, core % 2
    own = np.concatenate([np.arange((2 * m + h) * 128, (2 * m + h) * 128 + 128)
                          for m in range(8)])
    par = np.concatenate([np.arange((2 * m + 1 - h) * 128, (2 * m + 1 - h) * 128 + 128)
                          for m in range(8)])
    perm = np.concatenate([own, par])
    xT = np.ascontiguousarray(x[b][perm, :].T)
    gidx = perm
    masks = np.zeros((24, 128, 512), dtype=np.float32)
    row = 0
    for qi in (0, 1):
        gq = gidx[qi * 512: qi * 512 + 512]
        for cb in KBLIST[qi]:
            gk = gidx[cb * 128:(cb + 1) * 128]
            masks[row] = (gk[:, None] <= gq[None, :]).astype(np.float32)
            row += 1
    return xT, masks, own


def kernel(L0_attn_x, L0_attn_Wq, L0_attn_Wk, L0_attn_Wv, L0_attn_Wg, L0_attn_Wp,
           L0_ffn_W1, L0_ffn_W2, L0_ffn_W3):
    x = np.ascontiguousarray(np.asarray(L0_attn_x, dtype=np.float32))
    ws = {
        "wqT": np.ascontiguousarray(np.asarray(L0_attn_Wq).T),
        "wkT": np.ascontiguousarray(np.asarray(L0_attn_Wk).T),
        "wvT": np.ascontiguousarray(np.asarray(L0_attn_Wv).T),
        "wgT": np.ascontiguousarray(np.asarray(L0_attn_Wg).T),
        "wpT": np.ascontiguousarray(np.asarray(L0_attn_Wp).T),
        "w1T": np.ascontiguousarray(np.asarray(L0_ffn_W1).T),
        "w3T": np.ascontiguousarray(np.asarray(L0_ffn_W3).T),
        "w2T": np.ascontiguousarray(np.asarray(L0_ffn_W2).T),
    }
    had = _hadamard(128)
    in_maps = []
    owns = []
    for c in range(8):
        xT, masks, own = _staging(c, x)
        owns.append(own)
        m = {"xT": xT, "masks": masks, "had128": had}
        m.update(ws)
        in_maps.append(m)

    nc = Builder().build()
    res = run_bass_kernel_spmd(nc, in_maps, core_ids=list(range(8)))
    out = np.empty((4, 2048, 2048), dtype=np.float32)
    for c in range(8):
        b = c // 2
        out[b][owns[c], :] = res.results[c]["outT"].T
    return out


# version 04
# speedup vs baseline: 1.2859x; 1.1384x over previous
from contextlib import ExitStack

import numpy as np
import concourse.bass as bass
import concourse.mybir as mybir
from concourse.bass_utils import run_bass_kernel_spmd

F32 = mybir.dt.float32
F32R = mybir.dt.float32r
AF = mybir.ActivationFunctionType
ALU = mybir.AluOpType

D = 2048
DC = 16
TS = 1024
HIDDEN = 8192
HC = 64
NKB = {0: 8, 1: 16}
KBLIST = {0: [0, 1, 2, 3, 8, 9, 10, 11], 1: list(range(16))}
LN_EPS = 1e-5
RMS_EPS = float(np.finfo(np.float32).eps)
FWHT_SCALE = float(1.0 / np.sqrt(HIDDEN))

ENGINES = ("TE", "ACT", "DVE", "POOL", "SP")

STAGES = {"p1": True, "ffn": True, "fwht": True, "w2": True, "attn": True,
          "debug": False}


def ceil32(x):
    return (x + 31) & ~31


class Builder:
    def __init__(self, stages=STAGES):
        self.st = dict(stages)
        nc = bass.Bass("TRN2", target_bir_lowering=False)
        self.nc = nc
        self.ops = {e: [] for e in ENGINES}
        self.cnt = {}
        self.sem = {}
        self.ctx = ExitStack()
        for name in ("mm", "act", "dve", "pool", "sp"):
            self.sem[name] = self.ctx.enter_context(nc.semaphore("s_" + name))
            self.cnt[name] = 0
        self.qsem, self.qcnt, self.qeng = {}, {}, {}
        self._declare_dram()
        self._declare_sbuf()
        self._declare_psum()

    def inc(self, name, n=1):
        self.cnt[name] += n
        return self.cnt[name]

    def emit(self, eng, fn):
        self.ops[eng].append(fn)

    def wait(self, eng, sem, val):
        if val <= 0:
            return
        s = self.sem[sem]
        self.emit(eng, lambda e, s=s, v=val: e.wait_ge(s, v))

    def dma(self, eng, stream, out, in_):
        if stream not in self.qsem:
            self.qsem[stream] = self.ctx.enter_context(self.nc.semaphore("q_" + stream))
            self.qcnt[stream] = 0
            self.qeng[stream] = eng
        assert self.qeng[stream] == eng, (stream, eng)
        self.qcnt[stream] += 16
        v = self.qcnt[stream]
        s = self.qsem[stream]
        self.emit(eng, lambda e, o=out, i=in_, s=s: e.dma_start(out=o, in_=i).then_inc(s, 16))
        return (stream, v)

    def qwait(self, eng, tok):
        if tok is None:
            return
        stream, v = tok
        if v <= 0:
            return
        s = self.qsem[stream]
        self.emit(eng, lambda e, s=s, v=v: e.wait_ge(s, v))

    def qtotal(self, stream):
        return (stream, self.qcnt.get(stream, 0))

    @staticmethod
    def fm(ap2d):
        return ap2d.rearrange("(c p) t -> p c t", p=128)

    @staticmethod
    def fs(sb_ap, t):
        return sb_ap.rearrange("p (c t) -> p c t", t=t)

    def op(self, eng, esem, fn, inc=True):
        if inc:
            v = self.inc(esem)
            s = self.sem[esem]
            self.emit(eng, lambda e, f=fn, s=s: f(e).then_inc(s, 1))
            return v
        self.emit(eng, lambda e, f=fn: f(e))
        return self.cnt[esem]

    def fence(self, tag=""):
        vals = {}
        for e in ENGINES:
            for stream, owner in self.qeng.items():
                if owner == e:
                    self.qwait(e, self.qtotal(stream))
            es = {"TE": "mm", "ACT": "act", "DVE": "dve", "POOL": "pool", "SP": "sp"}[e]
            self.wait(e, es, self.cnt[es])
            v = self.inc(es)
            s = self.sem[es]
            self.emit(e, lambda en, s=s: en.nop().then_inc(s, 1))
            vals[e] = (es, v)
        for e in ENGINES:
            for o in ENGINES:
                if o != e:
                    sname, v = vals[o]
                    self.wait(e, sname, v)

    def _declare_dram(self):
        nc = self.nc
        t = nc.dram_tensor
        self.i_x = t("xT", [D, 2 * TS], F32, kind="ExternalInput").ap()
        self.i_w = {}
        for nm in ("wq", "wk", "wv", "wg", "wp"):
            self.i_w[nm] = t(nm + "T", [D, D], F32, kind="ExternalInput").ap()
        self.i_w["w1"] = t("w1T", [D, HIDDEN], F32, kind="ExternalInput").ap()
        self.i_w["w3"] = t("w3T", [D, HIDDEN], F32, kind="ExternalInput").ap()
        self.i_w["w2"] = t("w2T", [HIDDEN, D], F32, kind="ExternalInput").ap()
        self.i_masks = t("masks", [24, 128, 512], F32, kind="ExternalInput").ap()
        self.i_h128 = t("had128", [128, 128], F32, kind="ExternalInput").ap()

        kind = "ExternalOutput" if self.st["debug"] else "Internal"
        if self.st["debug"]:
            self.d_x2r = t("x2r_d", [D, TS], F32R, kind=kind).ap()
            self.d_x2e = t("x2e_d", [D, TS], F32R, kind=kind).ap()
            self.d_y1dbg = t("y1dbg_d", [128, HC * 256], F32, kind=kind).ap()
            self.d_ybf = t("ybf_d", [128, HC * 256], F32, kind=kind).ap()
        self.d_kr = t("kr_d", [D, 2 * TS], F32R, kind=kind).ap()
        self.d_ke = t("ke_d", [D, 2 * TS], F32R, kind=kind).ap()
        self.d_vr = t("vr_d", [2 * TS, D], F32R, kind=kind).ap()
        self.d_ve = t("ve_d", [2 * TS, D], F32R, kind=kind).ap()
        self.d_qr = t("qr_d", [D, TS], F32R, kind=kind).ap()
        self.d_qe = t("qe_d", [D, TS], F32R, kind=kind).ap()
        self.d_g = t("g_d", [D, TS], F32, kind=kind).ap()
        self.d_xn = t("xn_d", [D, TS], F32, kind=kind).ap()
        self.d_hr = t("hr_d", [HIDDEN, TS], F32R, kind=kind).ap()
        self.d_he = t("he_d", [HIDDEN, TS], F32R, kind=kind).ap()
        self.d_yr = t("yr_d", [HIDDEN, TS], F32R, kind=kind).ap()
        self.d_ye = t("ye_d", [HIDDEN, TS], F32R, kind=kind).ap()
        self.d_g2 = t("g2_d", [D, TS], F32, kind=kind).ap()
        self.o_out = t("outT", [D, TS], F32, kind="ExternalOutput").ap()

    def _declare_sbuf(self):
        nc = self.nc
        at = nc.alloc_sbuf_tensor_at
        base0 = nc.sbuf_base
        M = ceil32(base0)
        self.M = M
        self.A = M + 18432
        self.B = M + 38912
        self.C = M + 53248
        self.BIG = M + 77824
        end = self.BIG + 131072
        nc.alloc_sbuf_tensor("manual_map_reserved",
                             [128, end - base0], mybir.dt.uint8)
        self.ones_col = at("ones_col", [128, 1], F32, offset=M).ap()
        self.ones_row = at("ones_row", [1, 128], F32, offset=M + 32).ap()
        self.h128r = at("h128r", [128, 128], F32R, offset=M + 544).ap()
        self.h128f = at("h128f", [128, 128], F32, offset=M + 1056).ap()
        self.stat = [at(f"stat{i}", [1, 1024], F32, offset=M + 1600 + 4096 * i).ap()
                     for i in range(4)]
        self._tens = {}

    def sb(self, name, shape, dtype, off):
        key = (name, tuple(shape), dtype, off)
        if key not in self._tens:
            self._tens[key] = self.nc.alloc_sbuf_tensor_at(
                f"{name}_{off}", shape, dtype, offset=off).ap()
        return self._tens[key]

    def _declare_psum(self):
        self.P = [self.ctx.enter_context(
            self.nc.psum_tensor(f"pb{i}", [128, 512], F32)) for i in range(8)]

    def ph0(self):
        self.op("DVE", "dve", lambda e: e.memset(self.ones_col[:, :], 1.0), inc=False)
        self.op("DVE", "dve", lambda e: e.memset(self.ones_row[:, :], 1.0))
        v = self.dma("POOL", "ini", self.h128f[:, :], self.i_h128[:, :])
        self.qwait("POOL", v)
        vc = self.dma("POOL", "ini", self.h128r[:, :], self.h128f[:, :])
        self.qwait("TE", vc)
        self.wait("TE", "dve", self.cnt["dve"])

    def ph1_slab(self, s):
        c0 = s * TS
        own = (s == 0)
        BIG = self.BIG
        xn = self.sb("xn", [128, DC * TS], F32, BIG)
        xn2r = self.sb("xn2r", [128, DC * TS], F32R, BIG + 65536)
        xn2e = self.sb("xn2e", [128, DC * TS], F32R, BIG)
        sq = [self.sb("sqb", [128, 1024], F32, self.B + 4096 * i) for i in range(2)]
        bnc = [self.sb("lnb", [128, 1024], F32, self.B + 8192 + 4096 * i) for i in range(2)]

        src = self.fm(self.i_x[:, c0:c0 + TS])
        vx = self.dma("ACT", "xs", self.fs(xn[:, :], TS), src)

        self.qwait("TE", vx)
        self.qwait("ACT", vx)
        mm_done = []
        sq_done = [0, 0]
        for dc in range(DC):
            xc = xn[:, dc * TS:(dc + 1) * TS]
            sqb = sq[dc % 2]
            if dc >= 2:
                self.wait("ACT", "mm", sq_done[dc % 2])
            va = self.op("ACT", "act", lambda e, i=xc, o=sqb: e.activation(
                out=o[:, :], in_=i, func=AF.Square))
            for t in range(2):
                last = (dc == DC - 1)
                self.op("TE", "mm", lambda e, i=xc, t=t, dc=dc, last=last: e.matmul(
                    self.P[t][0:1, :], self.ones_col[:, :], i[:, t * 512:(t + 1) * 512],
                    start=(dc == 0), stop=last), inc=last)
            self.wait("TE", "act", va)
            for t in range(2):
                last = (dc == DC - 1)
                v = self.op("TE", "mm", lambda e, i=sqb, t=t, dc=dc, last=last: e.matmul(
                    self.P[2 + t][0:1, :], self.ones_col[:, :], i[:, t * 512:(t + 1) * 512],
                    start=(dc == 0), stop=last), inc=True)
                sq_done[dc % 2] = v
        v_stats = self.cnt["mm"]

        st0, st1, st2, st3 = self.stat
        self.wait("DVE", "mm", v_stats)

        def R(fn):
            self.wait("DVE", "dve", self.cnt["dve"])
            return self.op("DVE", "dve", fn, inc=True)
        for t in range(2):
            R(lambda e, t=t: e.tensor_scalar_mul(st0[:, t * 512:(t + 1) * 512],
                                                 self.P[t][0:1, :], 1.0 / D))
            R(lambda e, t=t: e.tensor_scalar_mul(st1[:, t * 512:(t + 1) * 512],
                                                 self.P[2 + t][0:1, :], 1.0 / D))
        R(lambda e: e.tensor_tensor(st2[:, :], st0[:, :], st0[:, :], ALU.mult))
        R(lambda e: e.tensor_sub(st1[:, :], st1[:, :], st2[:, :]))
        self._rsqrt_row(st1, st2, st3, LN_EPS)
        self.wait("DVE", "dve", self.cnt["dve"])
        v_rows = self.op("DVE", "dve", lambda e: e.tensor_tensor(
            st3[:, :], st0[:, :], st2[:, :], ALU.mult))

        self.wait("TE", "dve", v_rows)
        for t in range(2):
            self.op("TE", "mm", lambda e, t=t: e.matmul(
                self.P[4 + t][:, :], self.ones_row[:, :], st2[:, t * 512:(t + 1) * 512],
                start=True, stop=True), inc=False)
            self.op("TE", "mm", lambda e, t=t: e.matmul(
                self.P[6 + t][:, :], self.ones_row[:, :], st3[:, t * 512:(t + 1) * 512],
                start=True, stop=True), inc=(t == 1))
        v_bcast = self.cnt["mm"]

        self.wait("DVE", "mm", v_bcast)
        xn_chunk_done = []
        for dc in range(DC):
            xc = xn[:, dc * TS:(dc + 1) * TS]
            b = bnc[dc % 2]
            for t in range(2):
                R(lambda e, i=xc, b=b, t=t: e.tensor_tensor(
                    b[:, t * 512:(t + 1) * 512], i[:, t * 512:(t + 1) * 512],
                    self.P[4 + t][:, :], ALU.mult))
            self.wait("DVE", "dve", self.cnt["dve"])
            v = self.op("DVE", "dve", lambda e, i=xc, b=b: e.tensor_tensor(
                i[:, 0:512], b[:, 0:512], self.P[6][:, :], ALU.subtract), inc=True)
            self.wait("DVE", "dve", self.cnt["dve"])
            v = self.op("DVE", "dve", lambda e, i=xc, b=b: e.tensor_tensor(
                i[:, 512:1024], b[:, 512:1024], self.P[7][:, :], ALU.subtract))
            xn_chunk_done.append(v)
        if own:
            self.wait("POOL", "dve", xn_chunk_done[-1])
            self.v_xn_dma = self.dma("POOL", "xnst", self.fm(self.d_xn[:, :]),
                                     self.fs(xn[:, :], TS))

        sq_done = [0, 0]
        for dc in range(DC):
            xc = xn[:, dc * TS:(dc + 1) * TS]
            sqb = sq[dc % 2]
            self.wait("ACT", "dve", xn_chunk_done[dc])
            if dc >= 2:
                self.wait("ACT", "mm", sq_done[dc % 2])
            va = self.op("ACT", "act", lambda e, i=xc, o=sqb: e.activation(
                out=o[:, :], in_=i, func=AF.Square))
            self.wait("TE", "act", va)
            for t in range(2):
                last = (dc == DC - 1)
                v = self.op("TE", "mm", lambda e, i=sqb, t=t, dc=dc, last=last: e.matmul(
                    self.P[t][0:1, :], self.ones_col[:, :], i[:, t * 512:(t + 1) * 512],
                    start=(dc == 0), stop=last), inc=True)
                sq_done[dc % 2] = v
        v_stats2 = self.cnt["mm"]
        self.wait("DVE", "mm", v_stats2)
        for t in range(2):
            R(lambda e, t=t: e.tensor_scalar_mul(st0[:, t * 512:(t + 1) * 512],
                                                 self.P[t][0:1, :], 1.0 / D))
        self._rsqrt_row(st0, st2, st3, RMS_EPS)
        v_rows2 = self.cnt["dve"]
        self.wait("TE", "dve", v_rows2)
        for t in range(2):
            self.op("TE", "mm", lambda e, t=t: e.matmul(
                self.P[4 + t][:, :], self.ones_row[:, :], st2[:, t * 512:(t + 1) * 512],
                start=True, stop=True), inc=(t == 1))
        v_bc2 = self.cnt["mm"]

        self.wait("DVE", "mm", v_bc2)
        for dc in range(DC):
            xc = xn[:, dc * TS:(dc + 1) * TS]
            b = bnc[dc % 2]
            if dc >= 2:
                self.qwait("DVE", self._xn2cast[dc % 2])
            for t in range(2):
                R(lambda e, i=xc, b=b, t=t: e.tensor_tensor(
                    b[:, t * 512:(t + 1) * 512], i[:, t * 512:(t + 1) * 512],
                    self.P[4 + t][:, :], ALU.mult))
            vb = self.cnt["dve"]
            self.wait("POOL", "dve", vb)
            if own and dc == 0:
                pass
            vc = self.dma("POOL", "xc", xn2r[:, dc * TS:(dc + 1) * TS], b[:, :])
            if not hasattr(self, "_xn2cast"):
                self._xn2cast = [None, None]
            self._xn2cast[dc % 2] = vc
            self.qwait("DVE", vc)
            if own:
                self.qwait("DVE", self.v_xn_dma)
            self.op("DVE", "dve", lambda e, b=b, dc=dc: e.tensor_sub(
                xn2e[:, dc * TS:(dc + 1) * TS], b[:, :],
                xn2r[:, dc * TS:(dc + 1) * TS].bitcast(F32)))
        v_split = self.cnt["dve"]

        if self.st["debug"] and own:
            self.wait("POOL", "dve", v_split)
            self.dma("POOL", "dbg", self.fm(self.d_x2r[:, :]), self.fs(xn2r[:, :], TS))
            self.dma("POOL", "dbg", self.fm(self.d_x2e[:, :]), self.fs(xn2e[:, :], TS))

        self.fence()
        self.proj_featmaj("wk", self.i_w["wk"], AF.Silu, self.d_kr, self.d_ke,
                          xn2r, xn2e, out_cols=(c0, TS), v_ready=v_split)
        self.fence()
        self.proj_tokmaj(xn2r, xn2e, slab_off=c0, v_ready=v_split)
        if own:
            self.fence()
            self.proj_featmaj("wq", self.i_w["wq"], AF.Silu, self.d_qr, self.d_qe,
                              xn2r, xn2e, out_cols=(0, TS), v_ready=v_split)
            self.fence()
            self.proj_featmaj("wg", self.i_w["wg"], AF.Sigmoid, self.d_g, None,
                              xn2r, xn2e, out_cols=(0, TS), v_ready=v_split)

    def _rsqrt_row(self, vrow, out, tmp, eps):
        def R(fn):
            self.wait("DVE", "dve", self.cnt["dve"])
            return self.op("DVE", "dve", fn, inc=True)

        def A(fn):
            self.wait("ACT", "act", self.cnt["act"])
            return self.op("ACT", "act", fn, inc=True)
        R(lambda e: e.tensor_scalar_add(vrow[:, :], vrow[:, :], eps))
        self.wait("ACT", "dve", self.cnt["dve"])
        va = self.op("ACT", "act", lambda e: e.activation(out=tmp[:, :], in_=vrow[:, :],
                                                          func=AF.Sqrt))
        self.wait("DVE", "act", va)
        R(lambda e: e.reciprocal(out[:, :], tmp[:, :]))
        for i in range(2):
            R(lambda e: e.tensor_tensor(tmp[:, :], out[:, :], out[:, :], ALU.mult))
            R(lambda e: e.tensor_tensor(tmp[:, :], tmp[:, :], vrow[:, :], ALU.mult))
            R(lambda e: e.tensor_scalar(tmp[:, :], tmp[:, :], -0.5, 1.5,
                                        ALU.mult, ALU.add))
            self.wait("DVE", "dve", self.cnt["dve"])
            self.op("DVE", "dve", lambda e: e.tensor_tensor(
                out[:, :], out[:, :], tmp[:, :], ALU.mult), inc=True)

    def proj_featmaj(self, nm, w, act, dr, de, xr, xe, out_cols, v_ready,
                    n_out=D):
        c0, TS_ = out_cols
        wtmp = [self.sb(nm + "wt", [128, 512], F32, self.A + 2048 * i) for i in range(3)]
        wr = [self.sb(nm + "wr", [128, 512], F32R, self.A + 6144 + 2048 * i) for i in range(3)]
        we = [self.sb(nm + "we", [128, 512], F32R, self.A + 12288 + 2048 * i) for i in range(3)]
        rb = [self.sb(nm + "rb", [128, 512], F32R, self.B + 2048 * i) for i in range(2)]
        ebw = self.sb(nm + "ebw", [128, 8 * 512], F32, self.C + 2048)
        n_og = n_out // 512
        self.wait("TE", "dve", v_ready)
        mmrec = {}
        it = 0
        evac_v = 0
        st = {"prev_sub": None, "latest": None, "last_cast": None}
        pending = []

        def mkstep(og, j, va):
            def step():
                oc, t = divmod(j, 2)
                bsl = ebw[:, j * 512:(j + 1) * 512]
                rbb = rb[j % 2]
                row0 = (og * 4 + oc) * 128
                dst_cols = (c0 + t * 512, c0 + t * 512 + 512)
                self.wait("POOL", "act", va)
                if de is None:
                    vo = self.dma("POOL", "ps0",
                                  dr[row0:row0 + 128, dst_cols[0]:dst_cols[1]], bsl)
                    st["latest"] = vo
                    st["last_cast"] = vo
                    return
                if st["prev_sub"] is not None:
                    self.wait("POOL", "dve", st["prev_sub"])
                if st["latest"] is not None:
                    self.qwait("POOL", st["latest"])
                vc = self.dma("POOL", "pc0", rbb[:, :], bsl)
                st["last_cast"] = vc
                self.qwait("DVE", vc)
                vs = self.op("DVE", "dve", lambda e, b=bsl, rbb=rbb: e.tensor_sub(
                    b, b, rbb[:, :].bitcast(F32)))
                st["prev_sub"] = vs
                self.wait("POOL", "dve", vs)
                self.dma("POOL", "ps0",
                         dr[row0:row0 + 128, dst_cols[0]:dst_cols[1]], rbb[:, :])
                vo = self.dma("POOL", "ps0",
                              de[row0:row0 + 128, dst_cols[0]:dst_cols[1]],
                              bsl.bitcast(F32R))
                st["latest"] = vo
            return step

        for og in range(n_og):
            for dc in range(DC):
                r = it % 3
                if it >= 3:
                    self.wait("SP", "mm", mmrec[(it - 3)])
                else:
                    self.wait("SP", "mm", self.cnt["mm"])
                vload = self.dma("SP", f"w{r}", wtmp[r][:, :],
                                 w[dc * 128:(dc + 1) * 128, og * 512:(og + 1) * 512])
                self.qwait("POOL", vload)
                if it >= 3:
                    self.wait("POOL", "mm", mmrec[(it - 3)])
                vcast = self.dma("POOL", f"wc{r}", wr[r][:, :], wtmp[r][:, :])
                self.qwait("DVE", vcast)
                vsub = self.op("DVE", "dve", lambda e, r=r: e.tensor_sub(
                    we[r][:, :], wtmp[r][:, :], wr[r][:, :].bitcast(F32)))
                self.wait("TE", "dve", vsub)
                if og > 0 and dc == 0:
                    self.wait("TE", "act", evac_v)
                for oc in range(4):
                    for t in range(2):
                        pb = self.P[oc * 2 + t]
                        mv_r = xr[:, dc * TS_ + t * 512: dc * TS_ + t * 512 + 512]
                        mv_e = xe[:, dc * TS_ + t * 512: dc * TS_ + t * 512 + 512]
                        stt = (dc == 0)
                        stp = (dc == DC - 1)
                        lr = wr[r][:, oc * 128:(oc + 1) * 128]
                        le = we[r][:, oc * 128:(oc + 1) * 128]
                        self.op("TE", "mm", lambda e, pb=pb, lr=lr, mv=mv_r, s=stt: e.matmul(
                            pb[:, :], lr, mv, start=s, stop=False), inc=False)
                        self.op("TE", "mm", lambda e, pb=pb, lr=lr, mv=mv_e: e.matmul(
                            pb[:, :], lr, mv, start=False, stop=False), inc=False)
                        last = oc == 3 and t == 1
                        self.op("TE", "mm", lambda e, pb=pb, le=le, mv=mv_r, p=stp: e.matmul(
                            pb[:, :], le, mv, start=False, stop=p), inc=last)
                mmrec[it] = self.cnt["mm"]
                it += 1
                if dc % 2 == 1 and pending:
                    pending.pop(0)()
            while pending:
                pending.pop(0)()
            v_grp = self.cnt["mm"]
            self.wait("ACT", "mm", v_grp)
            if st["last_cast"] is not None:
                self.qwait("ACT", st["last_cast"])
            if st["latest"] is not None:
                self.qwait("ACT", st["latest"])
            for j in range(8):
                self.op("ACT", "act", lambda e, pb=self.P[j], j=j, f=act: e.activation(
                    out=ebw[:, j * 512:(j + 1) * 512], in_=pb[:, :], func=f),
                    inc=(j == 7))
            evac_v = self.cnt["act"]
            pending = [mkstep(og, j, evac_v) for j in range(8)]
        while pending:
            pending.pop(0)()

    def proj_tokmaj(self, xr, xe, slab_off, v_ready):
        nm = "wv"
        w = self.i_w["wv"]
        wtmp = [self.sb(nm + "wt", [128, 512], F32, self.A + 2048 * i) for i in range(3)]
        wr = [self.sb(nm + "wr", [128, 512], F32R, self.A + 6144 + 2048 * i) for i in range(3)]
        we = [self.sb(nm + "we", [128, 512], F32R, self.A + 12288 + 2048 * i) for i in range(3)]
        rb = [self.sb(nm + "rb", [128, 512], F32R, self.B + 2048 * i) for i in range(2)]
        ebw = self.sb(nm + "ebw", [128, 8 * 512], F32, self.C + 2048)
        self.wait("TE", "dve", v_ready)
        it = 0
        mmrec = {}
        evac_v = 0
        st = {"prev_sub": None, "latest": None, "last_cast": None}
        pending = []

        def mkstep(dt, tb, va):
            def step():
                bsl = ebw[:, tb * 512:(tb + 1) * 512]
                rbb = rb[tb % 2]
                row0 = slab_off + tb * 128
                self.wait("POOL", "act", va)
                if st["prev_sub"] is not None:
                    self.wait("POOL", "dve", st["prev_sub"])
                if st["latest"] is not None:
                    self.qwait("POOL", st["latest"])
                vc = self.dma("POOL", "pc0", rbb[:, :], bsl)
                st["last_cast"] = vc
                self.qwait("DVE", vc)
                vs = self.op("DVE", "dve", lambda e, b=bsl, rbb=rbb: e.tensor_sub(
                    b, b, rbb[:, :].bitcast(F32)))
                st["prev_sub"] = vs
                self.wait("POOL", "dve", vs)
                self.dma("POOL", "ps0",
                         self.d_vr[row0:row0 + 128, dt * 512:(dt + 1) * 512], rbb[:, :])
                vo = self.dma("POOL", "ps0",
                              self.d_ve[row0:row0 + 128, dt * 512:(dt + 1) * 512],
                              bsl.bitcast(F32R))
                st["latest"] = vo
            return step

        for dt in range(4):
            for dc in range(DC):
                r = it % 3
                if it >= 3:
                    self.wait("SP", "mm", mmrec[it - 3])
                else:
                    self.wait("SP", "mm", self.cnt["mm"])
                vload = self.dma("SP", f"w{r}", wtmp[r][:, :],
                                 w[dc * 128:(dc + 1) * 128, dt * 512:(dt + 1) * 512])
                self.qwait("POOL", vload)
                if it >= 3:
                    self.wait("POOL", "mm", mmrec[it - 3])
                vcast = self.dma("POOL", f"wc{r}", wr[r][:, :], wtmp[r][:, :])
                self.qwait("DVE", vcast)
                vsub = self.op("DVE", "dve", lambda e, r=r: e.tensor_sub(
                    we[r][:, :], wtmp[r][:, :], wr[r][:, :].bitcast(F32)))
                self.wait("TE", "dve", vsub)
                if dt > 0 and dc == 0:
                    self.wait("TE", "act", evac_v)
                for tb in range(8):
                    pb = self.P[tb]
                    st_r = xr[:, dc * TS + tb * 128: dc * TS + tb * 128 + 128]
                    st_e = xe[:, dc * TS + tb * 128: dc * TS + tb * 128 + 128]
                    stt = (dc == 0)
                    stp = (dc == DC - 1)
                    last = tb == 7
                    self.op("TE", "mm", lambda e, pb=pb, l=st_r, m=wr[r], s=stt: e.matmul(
                        pb[:, :], l, m[:, :], start=s, stop=False), inc=False)
                    self.op("TE", "mm", lambda e, pb=pb, l=st_r, m=we[r]: e.matmul(
                        pb[:, :], l, m[:, :], start=False, stop=False), inc=False)
                    self.op("TE", "mm", lambda e, pb=pb, l=st_e, m=wr[r], p=stp: e.matmul(
                        pb[:, :], l, m[:, :], start=False, stop=p), inc=last)
                mmrec[it] = self.cnt["mm"]
                it += 1
                if dc % 2 == 1 and pending:
                    pending.pop(0)()
            while pending:
                pending.pop(0)()
            v_grp = self.cnt["mm"]
            self.wait("ACT", "mm", v_grp)
            if st["last_cast"] is not None:
                self.qwait("ACT", st["last_cast"])
            if st["latest"] is not None:
                self.qwait("ACT", st["latest"])
            for tb in range(8):
                self.op("ACT", "act", lambda e, pb=self.P[tb], tb=tb: e.activation(
                    out=ebw[:, tb * 512:(tb + 1) * 512], in_=pb[:, :], func=AF.Silu),
                    inc=(tb == 7))
            evac_v = self.cnt["act"]
            pending = [mkstep(dt, tb, evac_v) for tb in range(8)]
        while pending:
            pending.pop(0)()

    def ph_ffn(self):
        BIG = self.BIG
        xn = self.sb("fxn", [128, DC * TS], F32, BIG)
        xnr = self.sb("fxnr", [128, DC * TS], F32R, BIG + 65536)
        xne = self.sb("fxne", [128, DC * TS], F32R, BIG)

        src = self.fm(self.d_xn[:, :])
        vx = self.dma("POOL", "xnst", self.fs(xn[:, :], TS), src)
        self.qwait("POOL", vx)
        cast_v = []
        for dc in range(DC):
            vc = self.dma("POOL", "fc", xnr[:, dc * TS:(dc + 1) * TS],
                          xn[:, dc * TS:(dc + 1) * TS])
            cast_v.append(vc)
        self.qwait("DVE", cast_v[-1])
        for dc in range(DC):
            self.op("DVE", "dve", lambda e, dc=dc: e.tensor_sub(
                xne[:, dc * TS:(dc + 1) * TS], xn[:, dc * TS:(dc + 1) * TS],
                xnr[:, dc * TS:(dc + 1) * TS].bitcast(F32)))
        v_split = self.cnt["dve"]

        w1, w3 = self.i_w["w1"], self.i_w["w3"]
        wt1 = [self.sb("f1t", [128, 512], F32, self.A + 2048 * i) for i in range(3)]
        wr1 = [self.sb("f1r", [128, 512], F32R, self.A + 6144 + 2048 * i) for i in range(3)]
        we1 = [self.sb("f1e", [128, 512], F32R, self.A + 12288 + 2048 * i) for i in range(3)]
        ug = self.sb("fug", [128, 8 * 512], F32, self.C)
        rb = [self.sb("fhr", [128, 512], F32R, self.C + 16384 + 2048 * i) for i in range(2)]
        self.wait("TE", "dve", v_split)
        it = 0
        mmrec = {}
        ug_evac_v = 0
        mul_v = 0
        st = {"prev_sub": None, "latest": None, "last_cast": None}
        pending = []

        def mkstep(hg, j, vmul):
            def step():
                hc, t = divmod(j, 2)
                bsl = ug[:, j * 512:(j + 1) * 512]
                rbb = rb[j % 2]
                row0 = (hg * 4 + hc) * 128
                if st["prev_sub"] is not None:
                    self.wait("POOL", "dve", st["prev_sub"])
                self.wait("POOL", "dve", vmul)
                if st["latest"] is not None:
                    self.qwait("POOL", st["latest"])
                vc = self.dma("POOL", "pc0", rbb[:, :], bsl)
                st["last_cast"] = vc
                self.qwait("DVE", vc)
                vs = self.op("DVE", "dve", lambda e, b=bsl, rbb=rbb: e.tensor_sub(
                    b, b, rbb[:, :].bitcast(F32)))
                st["prev_sub"] = vs
                self.wait("POOL", "dve", vs)
                self.dma("POOL", "ps0",
                         self.d_hr[row0:row0 + 128, t * 512:(t + 1) * 512], rbb[:, :])
                vo = self.dma("POOL", "ps0",
                              self.d_he[row0:row0 + 128, t * 512:(t + 1) * 512],
                              bsl.bitcast(F32R))
                st["latest"] = vo
            return step

        for hg in range(16):
            for wi, w in ((0, w1), (1, w3)):
                for dc in range(DC):
                    r = it % 3
                    if it >= 3:
                        self.wait("SP", "mm", mmrec[it - 3])
                    else:
                        self.wait("SP", "mm", self.cnt["mm"])
                    vload = self.dma("SP", f"w{r}", wt1[r][:, :],
                                     w[dc * 128:(dc + 1) * 128, hg * 512:(hg + 1) * 512])
                    self.qwait("POOL", vload)
                    if it >= 3:
                        self.wait("POOL", "mm", mmrec[it - 3])
                    vcast = self.dma("POOL", f"wc{r}", wr1[r][:, :], wt1[r][:, :])
                    self.qwait("DVE", vcast)
                    vsub = self.op("DVE", "dve", lambda e, r=r: e.tensor_sub(
                        we1[r][:, :], wt1[r][:, :], wr1[r][:, :].bitcast(F32)))
                    self.wait("TE", "dve", vsub)
                    if dc == 0 and (hg > 0 or wi > 0):
                        if wi == 0:
                            self.wait("TE", "dve", mul_v)
                        else:
                            self.wait("TE", "act", ug_evac_v)
                    for hc in range(4):
                        for t in range(2):
                            pb = self.P[hc * 2 + t]
                            mv_r = xnr[:, dc * TS + t * 512: dc * TS + t * 512 + 512]
                            mv_e = xne[:, dc * TS + t * 512: dc * TS + t * 512 + 512]
                            lr = wr1[r][:, hc * 128:(hc + 1) * 128]
                            le = we1[r][:, hc * 128:(hc + 1) * 128]
                            stt = (dc == 0)
                            stp = (dc == DC - 1)
                            last = hc == 3 and t == 1
                            self.op("TE", "mm", lambda e, pb=pb, l=lr, m=mv_r, s=stt: e.matmul(
                                pb[:, :], l, m, start=s, stop=False), inc=False)
                            self.op("TE", "mm", lambda e, pb=pb, l=lr, m=mv_e: e.matmul(
                                pb[:, :], l, m, start=False, stop=False), inc=False)
                            self.op("TE", "mm", lambda e, pb=pb, l=le, m=mv_r, p=stp: e.matmul(
                                pb[:, :], l, m, start=False, stop=p), inc=last)
                    mmrec[it] = self.cnt["mm"]
                    it += 1
                    if dc % 2 == 1 and pending:
                        pending.pop(0)()
                v_grp = self.cnt["mm"]
                if wi == 0:
                    while pending:
                        pending.pop(0)()
                    self.wait("ACT", "mm", v_grp)
                    if st["last_cast"] is not None:
                        self.qwait("ACT", st["last_cast"])
                    if st["latest"] is not None:
                        self.qwait("ACT", st["latest"])
                    for j in range(8):
                        self.op("ACT", "act", lambda e, pb=self.P[j], j=j: e.activation(
                            out=ug[:, j * 512:(j + 1) * 512], in_=pb[:, :], func=AF.Silu),
                            inc=(j == 7))
                    ug_evac_v = self.cnt["act"]
                else:
                    self.wait("DVE", "mm", v_grp)
                    self.wait("DVE", "act", ug_evac_v)
                    vmuls = []
                    for j in range(8):
                        self.wait("DVE", "dve", self.cnt["dve"])
                        vm = self.op("DVE", "dve", lambda e, pb=self.P[j], j=j: e.tensor_tensor(
                            ug[:, j * 512:(j + 1) * 512], ug[:, j * 512:(j + 1) * 512],
                            pb[:, :], ALU.mult), inc=True)
                        vmuls.append(vm)
                    mul_v = self.cnt["dve"]
                    pending = [mkstep(hg, j, vmuls[j]) for j in range(8)]
        while pending:
            pending.pop(0)()

    def ph_fwht(self):
        TT = 256
        for tt in range(4):
            self._fwht_tile(tt, TT)

    def _fwht_tile(self, tt, TT):
        BIG = self.BIG
        a = self.sb("wa", [128, HC * TT], F32, BIG)
        bq = self.sb("wb", [128, HC * TT], F32, BIG + 65536)
        hcr = [self.sb("whr", [128, TT], F32R, self.A + 1024 * i) for i in range(4)]
        hce = [self.sb("whe", [128, TT], F32R, self.A + 4096 + 1024 * i) for i in range(4)]
        rbnc = self.sb("wrb", [128, 8 * TT], F32R, self.C)
        ebnc = self.sb("web", [128, 8 * TT], F32, self.C + 8192)

        cols = (tt * TT, (tt + 1) * TT)
        self.qwait("ACT", self.qtotal("ys"))
        self.qwait("ACT", self.qtotal("dbg"))
        self.wait("TE", "act", self.cnt["act"])
        ev = {}
        ld = {}
        for c in range(HC):
            r = c % 4
            if c >= 4:
                self.wait("SP", "mm", ld[c - 4])
            else:
                self.wait("SP", "mm", self.cnt["mm"])
            rows = slice(c * 128, (c + 1) * 128)
            v1 = self.dma("SP", f"f{r}", hcr[r][:, :],
                          self.d_hr[rows, cols[0]:cols[1]])
            v2 = self.dma("SP", f"f{r}", hce[r][:, :],
                          self.d_he[rows, cols[0]:cols[1]])
            pb = self.P[c % 8]
            if c >= 8:
                self.wait("TE", "act", ev[c - 8])
            self.qwait("TE", v2)
            self.op("TE", "mm", lambda e, pb=pb, r=r: e.matmul(
                pb[:, 0:TT], self.h128r[:, :], hcr[r][:, :],
                start=True, stop=False), inc=False)
            v = self.op("TE", "mm", lambda e, pb=pb, r=r: e.matmul(
                pb[:, 0:TT], self.h128r[:, :], hce[r][:, :],
                start=False, stop=True))
            ld[c] = self.cnt["mm"]
            self.wait("ACT", "mm", v)
            ev[c] = self.op("ACT", "act", lambda e, pb=pb, c=c: e.activation(
                out=a[:, c * TT:(c + 1) * TT], in_=pb[:, 0:TT], func=AF.Copy,
                scale=FWHT_SCALE))
        v_y = self.cnt["act"]
        if self.st["debug"] and tt == 0:
            self.wait("POOL", "act", v_y)
            vdump = self.dma("POOL", "dbg", self.d_y1dbg[:, :], a[:, :])
            self.qwait("POOL", vdump)
            self.qwait("DVE", vdump)
        self.wait("DVE", "act", v_y)
        self.wait("POOL", "act", v_y)
        src, dst = a, bq
        for lev in range(6):
            st = 1 << lev
            npair = HC // (2 * st)
            def mk(sgn, half, src=src, dst=dst, st=st, npair=npair):
                def f(e):
                    o = dst[:, :].rearrange("p (g two s) -> p g two s", two=2,
                                            s=st * TT)
                    i0 = src[:, :].rearrange("p (g two s) -> p g two s", two=2,
                                             s=st * TT)
                    out = o[:, :, half, :]
                    return e.tensor_tensor(
                        out, i0[:, :, 0, :], i0[:, :, 1, :],
                        ALU.add if sgn > 0 else ALU.subtract)
                return f
            self.wait("DVE", "dve", self.cnt["dve"])
            self.wait("POOL", "pool", self.cnt["pool"])
            v0 = self.op("DVE", "dve", mk(+1, 0))
            v1 = self.op("POOL", "pool", mk(-1, 1))
            self.wait("DVE", "pool", v1)
            self.wait("POOL", "dve", v0)
            src, dst = dst, src
        v_res_dve = self.cnt["dve"]
        v_res_pool = self.cnt["pool"]
        self.wait("POOL", "dve", v_res_dve)
        self.wait("POOL", "pool", v_res_pool)
        if self.st["debug"] and tt == 0:
            vd = self.dma("POOL", "dbg", self.d_ybf[:, :], src[:, :])
            self.qwait("POOL", vd)
        prev = None
        for g in range(8):
            gs = slice(g * 8 * TT, (g + 1) * 8 * TT)
            if prev is not None:
                self.qwait("POOL", prev)
            vc = self.dma("POOL", "ys", rbnc[:, :], src[:, gs])
            self.qwait("DVE", vc)
            self.wait("DVE", "pool", v_res_pool)
            if prev is not None:
                self.qwait("DVE", prev)
            vs = self.op("DVE", "dve", lambda e, gs=gs: e.tensor_sub(
                ebnc[:, :], src[:, gs], rbnc[:, :].bitcast(F32)))
            self.wait("POOL", "dve", vs)
            rows = slice(g * 1024, (g + 1) * 1024)
            dst_r = self.fm(self.d_yr[rows, cols[0]:cols[1]])
            dst_e = self.fm(self.d_ye[rows, cols[0]:cols[1]])
            self.dma("POOL", "ys", dst_r, self.fs(rbnc[:, :], TT))
            vo = self.dma("POOL", "ys", dst_e, self.fs(ebnc[:, :].bitcast(F32R), TT))
            prev = vo

    def ph_w2(self):
        w2 = self.i_w["w2"]
        wt = [self.sb("w2t", [128, 1024], F32, self.C + 4096 * i) for i in range(2)]
        wr = [self.sb("w2r", [128, 1024], F32R, self.C + 8192 + 4096 * i) for i in range(2)]
        we = [self.sb("w2e", [128, 1024], F32R, self.C + 16384 + 4096 * i) for i in range(2)]
        yr = [self.sb("w2yr", [128, 512], F32R, self.A + 2048 * i) for i in range(2)]
        ye = [self.sb("w2ye", [128, 512], F32R, self.A + 4096 + 2048 * i) for i in range(2)]
        g2w = self.sb("w2g", [128, 8 * 512], F32, self.BIG)
        mmrec = {}
        it = 0
        evac_v = 0
        st = {"latest": None}
        pending = []

        def mkstep(t, og, oc, va):
            def step():
                row0 = (og * 8 + oc) * 128
                self.wait("POOL", "act", va)
                vo = self.dma("POOL", "gs0",
                              self.d_g2[row0:row0 + 128, t * 512:(t + 1) * 512],
                              g2w[:, oc * 512:(oc + 1) * 512])
                st["latest"] = vo
            return step

        for t in range(2):
            for og in range(2):
                for hc in range(HC):
                    r = it % 2
                    if it >= 2:
                        self.wait("SP", "mm", mmrec[it - 2])
                    else:
                        self.wait("SP", "mm", self.cnt["mm"])
                    vl = self.dma("SP", f"w{r}", wt[r][:, :],
                                  w2[hc * 128:(hc + 1) * 128, og * 1024:(og + 1) * 1024])
                    vy1 = self.dma("SP", f"w{r}", yr[r][:, :],
                                   self.d_yr[hc * 128:(hc + 1) * 128, t * 512:(t + 1) * 512])
                    vy2 = self.dma("SP", f"w{r}", ye[r][:, :],
                                   self.d_ye[hc * 128:(hc + 1) * 128, t * 512:(t + 1) * 512])
                    self.qwait("POOL", vy2)
                    if it >= 2:
                        self.wait("POOL", "mm", mmrec[it - 2])
                    vcast = self.dma("POOL", f"wc{r}", wr[r][:, :], wt[r][:, :])
                    self.qwait("DVE", vcast)
                    vsub = self.op("DVE", "dve", lambda e, r=r: e.tensor_sub(
                        we[r][:, :], wt[r][:, :], wr[r][:, :].bitcast(F32)))
                    self.wait("TE", "dve", vsub)
                    self.qwait("TE", vy2)
                    if hc == 0 and it > 0:
                        self.wait("TE", "act", evac_v)
                    for oc in range(8):
                        pb = self.P[oc]
                        lr = wr[r][:, oc * 128:(oc + 1) * 128]
                        le = we[r][:, oc * 128:(oc + 1) * 128]
                        stt = (hc == 0)
                        stp = (hc == HC - 1)
                        last = oc == 7
                        self.op("TE", "mm", lambda e, pb=pb, l=lr, m=yr[r], s=stt: e.matmul(
                            pb[:, :], l, m[:, :], start=s, stop=False), inc=False)
                        self.op("TE", "mm", lambda e, pb=pb, l=lr, m=ye[r]: e.matmul(
                            pb[:, :], l, m[:, :], start=False, stop=False), inc=False)
                        self.op("TE", "mm", lambda e, pb=pb, l=le, m=yr[r], p=stp: e.matmul(
                            pb[:, :], l, m[:, :], start=False, stop=p), inc=last)
                    mmrec[it] = self.cnt["mm"]
                    it += 1
                    if it % 8 == 0 and pending:
                        pending.pop(0)()
                while pending:
                    pending.pop(0)()
                v_grp = self.cnt["mm"]
                self.wait("ACT", "mm", v_grp)
                if st["latest"] is not None:
                    self.qwait("ACT", st["latest"])
                for oc in range(8):
                    self.op("ACT", "act", lambda e, pb=self.P[oc], oc=oc: e.activation(
                        out=g2w[:, oc * 512:(oc + 1) * 512], in_=pb[:, :], func=AF.Silu),
                        inc=(oc == 7))
                evac_v = self.cnt["act"]
                pending = [mkstep(t, og, oc, evac_v) for oc in range(8)]
        while pending:
            pending.pop(0)()

    def ph_attn(self, qi):
        BIG = self.BIG
        nkb = NKB[qi]
        qr = self.sb("aqr", [128, DC * 512], F32R, BIG)
        qe = self.sb("aqe", [128, DC * 512], F32R, BIG + 32768)
        stor = self.sb("astr", [128, 16 * 512], F32R, BIG + 65536)
        stoe = self.sb("aste", [128, 16 * 512], F32R, BIG + 98304)
        kbr = [self.sb("akbr", [128, DC * 128], F32R, self.A + 8192 * i) for i in range(2)]
        kbe = [self.sb("akbe", [128, DC * 128], F32R, self.C + 8192 * i) for i in range(2)]
        mbuf = [self.sb("amsk", [128, 512], F32, self.M + 1600 + 2048 * i) for i in range(2)]
        sbnc = [self.sb("asb", [128, 512], F32, self.B + 2048 * i) for i in range(2)]
        srb = [self.sb("asr", [128, 512], F32R, self.B + 4096 + 2048 * i) for i in range(2)]

        cq = qi * 512
        v1 = self.dma("SP", "qq", self.fs(qr[:, :], 512),
                      self.fm(self.d_qr[:, cq:cq + 512]))
        v2 = self.dma("SP", "qq", self.fs(qe[:, :], 512),
                      self.fm(self.d_qe[:, cq:cq + 512]))
        self.qwait("TE", v2)
        kload = {}
        mload = {}
        st_done = {}
        for j, cb in enumerate(KBLIST[qi]):
            r = j % 2
            if j >= 2:
                self.wait("SP", "mm", st_done[j - 2][0])
            else:
                self.wait("SP", "mm", self.cnt["mm"])
            lk1 = self.dma("SP", f"kb{r}", self.fs(kbr[r][:, :], 128),
                           self.fm(self.d_kr[:, cb * 128:(cb + 1) * 128]))
            lk2 = self.dma("SP", f"kb{r}", self.fs(kbe[r][:, :], 128),
                           self.fm(self.d_ke[:, cb * 128:(cb + 1) * 128]))
            mi = (8 if qi else 0) + j if qi == 0 else 8 + j
            mi = j if qi == 0 else 8 + j
            if j >= 2:
                self.wait("ACT", "dve", st_done[j - 2][1])
            else:
                self.wait("ACT", "mm", self.cnt["mm"])
            lm = self.dma("ACT", f"mk{r}", mbuf[r][:, :], self.i_masks[mi, :, :])
            self.qwait("TE", lk2)
            pb = self.P[j % 2]
            if j >= 2:
                self.wait("TE", "dve", st_done[j - 2][1])
            for dc in range(DC):
                lr = kbr[r][:, dc * 128:(dc + 1) * 128]
                le = kbe[r][:, dc * 128:(dc + 1) * 128]
                mr = qr[:, dc * 512:(dc + 1) * 512]
                me = qe[:, dc * 512:(dc + 1) * 512]
                stt = (dc == 0)
                stp = (dc == DC - 1)
                self.op("TE", "mm", lambda e, pb=pb, l=lr, m=mr, s=stt: e.matmul(
                    pb[:, :], l, m, start=s, stop=False), inc=False)
                self.op("TE", "mm", lambda e, pb=pb, l=lr, m=me: e.matmul(
                    pb[:, :], l, m, start=False, stop=False), inc=False)
                self.op("TE", "mm", lambda e, pb=pb, l=le, m=mr, p=stp: e.matmul(
                    pb[:, :], l, m, start=False, stop=p), inc=stp)
            v_s = self.cnt["mm"]
            self.wait("DVE", "mm", v_s)
            self.qwait("DVE", lm)
            b = sbnc[r]
            rbb = srb[r]
            if j >= 2:
                self.qwait("DVE", st_done[j - 2][2])
            vm = self.op("DVE", "dve", lambda e, pb=pb, b=b, m=mbuf[r]: e.tensor_tensor(
                b[:, :], pb[:, :], m[:, :], ALU.mult), inc=True)
            self.wait("POOL", "dve", vm)
            vc = self.dma("POOL", "ogc", stor[:, j * 512:(j + 1) * 512], b[:, :])
            self.qwait("DVE", vc)
            vs = self.op("DVE", "dve", lambda e, b=b, j=j: e.tensor_sub(
                stoe[:, j * 512:(j + 1) * 512], b[:, :],
                stor[:, j * 512:(j + 1) * 512].bitcast(F32)))
            st_done[j] = (v_s, vs, vc)
        v_scores = self.cnt["dve"]

        ogr = self.sb("aogr", [128, DC * 512], F32R, BIG)
        oge = self.sb("aoge", [128, DC * 512], F32R, BIG + 32768)
        vb = [self.sb("avb", [128, 512], F32R, self.A + 2048 * i) for i in range(4)]
        gb = [self.sb("agb", [128, 512], F32, self.A + 8192 + 2048 * i) for i in range(2)]
        ob = [self.sb("aob", [128, 512], F32, self.B + 2048 * i) for i in range(2)]
        orb = [self.sb("aor", [128, 512], F32R, self.B + 4096 + 2048 * i) for i in range(2)]
        self.wait("SP", "mm", self.cnt["mm"])
        self.wait("ACT", "mm", self.cnt["mm"])
        self.wait("TE", "dve", v_scores)
        vload = {}
        og_ev = 0
        o_ev = [None, None]
        guse = [0, 0]
        for og in range(4):
            for j, cb in enumerate(KBLIST[qi]):
                r = j % 2
                row0 = cb * 128
                if (og * nkb + j) >= 2:
                    self.wait("SP", "mm", vload[og * nkb + j - 2])
                else:
                    self.wait("SP", "mm", self.cnt["mm"])
                l1 = self.dma("SP", f"vb{r}", vb[r * 2][:, :],
                              self.d_vr[row0:row0 + 128, og * 512:(og + 1) * 512])
                l2 = self.dma("SP", f"vb{r}", vb[r * 2 + 1][:, :],
                              self.d_ve[row0:row0 + 128, og * 512:(og + 1) * 512])
                self.qwait("TE", l2)
                if j == 0 and og > 0:
                    self.wait("TE", "dve", og_ev)
                for oc in range(4):
                    pb = self.P[4 * (og % 2) + oc]
                    lr = vb[r * 2][:, oc * 128:(oc + 1) * 128]
                    le = vb[r * 2 + 1][:, oc * 128:(oc + 1) * 128]
                    mr = stor[:, j * 512:(j + 1) * 512]
                    me = stoe[:, j * 512:(j + 1) * 512]
                    stt = (j == 0)
                    stp = (j == nkb - 1)
                    last = oc == 3
                    self.op("TE", "mm", lambda e, pb=pb, l=lr, m=mr, s=stt: e.matmul(
                        pb[:, :], l, m, start=s, stop=False), inc=False)
                    self.op("TE", "mm", lambda e, pb=pb, l=le, m=mr: e.matmul(
                        pb[:, :], l, m, start=False, stop=False), inc=False)
                    self.op("TE", "mm", lambda e, pb=pb, l=lr, m=me, p=stp: e.matmul(
                        pb[:, :], l, m, start=False, stop=p), inc=last)
                vload[og * nkb + j] = self.cnt["mm"]
            v_og = self.cnt["mm"]
            self.wait("DVE", "mm", v_og)
            for oc in range(4):
                ch = og * 4 + oc
                b = ob[oc % 2]
                rbb = orb[oc % 2]
                row0 = ch * 128
                if ch >= 2:
                    self.wait("ACT", "dve", guse[oc % 2])
                else:
                    self.wait("ACT", "mm", self.cnt["mm"])
                lg = self.dma("ACT", f"gl{oc % 2}", gb[oc % 2][:, :],
                              self.d_g[row0:row0 + 128, cq:cq + 512])
                self.qwait("DVE", lg)
                if ch >= 2:
                    self.qwait("DVE", o_ev[oc % 2])
                vm = self.op("DVE", "dve", lambda e, pb=self.P[4 * (og % 2) + oc], b=b,
                        g=gb[oc % 2]: e.tensor_tensor(b[:, :], pb[:, :], g[:, :],
                                                      ALU.mult), inc=True)
                guse[oc % 2] = vm
                self.wait("POOL", "dve", vm)
                vc = self.dma("POOL", "ogc", ogr[:, ch * 512:(ch + 1) * 512], b[:, :])
                o_ev[oc % 2] = vc
                self.qwait("DVE", vc)
                self.op("DVE", "dve", lambda e, b=b, ch=ch: e.tensor_sub(
                    oge[:, ch * 512:(ch + 1) * 512], b[:, :],
                    ogr[:, ch * 512:(ch + 1) * 512].bitcast(F32)))
            og_ev = self.cnt["dve"]
        v_sv = self.cnt["dve"]

        self.wait("SP", "mm", self.cnt["mm"])
        wp = self.i_w["wp"]
        wt = [self.sb("apt", [128, 1024], F32, self.C + 4096 * i) for i in range(2)]
        wr = [self.sb("apr", [128, 1024], F32R, self.C + 8192 + 4096 * i) for i in range(2)]
        we = [self.sb("ape", [128, 1024], F32R, self.C + 16384 + 4096 * i) for i in range(2)]
        xg = self.sb("axg", [128, DC * 512], F32, BIG + 65536)
        g2t = self.sb("ag2", [128, DC * 512], F32, BIG + 98304)
        obo = [self.sb("aout", [128, 512], F32, self.B + 2048 * i) for i in range(4)]
        self.wait("ACT", "mm", self.cnt["mm"])
        vxg = self.dma("ACT", "xg", self.fs(xg[:, :], 512),
                       self.fm(self.i_x[:, cq:cq + 512]))
        vg2 = self.dma("ACT", "xg", self.fs(g2t[:, :], 512),
                       self.fm(self.d_g2[:, cq:cq + 512]))
        self.wait("TE", "dve", v_sv)
        mmrec = {}
        it = 0
        evac_v = 0
        out_dma = [None, None, None, None]
        for og2 in range(2):
            for dc in range(DC):
                r = it % 2
                if it >= 2:
                    self.wait("SP", "mm", mmrec[it - 2])
                else:
                    self.wait("SP", "mm", self.cnt["mm"])
                vl = self.dma("SP", f"w{r}", wt[r][:, :],
                              wp[dc * 128:(dc + 1) * 128, og2 * 1024:(og2 + 1) * 1024])
                self.qwait("POOL", vl)
                if it >= 2:
                    self.wait("POOL", "mm", mmrec[it - 2])
                vcast = self.dma("POOL", f"wc{r}", wr[r][:, :], wt[r][:, :])
                self.qwait("DVE", vcast)
                vsub = self.op("DVE", "dve", lambda e, r=r: e.tensor_sub(
                    we[r][:, :], wt[r][:, :], wr[r][:, :].bitcast(F32)))
                self.wait("TE", "dve", vsub)
                if dc == 0 and og2 > 0:
                    self.wait("TE", "dve", evac_v)
                for oc in range(8):
                    pb = self.P[oc]
                    lr = wr[r][:, oc * 128:(oc + 1) * 128]
                    le = we[r][:, oc * 128:(oc + 1) * 128]
                    mr = ogr[:, dc * 512:(dc + 1) * 512]
                    me = oge[:, dc * 512:(dc + 1) * 512]
                    stt = (dc == 0)
                    stp = (dc == DC - 1)
                    last = oc == 7
                    self.op("TE", "mm", lambda e, pb=pb, l=lr, m=mr, s=stt: e.matmul(
                        pb[:, :], l, m, start=s, stop=False), inc=False)
                    self.op("TE", "mm", lambda e, pb=pb, l=lr, m=me: e.matmul(
                        pb[:, :], l, m, start=False, stop=False), inc=False)
                    self.op("TE", "mm", lambda e, pb=pb, l=le, m=mr, p=stp: e.matmul(
                        pb[:, :], l, m, start=False, stop=p), inc=last)
                mmrec[it] = self.cnt["mm"]
                it += 1
            v_grp = self.cnt["mm"]
            self.wait("DVE", "mm", v_grp)
            self.qwait("DVE", vg2)
            for oc in range(8):
                ch = og2 * 8 + oc
                b = obo[oc % 4]
                if ch >= 4:
                    self.qwait("DVE", out_dma[oc % 4])
                vmul = self.op("DVE", "dve", lambda e, pb=self.P[oc], b=b, ch=ch: e.tensor_tensor(
                    b[:, :], pb[:, :], g2t[:, ch * 512:(ch + 1) * 512], ALU.mult),
                    inc=True)
                self.wait("DVE", "dve", vmul)
                vm = self.op("DVE", "dve", lambda e, b=b, ch=ch: e.tensor_tensor(
                    b[:, :], b[:, :], xg[:, ch * 512:(ch + 1) * 512], ALU.add))
                self.wait("ACT", "dve", vm)
                row0 = ch * 128
                vo = self.dma("ACT", f"o{oc % 4}",
                              self.o_out[row0:row0 + 128, cq:cq + 512], b[:, :])
                out_dma[oc % 4] = vo
            evac_v = self.cnt["dve"]

    def build(self):
        self.ph0()
        if self.st["p1"]:
            self.fence()
            self.ph1_slab(1)
            self.fence()
            self.ph1_slab(0)
        if self.st["ffn"]:
            self.fence()
            self.ph_ffn()
        if self.st["fwht"]:
            self.fence()
            self.ph_fwht()
        if self.st["w2"]:
            self.fence()
            self.ph_w2()
        if self.st["attn"]:
            self.fence()
            self.ph_attn(0)
            self.fence()
            self.ph_attn(1)
        self.fence()
        for s in ("o0", "o1", "o2", "o3"):
            self.qwait("ACT", self.qtotal(s))
            self.qwait("POOL", self.qtotal(s))
        with self.nc.Block() as block:
            @block.tensor
            def _(e):
                for f in self.ops["TE"]:
                    f(e)

            @block.scalar
            def _(e):
                for f in self.ops["ACT"]:
                    f(e)

            @block.vector
            def _(e):
                for f in self.ops["DVE"]:
                    f(e)

            @block.gpsimd
            def _(e):
                for f in self.ops["POOL"]:
                    f(e)

            @block.sync
            def _(e):
                for f in self.ops["SP"]:
                    f(e)
        self.ctx.close()
        return self.nc


def _hadamard(n):
    h = np.array([[1.0]], dtype=np.float32)
    while h.shape[0] < n:
        h = np.block([[h, h], [h, -h]])
    return h.astype(np.float32)


def _staging(core, x):
    b, h = core // 2, core % 2
    own = np.concatenate([np.arange((2 * m + h) * 128, (2 * m + h) * 128 + 128)
                          for m in range(8)])
    par = np.concatenate([np.arange((2 * m + 1 - h) * 128, (2 * m + 1 - h) * 128 + 128)
                          for m in range(8)])
    perm = np.concatenate([own, par])
    xT = np.ascontiguousarray(x[b][perm, :].T)
    gidx = perm
    masks = np.zeros((24, 128, 512), dtype=np.float32)
    row = 0
    for qi in (0, 1):
        gq = gidx[qi * 512: qi * 512 + 512]
        for cb in KBLIST[qi]:
            gk = gidx[cb * 128:(cb + 1) * 128]
            masks[row] = (gk[:, None] <= gq[None, :]).astype(np.float32)
            row += 1
    return xT, masks, own


def kernel(L0_attn_x, L0_attn_Wq, L0_attn_Wk, L0_attn_Wv, L0_attn_Wg, L0_attn_Wp,
           L0_ffn_W1, L0_ffn_W2, L0_ffn_W3):
    x = np.ascontiguousarray(np.asarray(L0_attn_x, dtype=np.float32))
    ws = {
        "wqT": np.ascontiguousarray(np.asarray(L0_attn_Wq).T),
        "wkT": np.ascontiguousarray(np.asarray(L0_attn_Wk).T),
        "wvT": np.ascontiguousarray(np.asarray(L0_attn_Wv).T),
        "wgT": np.ascontiguousarray(np.asarray(L0_attn_Wg).T),
        "wpT": np.ascontiguousarray(np.asarray(L0_attn_Wp).T),
        "w1T": np.ascontiguousarray(np.asarray(L0_ffn_W1).T),
        "w3T": np.ascontiguousarray(np.asarray(L0_ffn_W3).T),
        "w2T": np.ascontiguousarray(np.asarray(L0_ffn_W2).T),
    }
    had = _hadamard(128)
    in_maps = []
    owns = []
    for c in range(8):
        xT, masks, own = _staging(c, x)
        owns.append(own)
        m = {"xT": xT, "masks": masks, "had128": had}
        m.update(ws)
        in_maps.append(m)

    nc = Builder().build()
    res = run_bass_kernel_spmd(nc, in_maps, core_ids=list(range(8)))
    out = np.empty((4, 2048, 2048), dtype=np.float32)
    for c in range(8):
        b = c // 2
        out[b][owns[c], :] = res.results[c]["outT"].T
    return out


# version 06
# speedup vs baseline: 1.2976x; 1.0091x over previous
from contextlib import ExitStack

import numpy as np
import concourse.bass as bass
import concourse.mybir as mybir
from concourse.bass_utils import run_bass_kernel_spmd

F32 = mybir.dt.float32
F32R = mybir.dt.float32r
AF = mybir.ActivationFunctionType
ALU = mybir.AluOpType

D = 2048
DC = 16
TS = 1024
HIDDEN = 8192
HC = 64
NKB = {0: 8, 1: 16}
KBLIST = {0: [0, 1, 2, 3, 8, 9, 10, 11], 1: list(range(16))}
LN_EPS = 1e-5
RMS_EPS = float(np.finfo(np.float32).eps)
FWHT_SCALE = float(1.0 / np.sqrt(HIDDEN))

ENGINES = ("TE", "ACT", "DVE", "POOL", "SP")

STAGES = {"p1": True, "ffn": True, "fwht": True, "w2": True, "attn": True,
          "debug": False}


def ceil32(x):
    return (x + 31) & ~31


class Builder:
    def __init__(self, stages=STAGES):
        self.st = dict(stages)
        nc = bass.Bass("TRN2", target_bir_lowering=False)
        self.nc = nc
        self.ops = {e: [] for e in ENGINES}
        self.cnt = {}
        self.sem = {}
        self.ctx = ExitStack()
        for name in ("mm", "act", "dve", "pool", "sp"):
            self.sem[name] = self.ctx.enter_context(nc.semaphore("s_" + name))
            self.cnt[name] = 0
        self.qsem, self.qcnt, self.qeng = {}, {}, {}
        self._declare_dram()
        self._declare_sbuf()
        self._declare_psum()

    def inc(self, name, n=1):
        self.cnt[name] += n
        return self.cnt[name]

    def emit(self, eng, fn):
        self.ops[eng].append(fn)

    def wait(self, eng, sem, val):
        if val <= 0:
            return
        s = self.sem[sem]
        self.emit(eng, lambda e, s=s, v=val: e.wait_ge(s, v))

    def dma(self, eng, stream, out, in_):
        if stream not in self.qsem:
            self.qsem[stream] = self.ctx.enter_context(self.nc.semaphore("q_" + stream))
            self.qcnt[stream] = 0
            self.qeng[stream] = eng
        assert self.qeng[stream] == eng, (stream, eng)
        self.qcnt[stream] += 16
        v = self.qcnt[stream]
        s = self.qsem[stream]
        self.emit(eng, lambda e, o=out, i=in_, s=s: e.dma_start(out=o, in_=i).then_inc(s, 16))
        return (stream, v)

    def qwait(self, eng, tok):
        if tok is None:
            return
        stream, v = tok
        if v <= 0:
            return
        s = self.qsem[stream]
        self.emit(eng, lambda e, s=s, v=v: e.wait_ge(s, v))

    def qtotal(self, stream):
        return (stream, self.qcnt.get(stream, 0))

    @staticmethod
    def fm(ap2d):
        return ap2d.rearrange("(c p) t -> p c t", p=128)

    @staticmethod
    def fs(sb_ap, t):
        return sb_ap.rearrange("p (c t) -> p c t", t=t)

    def op(self, eng, esem, fn, inc=True):
        if inc:
            v = self.inc(esem)
            s = self.sem[esem]
            self.emit(eng, lambda e, f=fn, s=s: f(e).then_inc(s, 1))
            return v
        self.emit(eng, lambda e, f=fn: f(e))
        return self.cnt[esem]

    def fence(self, tag=""):
        vals = {}
        for e in ENGINES:
            for stream, owner in self.qeng.items():
                if owner == e:
                    self.qwait(e, self.qtotal(stream))
            es = {"TE": "mm", "ACT": "act", "DVE": "dve", "POOL": "pool", "SP": "sp"}[e]
            self.wait(e, es, self.cnt[es])
            v = self.inc(es)
            s = self.sem[es]
            self.emit(e, lambda en, s=s: en.nop().then_inc(s, 1))
            vals[e] = (es, v)
        for e in ENGINES:
            for o in ENGINES:
                if o != e:
                    sname, v = vals[o]
                    self.wait(e, sname, v)

    def _declare_dram(self):
        nc = self.nc
        t = nc.dram_tensor
        self.i_x = t("xT", [D, 2 * TS], F32, kind="ExternalInput").ap()
        self.i_w = {}
        for nm in ("wq", "wk", "wv", "wg", "wp"):
            self.i_w[nm] = t(nm + "T", [D, D], F32, kind="ExternalInput").ap()
        self.i_w["w1"] = t("w1T", [D, HIDDEN], F32, kind="ExternalInput").ap()
        self.i_w["w3"] = t("w3T", [D, HIDDEN], F32, kind="ExternalInput").ap()
        self.i_w["w2"] = t("w2T", [HIDDEN, D], F32, kind="ExternalInput").ap()
        self.i_masks = t("masks", [24, 128, 512], F32, kind="ExternalInput").ap()
        self.i_h128 = t("had128", [128, 128], F32, kind="ExternalInput").ap()

        kind = "ExternalOutput" if self.st["debug"] else "Internal"
        if self.st["debug"]:
            self.d_x2r = t("x2r_d", [D, TS], F32R, kind=kind).ap()
            self.d_x2e = t("x2e_d", [D, TS], F32R, kind=kind).ap()
            self.d_y1dbg = t("y1dbg_d", [128, HC * 256], F32, kind=kind).ap()
            self.d_ybf = t("ybf_d", [128, HC * 256], F32, kind=kind).ap()
        self.d_kr = t("kr_d", [D, 2 * TS], F32R, kind=kind).ap()
        self.d_ke = t("ke_d", [D, 2 * TS], F32R, kind=kind).ap()
        self.d_vr = t("vr_d", [2 * TS, D], F32R, kind=kind).ap()
        self.d_ve = t("ve_d", [2 * TS, D], F32R, kind=kind).ap()
        self.d_qr = t("qr_d", [D, TS], F32R, kind=kind).ap()
        self.d_qe = t("qe_d", [D, TS], F32R, kind=kind).ap()
        self.d_g = t("g_d", [D, TS], F32, kind=kind).ap()
        self.d_xn = t("xn_d", [D, TS], F32, kind=kind).ap()
        self.d_hr = t("hr_d", [HIDDEN, TS], F32R, kind=kind).ap()
        self.d_he = t("he_d", [HIDDEN, TS], F32R, kind=kind).ap()
        self.d_yr = t("yr_d", [HIDDEN, TS], F32R, kind=kind).ap()
        self.d_ye = t("ye_d", [HIDDEN, TS], F32R, kind=kind).ap()
        self.d_g2 = t("g2_d", [D, TS], F32, kind=kind).ap()
        self.o_out = t("outT", [D, TS], F32, kind="ExternalOutput").ap()

    def _declare_sbuf(self):
        nc = self.nc
        at = nc.alloc_sbuf_tensor_at
        base0 = nc.sbuf_base
        M = ceil32(base0)
        self.M = M
        self.A = M + 18432
        self.B = M + 38912
        self.C = M + 53248
        self.BIG = M + 77824
        end = self.BIG + 131072
        nc.alloc_sbuf_tensor("manual_map_reserved",
                             [128, end - base0], mybir.dt.uint8)
        self.ones_col = at("ones_col", [128, 1], F32, offset=M).ap()
        self.ones_row = at("ones_row", [1, 128], F32, offset=M + 32).ap()
        self.h128r = at("h128r", [128, 128], F32R, offset=M + 544).ap()
        self.h128f = at("h128f", [128, 128], F32, offset=M + 1056).ap()
        self.stat = [at(f"stat{i}", [1, 1024], F32, offset=M + 1600 + 4096 * i).ap()
                     for i in range(4)]
        self._tens = {}

    def sb(self, name, shape, dtype, off):
        key = (name, tuple(shape), dtype, off)
        if key not in self._tens:
            self._tens[key] = self.nc.alloc_sbuf_tensor_at(
                f"{name}_{off}", shape, dtype, offset=off).ap()
        return self._tens[key]

    def _declare_psum(self):
        self.P = [self.ctx.enter_context(
            self.nc.psum_tensor(f"pb{i}", [128, 512], F32)) for i in range(8)]

    def ph0(self):
        self.op("DVE", "dve", lambda e: e.memset(self.ones_col[:, :], 1.0), inc=False)
        self.op("DVE", "dve", lambda e: e.memset(self.ones_row[:, :], 1.0))
        v = self.dma("POOL", "ini", self.h128f[:, :], self.i_h128[:, :])
        self.qwait("POOL", v)
        vc = self.dma("POOL", "ini", self.h128r[:, :], self.h128f[:, :])
        self.qwait("TE", vc)
        self.wait("TE", "dve", self.cnt["dve"])

    def ph1_slab(self, s):
        c0 = s * TS
        own = (s == 0)
        BIG = self.BIG
        xn = self.sb("xn", [128, DC * TS], F32, BIG)
        xn2r = self.sb("xn2r", [128, DC * TS], F32R, BIG + 65536)
        xn2e = self.sb("xn2e", [128, DC * TS], F32R, BIG)
        sq = [self.sb("sqb", [128, 1024], F32, self.B + 4096 * i) for i in range(2)]
        bnc = [self.sb("lnb", [128, 1024], F32, self.B + 8192 + 4096 * i) for i in range(2)]

        src = self.fm(self.i_x[:, c0:c0 + TS])
        vx = self.dma("ACT", "xs", self.fs(xn[:, :], TS), src)

        self.qwait("TE", vx)
        self.qwait("ACT", vx)
        mm_done = []
        sq_done = [0, 0]
        for dc in range(DC):
            xc = xn[:, dc * TS:(dc + 1) * TS]
            sqb = sq[dc % 2]
            if dc >= 2:
                self.wait("ACT", "mm", sq_done[dc % 2])
            va = self.op("ACT", "act", lambda e, i=xc, o=sqb: e.activation(
                out=o[:, :], in_=i, func=AF.Square))
            for t in range(2):
                last = (dc == DC - 1)
                self.op("TE", "mm", lambda e, i=xc, t=t, dc=dc, last=last: e.matmul(
                    self.P[t][0:1, :], self.ones_col[:, :], i[:, t * 512:(t + 1) * 512],
                    start=(dc == 0), stop=last), inc=last)
            self.wait("TE", "act", va)
            for t in range(2):
                last = (dc == DC - 1)
                v = self.op("TE", "mm", lambda e, i=sqb, t=t, dc=dc, last=last: e.matmul(
                    self.P[2 + t][0:1, :], self.ones_col[:, :], i[:, t * 512:(t + 1) * 512],
                    start=(dc == 0), stop=last), inc=True)
                sq_done[dc % 2] = v
        v_stats = self.cnt["mm"]

        st0, st1, st2, st3 = self.stat
        self.wait("DVE", "mm", v_stats)

        def R(fn):
            self.wait("DVE", "dve", self.cnt["dve"])
            return self.op("DVE", "dve", fn, inc=True)
        for t in range(2):
            R(lambda e, t=t: e.tensor_scalar_mul(st0[:, t * 512:(t + 1) * 512],
                                                 self.P[t][0:1, :], 1.0 / D))
            R(lambda e, t=t: e.tensor_scalar_mul(st1[:, t * 512:(t + 1) * 512],
                                                 self.P[2 + t][0:1, :], 1.0 / D))
        R(lambda e: e.tensor_tensor(st2[:, :], st0[:, :], st0[:, :], ALU.mult))
        R(lambda e: e.tensor_sub(st1[:, :], st1[:, :], st2[:, :]))
        self._rsqrt_row(st1, st2, st3, LN_EPS)
        self.wait("DVE", "dve", self.cnt["dve"])
        v_rows = self.op("DVE", "dve", lambda e: e.tensor_tensor(
            st3[:, :], st0[:, :], st2[:, :], ALU.mult))

        self.wait("TE", "dve", v_rows)
        for t in range(2):
            self.op("TE", "mm", lambda e, t=t: e.matmul(
                self.P[4 + t][:, :], self.ones_row[:, :], st2[:, t * 512:(t + 1) * 512],
                start=True, stop=True), inc=False)
            self.op("TE", "mm", lambda e, t=t: e.matmul(
                self.P[6 + t][:, :], self.ones_row[:, :], st3[:, t * 512:(t + 1) * 512],
                start=True, stop=True), inc=(t == 1))
        v_bcast = self.cnt["mm"]

        self.wait("DVE", "mm", v_bcast)
        xn_chunk_done = []
        for dc in range(DC):
            xc = xn[:, dc * TS:(dc + 1) * TS]
            b = bnc[dc % 2]
            for t in range(2):
                R(lambda e, i=xc, b=b, t=t: e.tensor_tensor(
                    b[:, t * 512:(t + 1) * 512], i[:, t * 512:(t + 1) * 512],
                    self.P[4 + t][:, :], ALU.mult))
            self.wait("DVE", "dve", self.cnt["dve"])
            v = self.op("DVE", "dve", lambda e, i=xc, b=b: e.tensor_tensor(
                i[:, 0:512], b[:, 0:512], self.P[6][:, :], ALU.subtract), inc=True)
            self.wait("DVE", "dve", self.cnt["dve"])
            v = self.op("DVE", "dve", lambda e, i=xc, b=b: e.tensor_tensor(
                i[:, 512:1024], b[:, 512:1024], self.P[7][:, :], ALU.subtract))
            xn_chunk_done.append(v)
        if own:
            self.wait("POOL", "dve", xn_chunk_done[-1])
            self.v_xn_dma = self.dma("POOL", "xnst", self.fm(self.d_xn[:, :]),
                                     self.fs(xn[:, :], TS))

        sq_done = [0, 0]
        for dc in range(DC):
            xc = xn[:, dc * TS:(dc + 1) * TS]
            sqb = sq[dc % 2]
            self.wait("ACT", "dve", xn_chunk_done[dc])
            if dc >= 2:
                self.wait("ACT", "mm", sq_done[dc % 2])
            va = self.op("ACT", "act", lambda e, i=xc, o=sqb: e.activation(
                out=o[:, :], in_=i, func=AF.Square))
            self.wait("TE", "act", va)
            for t in range(2):
                last = (dc == DC - 1)
                v = self.op("TE", "mm", lambda e, i=sqb, t=t, dc=dc, last=last: e.matmul(
                    self.P[t][0:1, :], self.ones_col[:, :], i[:, t * 512:(t + 1) * 512],
                    start=(dc == 0), stop=last), inc=True)
                sq_done[dc % 2] = v
        v_stats2 = self.cnt["mm"]
        self.wait("DVE", "mm", v_stats2)
        for t in range(2):
            R(lambda e, t=t: e.tensor_scalar_mul(st0[:, t * 512:(t + 1) * 512],
                                                 self.P[t][0:1, :], 1.0 / D))
        self._rsqrt_row(st0, st2, st3, RMS_EPS)
        v_rows2 = self.cnt["dve"]
        self.wait("TE", "dve", v_rows2)
        for t in range(2):
            self.op("TE", "mm", lambda e, t=t: e.matmul(
                self.P[4 + t][:, :], self.ones_row[:, :], st2[:, t * 512:(t + 1) * 512],
                start=True, stop=True), inc=(t == 1))
        v_bc2 = self.cnt["mm"]

        self.wait("DVE", "mm", v_bc2)
        for dc in range(DC):
            xc = xn[:, dc * TS:(dc + 1) * TS]
            b = bnc[dc % 2]
            if dc >= 2:
                self.qwait("DVE", self._xn2cast[dc % 2])
            for t in range(2):
                R(lambda e, i=xc, b=b, t=t: e.tensor_tensor(
                    b[:, t * 512:(t + 1) * 512], i[:, t * 512:(t + 1) * 512],
                    self.P[4 + t][:, :], ALU.mult))
            vb = self.cnt["dve"]
            self.wait("POOL", "dve", vb)
            if own and dc == 0:
                pass
            vc = self.dma("POOL", "xc", xn2r[:, dc * TS:(dc + 1) * TS], b[:, :])
            if not hasattr(self, "_xn2cast"):
                self._xn2cast = [None, None]
            self._xn2cast[dc % 2] = vc
            self.qwait("DVE", vc)
            if own:
                self.qwait("DVE", self.v_xn_dma)
            self.op("DVE", "dve", lambda e, b=b, dc=dc: e.tensor_sub(
                xn2e[:, dc * TS:(dc + 1) * TS], b[:, :],
                xn2r[:, dc * TS:(dc + 1) * TS].bitcast(F32)))
        v_split = self.cnt["dve"]

        if self.st["debug"] and own:
            self.wait("POOL", "dve", v_split)
            self.dma("POOL", "dbg", self.fm(self.d_x2r[:, :]), self.fs(xn2r[:, :], TS))
            self.dma("POOL", "dbg", self.fm(self.d_x2e[:, :]), self.fs(xn2e[:, :], TS))

        self.fence()
        self.proj_featmaj("wk", self.i_w["wk"], AF.Silu, self.d_kr, self.d_ke,
                          xn2r, xn2e, out_cols=(c0, TS), v_ready=v_split)
        self.fence()
        self.proj_tokmaj(xn2r, xn2e, slab_off=c0, v_ready=v_split)
        if own:
            self.fence()
            self.proj_featmaj("wq", self.i_w["wq"], AF.Silu, self.d_qr, self.d_qe,
                              xn2r, xn2e, out_cols=(0, TS), v_ready=v_split)
            self.fence()
            self.proj_featmaj("wg", self.i_w["wg"], AF.Sigmoid, self.d_g, None,
                              xn2r, xn2e, out_cols=(0, TS), v_ready=v_split)

    def _rsqrt_row(self, vrow, out, tmp, eps):
        def R(fn):
            self.wait("DVE", "dve", self.cnt["dve"])
            return self.op("DVE", "dve", fn, inc=True)

        def A(fn):
            self.wait("ACT", "act", self.cnt["act"])
            return self.op("ACT", "act", fn, inc=True)
        R(lambda e: e.tensor_scalar_add(vrow[:, :], vrow[:, :], eps))
        self.wait("ACT", "dve", self.cnt["dve"])
        va = self.op("ACT", "act", lambda e: e.activation(out=tmp[:, :], in_=vrow[:, :],
                                                          func=AF.Sqrt))
        self.wait("DVE", "act", va)
        R(lambda e: e.reciprocal(out[:, :], tmp[:, :]))
        for i in range(2):
            R(lambda e: e.tensor_tensor(tmp[:, :], out[:, :], out[:, :], ALU.mult))
            R(lambda e: e.tensor_tensor(tmp[:, :], tmp[:, :], vrow[:, :], ALU.mult))
            R(lambda e: e.tensor_scalar(tmp[:, :], tmp[:, :], -0.5, 1.5,
                                        ALU.mult, ALU.add))
            self.wait("DVE", "dve", self.cnt["dve"])
            self.op("DVE", "dve", lambda e: e.tensor_tensor(
                out[:, :], out[:, :], tmp[:, :], ALU.mult), inc=True)

    def proj_featmaj(self, nm, w, act, dr, de, xr, xe, out_cols, v_ready,
                    n_out=D):
        c0, TS_ = out_cols
        wtmp = [self.sb(nm + "wt", [128, 512], F32, self.A + 2048 * i) for i in range(3)]
        wr = [self.sb(nm + "wr", [128, 512], F32R, self.A + 6144 + 2048 * i) for i in range(3)]
        we = [self.sb(nm + "we", [128, 512], F32R, self.A + 12288 + 2048 * i) for i in range(3)]
        rb = [self.sb(nm + "rb", [128, 512], F32R, self.B + 2048 * i) for i in range(2)]
        ebw = self.sb(nm + "ebw", [128, 8 * 512], F32, self.C + 2048)
        n_og = n_out // 512
        self.wait("TE", "dve", v_ready)
        mmrec = {}
        it = 0
        evac_v = 0
        st = {"prev_sub": None, "latest": None, "last_cast": None}
        pending = []

        def mkstep(og, j, va):
            def step():
                oc, t = divmod(j, 2)
                bsl = ebw[:, j * 512:(j + 1) * 512]
                rbb = rb[j % 2]
                row0 = (og * 4 + oc) * 128
                dst_cols = (c0 + t * 512, c0 + t * 512 + 512)
                self.wait("POOL", "act", va)
                if de is None:
                    vo = self.dma("POOL", "ps0",
                                  dr[row0:row0 + 128, dst_cols[0]:dst_cols[1]], bsl)
                    st["latest"] = vo
                    st["last_cast"] = vo
                    return
                if st["prev_sub"] is not None:
                    self.wait("POOL", "dve", st["prev_sub"])
                if st["latest"] is not None:
                    self.qwait("POOL", st["latest"])
                vc = self.dma("POOL", "pc0", rbb[:, :], bsl)
                st["last_cast"] = vc
                self.qwait("DVE", vc)
                vs = self.op("DVE", "dve", lambda e, b=bsl, rbb=rbb: e.tensor_sub(
                    b, b, rbb[:, :].bitcast(F32)))
                st["prev_sub"] = vs
                self.wait("POOL", "dve", vs)
                self.dma("POOL", "ps0",
                         dr[row0:row0 + 128, dst_cols[0]:dst_cols[1]], rbb[:, :])
                vo = self.dma("POOL", "ps0",
                              de[row0:row0 + 128, dst_cols[0]:dst_cols[1]],
                              bsl.bitcast(F32R))
                st["latest"] = vo
            return step

        for og in range(n_og):
            for dc in range(DC):
                r = it % 3
                if it >= 3:
                    self.wait("SP", "mm", mmrec[(it - 3)])
                else:
                    self.wait("SP", "mm", self.cnt["mm"])
                vload = self.dma("SP", f"w{r}", wtmp[r][:, :],
                                 w[dc * 128:(dc + 1) * 128, og * 512:(og + 1) * 512])
                self.qwait("POOL", vload)
                if it >= 3:
                    self.wait("POOL", "mm", mmrec[(it - 3)])
                vcast = self.dma("POOL", f"wc{r}", wr[r][:, :], wtmp[r][:, :])
                self.qwait("DVE", vcast)
                vsub = self.op("DVE", "dve", lambda e, r=r: e.tensor_sub(
                    we[r][:, :], wtmp[r][:, :], wr[r][:, :].bitcast(F32)))
                self.wait("TE", "dve", vsub)
                if og > 0 and dc == 0:
                    self.wait("TE", "act", evac_v)
                for oc in range(4):
                    for t in range(2):
                        pb = self.P[oc * 2 + t]
                        mv_r = xr[:, dc * TS_ + t * 512: dc * TS_ + t * 512 + 512]
                        mv_e = xe[:, dc * TS_ + t * 512: dc * TS_ + t * 512 + 512]
                        stt = (dc == 0)
                        stp = (dc == DC - 1)
                        lr = wr[r][:, oc * 128:(oc + 1) * 128]
                        le = we[r][:, oc * 128:(oc + 1) * 128]
                        self.op("TE", "mm", lambda e, pb=pb, lr=lr, mv=mv_r, s=stt: e.matmul(
                            pb[:, :], lr, mv, start=s, stop=False), inc=False)
                        self.op("TE", "mm", lambda e, pb=pb, lr=lr, mv=mv_e: e.matmul(
                            pb[:, :], lr, mv, start=False, stop=False), inc=False)
                        last = oc == 3 and t == 1
                        self.op("TE", "mm", lambda e, pb=pb, le=le, mv=mv_r, p=stp: e.matmul(
                            pb[:, :], le, mv, start=False, stop=p), inc=last)
                mmrec[it] = self.cnt["mm"]
                it += 1
                if dc % 2 == 1 and pending:
                    pending.pop(0)()
            while pending:
                pending.pop(0)()
            v_grp = self.cnt["mm"]
            self.wait("ACT", "mm", v_grp)
            if st["last_cast"] is not None:
                self.qwait("ACT", st["last_cast"])
            if st["latest"] is not None:
                self.qwait("ACT", st["latest"])
            for j in range(8):
                self.op("ACT", "act", lambda e, pb=self.P[j], j=j, f=act: e.activation(
                    out=ebw[:, j * 512:(j + 1) * 512], in_=pb[:, :], func=f),
                    inc=(j == 7))
            evac_v = self.cnt["act"]
            pending = [mkstep(og, j, evac_v) for j in range(8)]
        while pending:
            pending.pop(0)()

    def proj_tokmaj(self, xr, xe, slab_off, v_ready):
        nm = "wv"
        w = self.i_w["wv"]
        wtmp = [self.sb(nm + "wt", [128, 512], F32, self.A + 2048 * i) for i in range(3)]
        wr = [self.sb(nm + "wr", [128, 512], F32R, self.A + 6144 + 2048 * i) for i in range(3)]
        we = [self.sb(nm + "we", [128, 512], F32R, self.A + 12288 + 2048 * i) for i in range(3)]
        rb = [self.sb(nm + "rb", [128, 512], F32R, self.B + 2048 * i) for i in range(2)]
        ebw = self.sb(nm + "ebw", [128, 8 * 512], F32, self.C + 2048)
        self.wait("TE", "dve", v_ready)
        it = 0
        mmrec = {}
        evac_v = 0
        st = {"prev_sub": None, "latest": None, "last_cast": None}
        pending = []

        def mkstep(dt, tb, va):
            def step():
                bsl = ebw[:, tb * 512:(tb + 1) * 512]
                rbb = rb[tb % 2]
                row0 = slab_off + tb * 128
                self.wait("POOL", "act", va)
                if st["prev_sub"] is not None:
                    self.wait("POOL", "dve", st["prev_sub"])
                if st["latest"] is not None:
                    self.qwait("POOL", st["latest"])
                vc = self.dma("POOL", "pc0", rbb[:, :], bsl)
                st["last_cast"] = vc
                self.qwait("DVE", vc)
                vs = self.op("DVE", "dve", lambda e, b=bsl, rbb=rbb: e.tensor_sub(
                    b, b, rbb[:, :].bitcast(F32)))
                st["prev_sub"] = vs
                self.wait("POOL", "dve", vs)
                self.dma("POOL", "ps0",
                         self.d_vr[row0:row0 + 128, dt * 512:(dt + 1) * 512], rbb[:, :])
                vo = self.dma("POOL", "ps0",
                              self.d_ve[row0:row0 + 128, dt * 512:(dt + 1) * 512],
                              bsl.bitcast(F32R))
                st["latest"] = vo
            return step

        for dt in range(4):
            for dc in range(DC):
                r = it % 3
                if it >= 3:
                    self.wait("SP", "mm", mmrec[it - 3])
                else:
                    self.wait("SP", "mm", self.cnt["mm"])
                vload = self.dma("SP", f"w{r}", wtmp[r][:, :],
                                 w[dc * 128:(dc + 1) * 128, dt * 512:(dt + 1) * 512])
                self.qwait("POOL", vload)
                if it >= 3:
                    self.wait("POOL", "mm", mmrec[it - 3])
                vcast = self.dma("POOL", f"wc{r}", wr[r][:, :], wtmp[r][:, :])
                self.qwait("DVE", vcast)
                vsub = self.op("DVE", "dve", lambda e, r=r: e.tensor_sub(
                    we[r][:, :], wtmp[r][:, :], wr[r][:, :].bitcast(F32)))
                self.wait("TE", "dve", vsub)
                if dt > 0 and dc == 0:
                    self.wait("TE", "act", evac_v)
                for tb in range(8):
                    pb = self.P[tb]
                    st_r = xr[:, dc * TS + tb * 128: dc * TS + tb * 128 + 128]
                    st_e = xe[:, dc * TS + tb * 128: dc * TS + tb * 128 + 128]
                    stt = (dc == 0)
                    stp = (dc == DC - 1)
                    last = tb == 7
                    self.op("TE", "mm", lambda e, pb=pb, l=st_r, m=wr[r], s=stt: e.matmul(
                        pb[:, :], l, m[:, :], start=s, stop=False), inc=False)
                    self.op("TE", "mm", lambda e, pb=pb, l=st_r, m=we[r]: e.matmul(
                        pb[:, :], l, m[:, :], start=False, stop=False), inc=False)
                    self.op("TE", "mm", lambda e, pb=pb, l=st_e, m=wr[r], p=stp: e.matmul(
                        pb[:, :], l, m[:, :], start=False, stop=p), inc=last)
                mmrec[it] = self.cnt["mm"]
                it += 1
                if dc % 2 == 1 and pending:
                    pending.pop(0)()
            while pending:
                pending.pop(0)()
            v_grp = self.cnt["mm"]
            self.wait("ACT", "mm", v_grp)
            if st["last_cast"] is not None:
                self.qwait("ACT", st["last_cast"])
            if st["latest"] is not None:
                self.qwait("ACT", st["latest"])
            for tb in range(8):
                self.op("ACT", "act", lambda e, pb=self.P[tb], tb=tb: e.activation(
                    out=ebw[:, tb * 512:(tb + 1) * 512], in_=pb[:, :], func=AF.Silu),
                    inc=(tb == 7))
            evac_v = self.cnt["act"]
            pending = [mkstep(dt, tb, evac_v) for tb in range(8)]
        while pending:
            pending.pop(0)()

    def ph_ffn(self):
        BIG = self.BIG
        xn = self.sb("fxn", [128, DC * TS], F32, BIG)
        xnr = self.sb("fxnr", [128, DC * TS], F32R, BIG + 65536)
        xne = self.sb("fxne", [128, DC * TS], F32R, BIG)

        src = self.fm(self.d_xn[:, :])
        vx = self.dma("POOL", "xnst", self.fs(xn[:, :], TS), src)
        self.qwait("POOL", vx)
        cast_v = []
        for dc in range(DC):
            vc = self.dma("POOL", "fc", xnr[:, dc * TS:(dc + 1) * TS],
                          xn[:, dc * TS:(dc + 1) * TS])
            cast_v.append(vc)
        self.qwait("DVE", cast_v[-1])
        for dc in range(DC):
            self.op("DVE", "dve", lambda e, dc=dc: e.tensor_sub(
                xne[:, dc * TS:(dc + 1) * TS], xn[:, dc * TS:(dc + 1) * TS],
                xnr[:, dc * TS:(dc + 1) * TS].bitcast(F32)))
        v_split = self.cnt["dve"]

        w1, w3 = self.i_w["w1"], self.i_w["w3"]
        wt1 = [self.sb("f1t", [128, 512], F32, self.A + 2048 * i) for i in range(3)]
        wr1 = [self.sb("f1r", [128, 512], F32R, self.A + 6144 + 2048 * i) for i in range(3)]
        we1 = [self.sb("f1e", [128, 512], F32R, self.A + 12288 + 2048 * i) for i in range(3)]
        ug = self.sb("fug", [128, 8 * 512], F32, self.C)
        rb = [self.sb("fhr", [128, 512], F32R, self.C + 16384 + 2048 * i) for i in range(2)]
        self.wait("TE", "dve", v_split)
        it = 0
        mmrec = {}
        ug_evac_v = 0
        mul_v = 0
        st = {"prev_sub": None, "latest": None, "last_cast": None}
        pending = []

        def mkstep(hg, j, vmul):
            def step():
                hc, t = divmod(j, 2)
                bsl = ug[:, j * 512:(j + 1) * 512]
                rbb = rb[j % 2]
                row0 = (hg * 4 + hc) * 128
                if st["prev_sub"] is not None:
                    self.wait("POOL", "dve", st["prev_sub"])
                self.wait("POOL", "dve", vmul)
                if st["latest"] is not None:
                    self.qwait("POOL", st["latest"])
                vc = self.dma("POOL", "pc0", rbb[:, :], bsl)
                st["last_cast"] = vc
                self.qwait("DVE", vc)
                vs = self.op("DVE", "dve", lambda e, b=bsl, rbb=rbb: e.tensor_sub(
                    b, b, rbb[:, :].bitcast(F32)))
                st["prev_sub"] = vs
                self.wait("POOL", "dve", vs)
                self.dma("POOL", "ps0",
                         self.d_hr[row0:row0 + 128, t * 512:(t + 1) * 512], rbb[:, :])
                vo = self.dma("POOL", "ps0",
                              self.d_he[row0:row0 + 128, t * 512:(t + 1) * 512],
                              bsl.bitcast(F32R))
                st["latest"] = vo
            return step

        for hg in range(16):
            for wi, w in ((0, w1), (1, w3)):
                for dc in range(DC):
                    r = it % 3
                    if it >= 3:
                        self.wait("SP", "mm", mmrec[it - 3])
                    else:
                        self.wait("SP", "mm", self.cnt["mm"])
                    vload = self.dma("SP", f"w{r}", wt1[r][:, :],
                                     w[dc * 128:(dc + 1) * 128, hg * 512:(hg + 1) * 512])
                    self.qwait("POOL", vload)
                    if it >= 3:
                        self.wait("POOL", "mm", mmrec[it - 3])
                    vcast = self.dma("POOL", f"wc{r}", wr1[r][:, :], wt1[r][:, :])
                    self.qwait("DVE", vcast)
                    vsub = self.op("DVE", "dve", lambda e, r=r: e.tensor_sub(
                        we1[r][:, :], wt1[r][:, :], wr1[r][:, :].bitcast(F32)))
                    self.wait("TE", "dve", vsub)
                    if dc == 0 and (hg > 0 or wi > 0):
                        if wi == 0:
                            self.wait("TE", "dve", mul_v)
                        else:
                            self.wait("TE", "act", ug_evac_v)
                    for hc in range(4):
                        for t in range(2):
                            pb = self.P[hc * 2 + t]
                            mv_r = xnr[:, dc * TS + t * 512: dc * TS + t * 512 + 512]
                            mv_e = xne[:, dc * TS + t * 512: dc * TS + t * 512 + 512]
                            lr = wr1[r][:, hc * 128:(hc + 1) * 128]
                            le = we1[r][:, hc * 128:(hc + 1) * 128]
                            stt = (dc == 0)
                            stp = (dc == DC - 1)
                            last = hc == 3 and t == 1
                            self.op("TE", "mm", lambda e, pb=pb, l=lr, m=mv_r, s=stt: e.matmul(
                                pb[:, :], l, m, start=s, stop=False), inc=False)
                            self.op("TE", "mm", lambda e, pb=pb, l=lr, m=mv_e: e.matmul(
                                pb[:, :], l, m, start=False, stop=False), inc=False)
                            self.op("TE", "mm", lambda e, pb=pb, l=le, m=mv_r, p=stp: e.matmul(
                                pb[:, :], l, m, start=False, stop=p), inc=last)
                    mmrec[it] = self.cnt["mm"]
                    it += 1
                    if dc % 2 == 1 and pending:
                        pending.pop(0)()
                v_grp = self.cnt["mm"]
                if wi == 0:
                    while pending:
                        pending.pop(0)()
                    self.wait("ACT", "mm", v_grp)
                    if st["last_cast"] is not None:
                        self.qwait("ACT", st["last_cast"])
                    if st["latest"] is not None:
                        self.qwait("ACT", st["latest"])
                    for j in range(8):
                        self.op("ACT", "act", lambda e, pb=self.P[j], j=j: e.activation(
                            out=ug[:, j * 512:(j + 1) * 512], in_=pb[:, :], func=AF.Silu),
                            inc=(j == 7))
                    ug_evac_v = self.cnt["act"]
                else:
                    self.wait("DVE", "mm", v_grp)
                    self.wait("DVE", "act", ug_evac_v)
                    vmuls = []
                    for j in range(8):
                        self.wait("DVE", "dve", self.cnt["dve"])
                        vm = self.op("DVE", "dve", lambda e, pb=self.P[j], j=j: e.tensor_tensor(
                            ug[:, j * 512:(j + 1) * 512], ug[:, j * 512:(j + 1) * 512],
                            pb[:, :], ALU.mult), inc=True)
                        vmuls.append(vm)
                    mul_v = self.cnt["dve"]
                    pending = [mkstep(hg, j, vmuls[j]) for j in range(8)]
        while pending:
            pending.pop(0)()

    def ph_fwht(self):
        TT = 256
        for tt in range(4):
            self._fwht_tile(tt, TT)

    def _fwht_tile(self, tt, TT):
        BIG = self.BIG
        a = self.sb("wa", [128, HC * TT], F32, BIG)
        bq = self.sb("wb", [128, HC * TT], F32, BIG + 65536)
        bqr = self.sb("wbr", [128, HC * TT], F32R, BIG + 65536)
        hcr = [self.sb("whr", [128, 8 * TT], F32R, self.A + 8192 * i) for i in range(2)]
        hce = [self.sb("whe", [128, 8 * TT], F32R, self.C + 8192 * i) for i in range(2)]

        cols = (tt * TT, (tt + 1) * TT)
        self.qwait("ACT", self.qtotal("ys"))
        self.qwait("ACT", self.qtotal("dbg"))
        self.wait("TE", "act", self.cnt["act"])
        ev = {}
        ld = {}
        for g in range(8):
            r = g % 2
            if g >= 2:
                self.wait("SP", "mm", ld[g - 2])
            else:
                self.wait("SP", "mm", self.cnt["mm"])
            rows = slice(g * 1024, (g + 1) * 1024)
            v1 = self.dma("SP", f"f{r}", self.fs(hcr[r][:, :], TT),
                          self.fm(self.d_hr[rows, cols[0]:cols[1]]))
            v2 = self.dma("SP", f"f{r}", self.fs(hce[r][:, :], TT),
                          self.fm(self.d_he[rows, cols[0]:cols[1]]))
            self.qwait("TE", v2)
            for k in range(8):
                c = g * 8 + k
                pb = self.P[c % 8]
                if c >= 8:
                    self.wait("TE", "act", ev[c - 8])
                self.op("TE", "mm", lambda e, pb=pb, r=r, k=k: e.matmul(
                    pb[:, 0:TT], self.h128r[:, :], hcr[r][:, k * TT:(k + 1) * TT],
                    start=True, stop=False), inc=False)
                v = self.op("TE", "mm", lambda e, pb=pb, r=r, k=k: e.matmul(
                    pb[:, 0:TT], self.h128r[:, :], hce[r][:, k * TT:(k + 1) * TT],
                    start=False, stop=True))
                self.wait("ACT", "mm", v)
                ev[c] = self.op("ACT", "act", lambda e, pb=pb, c=c: e.activation(
                    out=a[:, c * TT:(c + 1) * TT], in_=pb[:, 0:TT], func=AF.Copy,
                    scale=FWHT_SCALE))
            ld[g] = self.cnt["mm"]
        v_y = self.cnt["act"]
        self.wait("DVE", "act", v_y)
        self.wait("POOL", "act", v_y)
        src_, dst_ = a, bq
        for lev in range(6):
            stq = 1 << lev
            span = stq * TT
            L2 = (2 * span) // 3 // 32 * 32
            def mk(which, src=src_, dst=dst_, span=span, L2=L2):
                def f(e):
                    o = dst[:, :].rearrange("p (g two s) -> p g two s", two=2, s=span)
                    i0 = src[:, :].rearrange("p (g two s) -> p g two s", two=2, s=span)
                    if which == "add":
                        return e.tensor_tensor(o[:, :, 0, :], i0[:, :, 0, :],
                                               i0[:, :, 1, :], ALU.add)
                    if which == "sub_pool":
                        return e.tensor_tensor(o[:, :, 1, 0:L2], i0[:, :, 0, 0:L2],
                                               i0[:, :, 1, 0:L2], ALU.subtract)
                    return e.tensor_tensor(o[:, :, 1, L2:span], i0[:, :, 0, L2:span],
                                           i0[:, :, 1, L2:span], ALU.subtract)
                return f
            self.wait("DVE", "dve", self.cnt["dve"])
            self.wait("POOL", "pool", self.cnt["pool"])
            v0 = self.op("DVE", "dve", mk("add"), inc=False)
            v0 = self.op("DVE", "dve", mk("sub_dve"))
            v1 = self.op("POOL", "pool", mk("sub_pool"))
            self.wait("DVE", "pool", v1)
            self.wait("POOL", "dve", v0)
            src_, dst_ = dst_, src_
        v_res_dve = self.cnt["dve"]
        v_res_pool = self.cnt["pool"]
        self.wait("ACT", "dve", v_res_dve)
        self.wait("ACT", "pool", v_res_pool)
        vc = self.op("ACT", "act", lambda e: e.activation(
            out=bqr[:, :], in_=a[:, :], func=AF.Copy))
        self.wait("DVE", "act", vc)
        vs = self.op("DVE", "dve", lambda e: e.tensor_sub(
            a[:, :], a[:, :], bqr[:, :].bitcast(F32)))
        self.wait("POOL", "dve", vs)
        if self.st["debug"] and tt == 0:
            vd = self.dma("POOL", "dbg", self.d_ybf[:, :HC * TT],
                          a[:, :])
            self.qwait("POOL", vd)
        self.dma("POOL", "ys", self.fm(self.d_yr[:, cols[0]:cols[1]]),
                 self.fs(bqr[:, :], TT))
        self.dma("POOL", "ys", self.fm(self.d_ye[:, cols[0]:cols[1]]),
                 self.fs(a[:, :].bitcast(F32R), TT))

    def ph_w2(self):
        w2 = self.i_w["w2"]
        wt = [self.sb("w2t", [128, 1024], F32, self.C + 4096 * i) for i in range(2)]
        wr = [self.sb("w2r", [128, 1024], F32R, self.C + 8192 + 4096 * i) for i in range(2)]
        we = [self.sb("w2e", [128, 1024], F32R, self.C + 16384 + 4096 * i) for i in range(2)]
        yr = [self.sb("w2yr", [128, 512], F32R, self.A + 2048 * i) for i in range(2)]
        ye = [self.sb("w2ye", [128, 512], F32R, self.A + 4096 + 2048 * i) for i in range(2)]
        g2w = self.sb("w2g", [128, 8 * 512], F32, self.BIG)
        mmrec = {}
        it = 0
        evac_v = 0
        st = {"latest": None}
        pending = []

        def mkstep(t, og, oc, va):
            def step():
                row0 = (og * 8 + oc) * 128
                self.wait("POOL", "act", va)
                vo = self.dma("POOL", "gs0",
                              self.d_g2[row0:row0 + 128, t * 512:(t + 1) * 512],
                              g2w[:, oc * 512:(oc + 1) * 512])
                st["latest"] = vo
            return step

        for t in range(2):
            for og in range(2):
                for hc in range(HC):
                    r = it % 2
                    if it >= 2:
                        self.wait("SP", "mm", mmrec[it - 2])
                    else:
                        self.wait("SP", "mm", self.cnt["mm"])
                    vl = self.dma("SP", f"w{r}", wt[r][:, :],
                                  w2[hc * 128:(hc + 1) * 128, og * 1024:(og + 1) * 1024])
                    vy1 = self.dma("SP", f"w{r}", yr[r][:, :],
                                   self.d_yr[hc * 128:(hc + 1) * 128, t * 512:(t + 1) * 512])
                    vy2 = self.dma("SP", f"w{r}", ye[r][:, :],
                                   self.d_ye[hc * 128:(hc + 1) * 128, t * 512:(t + 1) * 512])
                    self.qwait("POOL", vy2)
                    if it >= 2:
                        self.wait("POOL", "mm", mmrec[it - 2])
                    vcast = self.dma("POOL", f"wc{r}", wr[r][:, :], wt[r][:, :])
                    self.qwait("DVE", vcast)
                    vsub = self.op("DVE", "dve", lambda e, r=r: e.tensor_sub(
                        we[r][:, :], wt[r][:, :], wr[r][:, :].bitcast(F32)))
                    self.wait("TE", "dve", vsub)
                    self.qwait("TE", vy2)
                    if hc == 0 and it > 0:
                        self.wait("TE", "act", evac_v)
                    for oc in range(8):
                        pb = self.P[oc]
                        lr = wr[r][:, oc * 128:(oc + 1) * 128]
                        le = we[r][:, oc * 128:(oc + 1) * 128]
                        stt = (hc == 0)
                        stp = (hc == HC - 1)
                        last = oc == 7
                        self.op("TE", "mm", lambda e, pb=pb, l=lr, m=yr[r], s=stt: e.matmul(
                            pb[:, :], l, m[:, :], start=s, stop=False), inc=False)
                        self.op("TE", "mm", lambda e, pb=pb, l=lr, m=ye[r]: e.matmul(
                            pb[:, :], l, m[:, :], start=False, stop=False), inc=False)
                        self.op("TE", "mm", lambda e, pb=pb, l=le, m=yr[r], p=stp: e.matmul(
                            pb[:, :], l, m[:, :], start=False, stop=p), inc=last)
                    mmrec[it] = self.cnt["mm"]
                    it += 1
                    if it % 8 == 0 and pending:
                        pending.pop(0)()
                while pending:
                    pending.pop(0)()
                v_grp = self.cnt["mm"]
                self.wait("ACT", "mm", v_grp)
                if st["latest"] is not None:
                    self.qwait("ACT", st["latest"])
                for oc in range(8):
                    self.op("ACT", "act", lambda e, pb=self.P[oc], oc=oc: e.activation(
                        out=g2w[:, oc * 512:(oc + 1) * 512], in_=pb[:, :], func=AF.Silu),
                        inc=(oc == 7))
                evac_v = self.cnt["act"]
                pending = [mkstep(t, og, oc, evac_v) for oc in range(8)]
        while pending:
            pending.pop(0)()

    def ph_attn(self, qi):
        BIG = self.BIG
        nkb = NKB[qi]
        qr = self.sb("aqr", [128, DC * 512], F32R, BIG)
        qe = self.sb("aqe", [128, DC * 512], F32R, BIG + 32768)
        stor = self.sb("astr", [128, 16 * 512], F32R, BIG + 65536)
        stoe = self.sb("aste", [128, 16 * 512], F32R, BIG + 98304)
        kbr = [self.sb("akbr", [128, DC * 128], F32R, self.A + 8192 * i) for i in range(2)]
        kbe = [self.sb("akbe", [128, DC * 128], F32R, self.C + 8192 * i) for i in range(2)]
        mbuf = [self.sb("amsk", [128, 512], F32, self.M + 1600 + 2048 * i) for i in range(2)]
        sbnc = [self.sb("asb", [128, 512], F32, self.B + 2048 * i) for i in range(2)]
        srb = [self.sb("asr", [128, 512], F32R, self.B + 4096 + 2048 * i) for i in range(2)]

        cq = qi * 512
        v1 = self.dma("SP", "qq", self.fs(qr[:, :], 512),
                      self.fm(self.d_qr[:, cq:cq + 512]))
        v2 = self.dma("SP", "qq", self.fs(qe[:, :], 512),
                      self.fm(self.d_qe[:, cq:cq + 512]))
        self.qwait("TE", v2)
        kload = {}
        mload = {}
        st_done = {}
        for j, cb in enumerate(KBLIST[qi]):
            r = j % 2
            if j >= 2:
                self.wait("SP", "mm", st_done[j - 2][0])
            else:
                self.wait("SP", "mm", self.cnt["mm"])
            lk1 = self.dma("SP", f"kb{r}", self.fs(kbr[r][:, :], 128),
                           self.fm(self.d_kr[:, cb * 128:(cb + 1) * 128]))
            lk2 = self.dma("SP", f"kb{r}", self.fs(kbe[r][:, :], 128),
                           self.fm(self.d_ke[:, cb * 128:(cb + 1) * 128]))
            mi = (8 if qi else 0) + j if qi == 0 else 8 + j
            mi = j if qi == 0 else 8 + j
            if j >= 2:
                self.wait("ACT", "dve", st_done[j - 2][1])
            else:
                self.wait("ACT", "mm", self.cnt["mm"])
            lm = self.dma("ACT", f"mk{r}", mbuf[r][:, :], self.i_masks[mi, :, :])
            self.qwait("TE", lk2)
            pb = self.P[j % 2]
            if j >= 2:
                self.wait("TE", "dve", st_done[j - 2][1])
            for dc in range(DC):
                lr = kbr[r][:, dc * 128:(dc + 1) * 128]
                le = kbe[r][:, dc * 128:(dc + 1) * 128]
                mr = qr[:, dc * 512:(dc + 1) * 512]
                me = qe[:, dc * 512:(dc + 1) * 512]
                stt = (dc == 0)
                stp = (dc == DC - 1)
                self.op("TE", "mm", lambda e, pb=pb, l=lr, m=mr, s=stt: e.matmul(
                    pb[:, :], l, m, start=s, stop=False), inc=False)
                self.op("TE", "mm", lambda e, pb=pb, l=lr, m=me: e.matmul(
                    pb[:, :], l, m, start=False, stop=False), inc=False)
                self.op("TE", "mm", lambda e, pb=pb, l=le, m=mr, p=stp: e.matmul(
                    pb[:, :], l, m, start=False, stop=p), inc=stp)
            v_s = self.cnt["mm"]
            self.wait("DVE", "mm", v_s)
            self.qwait("DVE", lm)
            b = sbnc[r]
            rbb = srb[r]
            if j >= 2:
                self.qwait("DVE", st_done[j - 2][2])
            vm = self.op("DVE", "dve", lambda e, pb=pb, b=b, m=mbuf[r]: e.tensor_tensor(
                b[:, :], pb[:, :], m[:, :], ALU.mult), inc=True)
            self.wait("POOL", "dve", vm)
            vc = self.dma("POOL", "ogc", stor[:, j * 512:(j + 1) * 512], b[:, :])
            self.qwait("DVE", vc)
            vs = self.op("DVE", "dve", lambda e, b=b, j=j: e.tensor_sub(
                stoe[:, j * 512:(j + 1) * 512], b[:, :],
                stor[:, j * 512:(j + 1) * 512].bitcast(F32)))
            st_done[j] = (v_s, vs, vc)
        v_scores = self.cnt["dve"]

        ogr = self.sb("aogr", [128, DC * 512], F32R, BIG)
        oge = self.sb("aoge", [128, DC * 512], F32R, BIG + 32768)
        vb = [self.sb("avb", [128, 512], F32R, self.A + 2048 * i) for i in range(4)]
        gb = [self.sb("agb", [128, 512], F32, self.A + 8192 + 2048 * i) for i in range(2)]
        ob = [self.sb("aob", [128, 512], F32, self.B + 2048 * i) for i in range(2)]
        orb = [self.sb("aor", [128, 512], F32R, self.B + 4096 + 2048 * i) for i in range(2)]
        self.wait("SP", "mm", self.cnt["mm"])
        self.wait("ACT", "mm", self.cnt["mm"])
        self.wait("TE", "dve", v_scores)
        vload = {}
        og_ev = 0
        o_ev = [None, None]
        guse = [0, 0]
        for og in range(4):
            for j, cb in enumerate(KBLIST[qi]):
                r = j % 2
                row0 = cb * 128
                if (og * nkb + j) >= 2:
                    self.wait("SP", "mm", vload[og * nkb + j - 2])
                else:
                    self.wait("SP", "mm", self.cnt["mm"])
                l1 = self.dma("SP", f"vb{r}", vb[r * 2][:, :],
                              self.d_vr[row0:row0 + 128, og * 512:(og + 1) * 512])
                l2 = self.dma("SP", f"vb{r}", vb[r * 2 + 1][:, :],
                              self.d_ve[row0:row0 + 128, og * 512:(og + 1) * 512])
                self.qwait("TE", l2)
                if j == 0 and og > 0:
                    self.wait("TE", "dve", og_ev)
                for oc in range(4):
                    pb = self.P[4 * (og % 2) + oc]
                    lr = vb[r * 2][:, oc * 128:(oc + 1) * 128]
                    le = vb[r * 2 + 1][:, oc * 128:(oc + 1) * 128]
                    mr = stor[:, j * 512:(j + 1) * 512]
                    me = stoe[:, j * 512:(j + 1) * 512]
                    stt = (j == 0)
                    stp = (j == nkb - 1)
                    last = oc == 3
                    self.op("TE", "mm", lambda e, pb=pb, l=lr, m=mr, s=stt: e.matmul(
                        pb[:, :], l, m, start=s, stop=False), inc=False)
                    self.op("TE", "mm", lambda e, pb=pb, l=le, m=mr: e.matmul(
                        pb[:, :], l, m, start=False, stop=False), inc=False)
                    self.op("TE", "mm", lambda e, pb=pb, l=lr, m=me, p=stp: e.matmul(
                        pb[:, :], l, m, start=False, stop=p), inc=last)
                vload[og * nkb + j] = self.cnt["mm"]
            v_og = self.cnt["mm"]
            self.wait("DVE", "mm", v_og)
            for oc in range(4):
                ch = og * 4 + oc
                b = ob[oc % 2]
                rbb = orb[oc % 2]
                row0 = ch * 128
                if ch >= 2:
                    self.wait("ACT", "dve", guse[oc % 2])
                else:
                    self.wait("ACT", "mm", self.cnt["mm"])
                lg = self.dma("ACT", f"gl{oc % 2}", gb[oc % 2][:, :],
                              self.d_g[row0:row0 + 128, cq:cq + 512])
                self.qwait("DVE", lg)
                if ch >= 2:
                    self.qwait("DVE", o_ev[oc % 2])
                vm = self.op("DVE", "dve", lambda e, pb=self.P[4 * (og % 2) + oc], b=b,
                        g=gb[oc % 2]: e.tensor_tensor(b[:, :], pb[:, :], g[:, :],
                                                      ALU.mult), inc=True)
                guse[oc % 2] = vm
                self.wait("POOL", "dve", vm)
                vc = self.dma("POOL", "ogc", ogr[:, ch * 512:(ch + 1) * 512], b[:, :])
                o_ev[oc % 2] = vc
                self.qwait("DVE", vc)
                self.op("DVE", "dve", lambda e, b=b, ch=ch: e.tensor_sub(
                    oge[:, ch * 512:(ch + 1) * 512], b[:, :],
                    ogr[:, ch * 512:(ch + 1) * 512].bitcast(F32)))
            og_ev = self.cnt["dve"]
        v_sv = self.cnt["dve"]

        self.wait("SP", "mm", self.cnt["mm"])
        wp = self.i_w["wp"]
        wt = [self.sb("apt", [128, 1024], F32, self.C + 4096 * i) for i in range(2)]
        wr = [self.sb("apr", [128, 1024], F32R, self.C + 8192 + 4096 * i) for i in range(2)]
        we = [self.sb("ape", [128, 1024], F32R, self.C + 16384 + 4096 * i) for i in range(2)]
        xg = self.sb("axg", [128, DC * 512], F32, BIG + 65536)
        g2t = self.sb("ag2", [128, DC * 512], F32, BIG + 98304)
        obo = [self.sb("aout", [128, 512], F32, self.B + 2048 * i) for i in range(4)]
        self.wait("ACT", "mm", self.cnt["mm"])
        vxg = self.dma("ACT", "xg", self.fs(xg[:, :], 512),
                       self.fm(self.i_x[:, cq:cq + 512]))
        vg2 = self.dma("ACT", "xg", self.fs(g2t[:, :], 512),
                       self.fm(self.d_g2[:, cq:cq + 512]))
        self.wait("TE", "dve", v_sv)
        mmrec = {}
        it = 0
        evac_v = 0
        out_dma = [None, None, None, None]
        for og2 in range(2):
            for dc in range(DC):
                r = it % 2
                if it >= 2:
                    self.wait("SP", "mm", mmrec[it - 2])
                else:
                    self.wait("SP", "mm", self.cnt["mm"])
                vl = self.dma("SP", f"w{r}", wt[r][:, :],
                              wp[dc * 128:(dc + 1) * 128, og2 * 1024:(og2 + 1) * 1024])
                self.qwait("POOL", vl)
                if it >= 2:
                    self.wait("POOL", "mm", mmrec[it - 2])
                vcast = self.dma("POOL", f"wc{r}", wr[r][:, :], wt[r][:, :])
                self.qwait("DVE", vcast)
                vsub = self.op("DVE", "dve", lambda e, r=r: e.tensor_sub(
                    we[r][:, :], wt[r][:, :], wr[r][:, :].bitcast(F32)))
                self.wait("TE", "dve", vsub)
                if dc == 0 and og2 > 0:
                    self.wait("TE", "dve", evac_v)
                for oc in range(8):
                    pb = self.P[oc]
                    lr = wr[r][:, oc * 128:(oc + 1) * 128]
                    le = we[r][:, oc * 128:(oc + 1) * 128]
                    mr = ogr[:, dc * 512:(dc + 1) * 512]
                    me = oge[:, dc * 512:(dc + 1) * 512]
                    stt = (dc == 0)
                    stp = (dc == DC - 1)
                    last = oc == 7
                    self.op("TE", "mm", lambda e, pb=pb, l=lr, m=mr, s=stt: e.matmul(
                        pb[:, :], l, m, start=s, stop=False), inc=False)
                    self.op("TE", "mm", lambda e, pb=pb, l=lr, m=me: e.matmul(
                        pb[:, :], l, m, start=False, stop=False), inc=False)
                    self.op("TE", "mm", lambda e, pb=pb, l=le, m=mr, p=stp: e.matmul(
                        pb[:, :], l, m, start=False, stop=p), inc=last)
                mmrec[it] = self.cnt["mm"]
                it += 1
            v_grp = self.cnt["mm"]
            self.wait("DVE", "mm", v_grp)
            self.qwait("DVE", vg2)
            for oc in range(8):
                ch = og2 * 8 + oc
                b = obo[oc % 4]
                if ch >= 4:
                    self.qwait("DVE", out_dma[oc % 4])
                vmul = self.op("DVE", "dve", lambda e, pb=self.P[oc], b=b, ch=ch: e.tensor_tensor(
                    b[:, :], pb[:, :], g2t[:, ch * 512:(ch + 1) * 512], ALU.mult),
                    inc=True)
                self.wait("DVE", "dve", vmul)
                vm = self.op("DVE", "dve", lambda e, b=b, ch=ch: e.tensor_tensor(
                    b[:, :], b[:, :], xg[:, ch * 512:(ch + 1) * 512], ALU.add))
                self.wait("ACT", "dve", vm)
                row0 = ch * 128
                vo = self.dma("ACT", f"o{oc % 4}",
                              self.o_out[row0:row0 + 128, cq:cq + 512], b[:, :])
                out_dma[oc % 4] = vo
            evac_v = self.cnt["dve"]

    def build(self):
        self.ph0()
        if self.st["p1"]:
            self.fence()
            self.ph1_slab(1)
            self.fence()
            self.ph1_slab(0)
        if self.st["ffn"]:
            self.fence()
            self.ph_ffn()
        if self.st["fwht"]:
            self.fence()
            self.ph_fwht()
        if self.st["w2"]:
            self.fence()
            self.ph_w2()
        if self.st["attn"]:
            self.fence()
            self.ph_attn(0)
            self.fence()
            self.ph_attn(1)
        self.fence()
        for s in ("o0", "o1", "o2", "o3"):
            self.qwait("ACT", self.qtotal(s))
            self.qwait("POOL", self.qtotal(s))
        with self.nc.Block() as block:
            @block.tensor
            def _(e):
                for f in self.ops["TE"]:
                    f(e)

            @block.scalar
            def _(e):
                for f in self.ops["ACT"]:
                    f(e)

            @block.vector
            def _(e):
                for f in self.ops["DVE"]:
                    f(e)

            @block.gpsimd
            def _(e):
                for f in self.ops["POOL"]:
                    f(e)

            @block.sync
            def _(e):
                for f in self.ops["SP"]:
                    f(e)
        self.ctx.close()
        return self.nc


def _hadamard(n):
    h = np.array([[1.0]], dtype=np.float32)
    while h.shape[0] < n:
        h = np.block([[h, h], [h, -h]])
    return h.astype(np.float32)


def _staging(core, x):
    b, h = core // 2, core % 2
    own = np.concatenate([np.arange((2 * m + h) * 128, (2 * m + h) * 128 + 128)
                          for m in range(8)])
    par = np.concatenate([np.arange((2 * m + 1 - h) * 128, (2 * m + 1 - h) * 128 + 128)
                          for m in range(8)])
    perm = np.concatenate([own, par])
    xT = np.ascontiguousarray(x[b][perm, :].T)
    gidx = perm
    masks = np.zeros((24, 128, 512), dtype=np.float32)
    row = 0
    for qi in (0, 1):
        gq = gidx[qi * 512: qi * 512 + 512]
        for cb in KBLIST[qi]:
            gk = gidx[cb * 128:(cb + 1) * 128]
            masks[row] = (gk[:, None] <= gq[None, :]).astype(np.float32)
            row += 1
    return xT, masks, own


def kernel(L0_attn_x, L0_attn_Wq, L0_attn_Wk, L0_attn_Wv, L0_attn_Wg, L0_attn_Wp,
           L0_ffn_W1, L0_ffn_W2, L0_ffn_W3):
    x = np.ascontiguousarray(np.asarray(L0_attn_x, dtype=np.float32))
    ws = {
        "wqT": np.ascontiguousarray(np.asarray(L0_attn_Wq).T),
        "wkT": np.ascontiguousarray(np.asarray(L0_attn_Wk).T),
        "wvT": np.ascontiguousarray(np.asarray(L0_attn_Wv).T),
        "wgT": np.ascontiguousarray(np.asarray(L0_attn_Wg).T),
        "wpT": np.ascontiguousarray(np.asarray(L0_attn_Wp).T),
        "w1T": np.ascontiguousarray(np.asarray(L0_ffn_W1).T),
        "w3T": np.ascontiguousarray(np.asarray(L0_ffn_W3).T),
        "w2T": np.ascontiguousarray(np.asarray(L0_ffn_W2).T),
    }
    had = _hadamard(128)
    in_maps = []
    owns = []
    for c in range(8):
        xT, masks, own = _staging(c, x)
        owns.append(own)
        m = {"xT": xT, "masks": masks, "had128": had}
        m.update(ws)
        in_maps.append(m)

    nc = Builder().build()
    res = run_bass_kernel_spmd(nc, in_maps, core_ids=list(range(8)))
    out = np.empty((4, 2048, 2048), dtype=np.float32)
    for c in range(8):
        b = c // 2
        out[b][owns[c], :] = res.results[c]["outT"].T
    return out


# version 07
# speedup vs baseline: 1.4475x; 1.1156x over previous
from contextlib import ExitStack

import numpy as np
import concourse.bass as bass
import concourse.mybir as mybir
from concourse.bass_utils import run_bass_kernel_spmd

F32 = mybir.dt.float32
F32R = mybir.dt.float32r
AF = mybir.ActivationFunctionType
ALU = mybir.AluOpType

D = 2048
DC = 16
TS = 1024
HIDDEN = 8192
HC = 64
NKB = {0: 8, 1: 16}
KBLIST = {0: [0, 1, 2, 3, 8, 9, 10, 11], 1: list(range(16))}
LN_EPS = 1e-5
RMS_EPS = float(np.finfo(np.float32).eps)
FWHT_SCALE = float(1.0 / np.sqrt(HIDDEN))

ENGINES = ("TE", "ACT", "DVE", "POOL", "SP")

STAGES = {"p1": True, "ffn": True, "fwht": True, "w2": True, "attn": True,
          "debug": False}


def ceil32(x):
    return (x + 31) & ~31


class Builder:
    def __init__(self, stages=STAGES):
        self.st = dict(stages)
        nc = bass.Bass("TRN2", target_bir_lowering=False)
        self.nc = nc
        self.ops = {e: [] for e in ENGINES}
        self.cnt = {}
        self.sem = {}
        self.ctx = ExitStack()
        for name in ("mm", "act", "dve", "pool", "sp"):
            self.sem[name] = self.ctx.enter_context(nc.semaphore("s_" + name))
            self.cnt[name] = 0
        self.qsem, self.qcnt, self.qeng = {}, {}, {}
        self._declare_dram()
        self._declare_sbuf()
        self._declare_psum()

    def inc(self, name, n=1):
        self.cnt[name] += n
        return self.cnt[name]

    def emit(self, eng, fn):
        self.ops[eng].append(fn)

    def wait(self, eng, sem, val):
        if val <= 0:
            return
        s = self.sem[sem]
        self.emit(eng, lambda e, s=s, v=val: e.wait_ge(s, v))

    def dma(self, eng, stream, out, in_):
        if stream not in self.qsem:
            self.qsem[stream] = self.ctx.enter_context(self.nc.semaphore("q_" + stream))
            self.qcnt[stream] = 0
            self.qeng[stream] = eng
        assert self.qeng[stream] == eng, (stream, eng)
        self.qcnt[stream] += 16
        v = self.qcnt[stream]
        s = self.qsem[stream]
        self.emit(eng, lambda e, o=out, i=in_, s=s: e.dma_start(out=o, in_=i).then_inc(s, 16))
        return (stream, v)

    def qwait(self, eng, tok):
        if tok is None:
            return
        stream, v = tok
        if v <= 0:
            return
        s = self.qsem[stream]
        self.emit(eng, lambda e, s=s, v=v: e.wait_ge(s, v))

    def qtotal(self, stream):
        return (stream, self.qcnt.get(stream, 0))

    @staticmethod
    def fm(ap2d):
        return ap2d.rearrange("(c p) t -> p c t", p=128)

    @staticmethod
    def fs(sb_ap, t):
        return sb_ap.rearrange("p (c t) -> p c t", t=t)

    def op(self, eng, esem, fn, inc=True):
        if inc:
            v = self.inc(esem)
            s = self.sem[esem]
            self.emit(eng, lambda e, f=fn, s=s: f(e).then_inc(s, 1))
            return v
        self.emit(eng, lambda e, f=fn: f(e))
        return self.cnt[esem]

    def fence(self, tag=""):
        vals = {}
        for e in ENGINES:
            for stream, owner in self.qeng.items():
                if owner == e:
                    self.qwait(e, self.qtotal(stream))
            es = {"TE": "mm", "ACT": "act", "DVE": "dve", "POOL": "pool", "SP": "sp"}[e]
            self.wait(e, es, self.cnt[es])
            v = self.inc(es)
            s = self.sem[es]
            self.emit(e, lambda en, s=s: en.nop().then_inc(s, 1))
            vals[e] = (es, v)
        for e in ENGINES:
            for o in ENGINES:
                if o != e:
                    sname, v = vals[o]
                    self.wait(e, sname, v)

    def _declare_dram(self):
        nc = self.nc
        t = nc.dram_tensor
        self.i_x = t("xT", [D, 2 * TS], F32, kind="ExternalInput").ap()
        self.i_w = {}
        for nm in ("wq", "wk", "wv", "wg", "wp"):
            self.i_w[nm] = t(nm + "T", [D, D], F32, kind="ExternalInput").ap()
        self.i_w["w1"] = t("w1T", [D, HIDDEN], F32, kind="ExternalInput").ap()
        self.i_w["w3"] = t("w3T", [D, HIDDEN], F32, kind="ExternalInput").ap()
        self.i_w["w2"] = t("w2T", [HIDDEN, D], F32, kind="ExternalInput").ap()
        self.i_masks = t("masks", [24, 128, 512], F32, kind="ExternalInput").ap()
        self.i_h128 = t("had128", [128, 128], F32, kind="ExternalInput").ap()

        kind = "ExternalOutput" if self.st["debug"] else "Internal"
        if self.st["debug"]:
            self.d_x2r = t("x2r_d", [D, TS], F32R, kind=kind).ap()
            self.d_x2e = t("x2e_d", [D, TS], F32R, kind=kind).ap()
            self.d_y1dbg = t("y1dbg_d", [128, HC * 256], F32, kind=kind).ap()
            self.d_ybf = t("ybf_d", [128, HC * 256], F32, kind=kind).ap()
        self.d_kr = t("kr_d", [D, 2 * TS], F32R, kind=kind).ap()
        self.d_ke = t("ke_d", [D, 2 * TS], F32R, kind=kind).ap()
        self.d_vr = t("vr_d", [2 * TS, D], F32R, kind=kind).ap()
        self.d_ve = t("ve_d", [2 * TS, D], F32R, kind=kind).ap()
        self.d_qr = t("qr_d", [D, TS], F32R, kind=kind).ap()
        self.d_qe = t("qe_d", [D, TS], F32R, kind=kind).ap()
        self.d_g = t("g_d", [D, TS], F32, kind=kind).ap()
        self.d_xn = t("xn_d", [D, TS], F32, kind=kind).ap()
        self.d_hr = t("hr_d", [HIDDEN, TS], F32R, kind=kind).ap()
        self.d_he = t("he_d", [HIDDEN, TS], F32R, kind=kind).ap()
        self.d_yr = t("yr_d", [HIDDEN, TS], F32R, kind=kind).ap()
        self.d_ye = t("ye_d", [HIDDEN, TS], F32R, kind=kind).ap()
        self.d_g2 = t("g2_d", [D, TS], F32, kind=kind).ap()
        self.o_out = t("outT", [D, TS], F32, kind="ExternalOutput").ap()

    def _declare_sbuf(self):
        nc = self.nc
        at = nc.alloc_sbuf_tensor_at
        base0 = nc.sbuf_base
        M = ceil32(base0)
        self.M = M
        self.A = M + 18432
        self.B = M + 38912
        self.C = M + 53248
        self.BIG = M + 77824
        end = self.BIG + 131072
        nc.alloc_sbuf_tensor("manual_map_reserved",
                             [128, end - base0], mybir.dt.uint8)
        self.ones_col = at("ones_col", [128, 1], F32, offset=M).ap()
        self.ones_row = at("ones_row", [1, 128], F32, offset=M + 32).ap()
        self.h128r = at("h128r", [128, 128], F32R, offset=M + 544).ap()
        self.h128f = at("h128f", [128, 128], F32, offset=M + 1056).ap()
        self.stat = [at(f"stat{i}", [1, 1024], F32, offset=M + 1600 + 4096 * i).ap()
                     for i in range(4)]
        self._tens = {}

    def sb(self, name, shape, dtype, off):
        key = (name, tuple(shape), dtype, off)
        if key not in self._tens:
            self._tens[key] = self.nc.alloc_sbuf_tensor_at(
                f"{name}_{off}", shape, dtype, offset=off).ap()
        return self._tens[key]

    def _declare_psum(self):
        self.P = [self.ctx.enter_context(
            self.nc.psum_tensor(f"pb{i}", [128, 512], F32)) for i in range(8)]

    def ph0(self):
        self.op("DVE", "dve", lambda e: e.memset(self.ones_col[:, :], 1.0), inc=False)
        self.op("DVE", "dve", lambda e: e.memset(self.ones_row[:, :], 1.0))
        v = self.dma("POOL", "ini", self.h128f[:, :], self.i_h128[:, :])
        self.qwait("POOL", v)
        vc = self.dma("POOL", "ini", self.h128r[:, :], self.h128f[:, :])
        self.qwait("TE", vc)
        self.wait("TE", "dve", self.cnt["dve"])

    def ph1_slab(self, s):
        c0 = s * TS
        own = (s == 0)
        BIG = self.BIG
        xn = self.sb("xn", [128, DC * TS], F32, BIG)
        xn2r = self.sb("xn2r", [128, DC * TS], F32R, BIG + 65536)
        xn2e = self.sb("xn2e", [128, DC * TS], F32R, BIG)
        sq = [self.sb("sqb", [128, 1024], F32, self.B + 4096 * i) for i in range(2)]
        bnc = [self.sb("lnb", [128, 1024], F32, self.B + 8192 + 4096 * i) for i in range(2)]

        src = self.fm(self.i_x[:, c0:c0 + TS])
        vx = self.dma("ACT", "xs", self.fs(xn[:, :], TS), src)

        self.qwait("TE", vx)
        self.qwait("ACT", vx)
        mm_done = []
        sq_done = [0, 0]
        for dc in range(DC):
            xc = xn[:, dc * TS:(dc + 1) * TS]
            sqb = sq[dc % 2]
            if dc >= 2:
                self.wait("ACT", "mm", sq_done[dc % 2])
            va = self.op("ACT", "act", lambda e, i=xc, o=sqb: e.activation(
                out=o[:, :], in_=i, func=AF.Square))
            for t in range(2):
                last = (dc == DC - 1)
                self.op("TE", "mm", lambda e, i=xc, t=t, dc=dc, last=last: e.matmul(
                    self.P[t][0:1, :], self.ones_col[:, :], i[:, t * 512:(t + 1) * 512],
                    start=(dc == 0), stop=last), inc=last)
            self.wait("TE", "act", va)
            for t in range(2):
                last = (dc == DC - 1)
                v = self.op("TE", "mm", lambda e, i=sqb, t=t, dc=dc, last=last: e.matmul(
                    self.P[2 + t][0:1, :], self.ones_col[:, :], i[:, t * 512:(t + 1) * 512],
                    start=(dc == 0), stop=last), inc=True)
                sq_done[dc % 2] = v
        v_stats = self.cnt["mm"]

        st0, st1, st2, st3 = self.stat
        self.wait("DVE", "mm", v_stats)

        def R(fn):
            self.wait("DVE", "dve", self.cnt["dve"])
            return self.op("DVE", "dve", fn, inc=True)
        for t in range(2):
            R(lambda e, t=t: e.tensor_scalar_mul(st0[:, t * 512:(t + 1) * 512],
                                                 self.P[t][0:1, :], 1.0 / D))
            R(lambda e, t=t: e.tensor_scalar_mul(st1[:, t * 512:(t + 1) * 512],
                                                 self.P[2 + t][0:1, :], 1.0 / D))
        R(lambda e: e.tensor_tensor(st2[:, :], st0[:, :], st0[:, :], ALU.mult))
        R(lambda e: e.tensor_sub(st1[:, :], st1[:, :], st2[:, :]))
        self._rsqrt_row(st1, st2, st3, LN_EPS)
        self.wait("DVE", "dve", self.cnt["dve"])
        v_rows = self.op("DVE", "dve", lambda e: e.tensor_tensor(
            st3[:, :], st0[:, :], st2[:, :], ALU.mult))

        self.wait("TE", "dve", v_rows)
        for t in range(2):
            self.op("TE", "mm", lambda e, t=t: e.matmul(
                self.P[4 + t][:, :], self.ones_row[:, :], st2[:, t * 512:(t + 1) * 512],
                start=True, stop=True), inc=False)
            self.op("TE", "mm", lambda e, t=t: e.matmul(
                self.P[6 + t][:, :], self.ones_row[:, :], st3[:, t * 512:(t + 1) * 512],
                start=True, stop=True), inc=(t == 1))
        v_bcast = self.cnt["mm"]

        self.wait("DVE", "mm", v_bcast)
        xn_chunk_done = []
        for dc in range(DC):
            xc = xn[:, dc * TS:(dc + 1) * TS]
            b = bnc[dc % 2]
            for t in range(2):
                R(lambda e, i=xc, b=b, t=t: e.tensor_tensor(
                    b[:, t * 512:(t + 1) * 512], i[:, t * 512:(t + 1) * 512],
                    self.P[4 + t][:, :], ALU.mult))
            self.wait("DVE", "dve", self.cnt["dve"])
            v = self.op("DVE", "dve", lambda e, i=xc, b=b: e.tensor_tensor(
                i[:, 0:512], b[:, 0:512], self.P[6][:, :], ALU.subtract), inc=True)
            self.wait("DVE", "dve", self.cnt["dve"])
            v = self.op("DVE", "dve", lambda e, i=xc, b=b: e.tensor_tensor(
                i[:, 512:1024], b[:, 512:1024], self.P[7][:, :], ALU.subtract))
            xn_chunk_done.append(v)
        if own:
            self.wait("POOL", "dve", xn_chunk_done[-1])
            self.v_xn_dma = self.dma("POOL", "xnst", self.fm(self.d_xn[:, :]),
                                     self.fs(xn[:, :], TS))

        sq_done = [0, 0]
        for dc in range(DC):
            xc = xn[:, dc * TS:(dc + 1) * TS]
            sqb = sq[dc % 2]
            self.wait("ACT", "dve", xn_chunk_done[dc])
            if dc >= 2:
                self.wait("ACT", "mm", sq_done[dc % 2])
            va = self.op("ACT", "act", lambda e, i=xc, o=sqb: e.activation(
                out=o[:, :], in_=i, func=AF.Square))
            self.wait("TE", "act", va)
            for t in range(2):
                last = (dc == DC - 1)
                v = self.op("TE", "mm", lambda e, i=sqb, t=t, dc=dc, last=last: e.matmul(
                    self.P[t][0:1, :], self.ones_col[:, :], i[:, t * 512:(t + 1) * 512],
                    start=(dc == 0), stop=last), inc=True)
                sq_done[dc % 2] = v
        v_stats2 = self.cnt["mm"]
        self.wait("DVE", "mm", v_stats2)
        for t in range(2):
            R(lambda e, t=t: e.tensor_scalar_mul(st0[:, t * 512:(t + 1) * 512],
                                                 self.P[t][0:1, :], 1.0 / D))
        self._rsqrt_row(st0, st2, st3, RMS_EPS)
        v_rows2 = self.cnt["dve"]
        self.wait("TE", "dve", v_rows2)
        for t in range(2):
            self.op("TE", "mm", lambda e, t=t: e.matmul(
                self.P[4 + t][:, :], self.ones_row[:, :], st2[:, t * 512:(t + 1) * 512],
                start=True, stop=True), inc=(t == 1))
        v_bc2 = self.cnt["mm"]

        self.wait("DVE", "mm", v_bc2)
        for dc in range(DC):
            xc = xn[:, dc * TS:(dc + 1) * TS]
            b = bnc[dc % 2]
            if dc >= 2:
                self.qwait("DVE", self._xn2cast[dc % 2])
            for t in range(2):
                R(lambda e, i=xc, b=b, t=t: e.tensor_tensor(
                    b[:, t * 512:(t + 1) * 512], i[:, t * 512:(t + 1) * 512],
                    self.P[4 + t][:, :], ALU.mult))
            vb = self.cnt["dve"]
            self.wait("POOL", "dve", vb)
            if own and dc == 0:
                pass
            vc = self.dma("POOL", "xc", xn2r[:, dc * TS:(dc + 1) * TS], b[:, :])
            if not hasattr(self, "_xn2cast"):
                self._xn2cast = [None, None]
            self._xn2cast[dc % 2] = vc
            self.qwait("DVE", vc)
            if own:
                self.qwait("DVE", self.v_xn_dma)
            self.op("DVE", "dve", lambda e, b=b, dc=dc: e.tensor_sub(
                xn2e[:, dc * TS:(dc + 1) * TS], b[:, :],
                xn2r[:, dc * TS:(dc + 1) * TS].bitcast(F32)))
        v_split = self.cnt["dve"]

        if self.st["debug"] and own:
            self.wait("POOL", "dve", v_split)
            self.dma("POOL", "dbg", self.fm(self.d_x2r[:, :]), self.fs(xn2r[:, :], TS))
            self.dma("POOL", "dbg", self.fm(self.d_x2e[:, :]), self.fs(xn2e[:, :], TS))

        self.fence()
        self.proj_featmaj("wk", self.i_w["wk"], AF.Silu, self.d_kr, self.d_ke,
                          xn2r, xn2e, out_cols=(c0, TS), v_ready=v_split)
        self.fence()
        self.proj_tokmaj(xn2r, xn2e, slab_off=c0, v_ready=v_split)
        if own:
            self.fence()
            self.proj_featmaj("wq", self.i_w["wq"], AF.Silu, self.d_qr, self.d_qe,
                              xn2r, xn2e, out_cols=(0, TS), v_ready=v_split)
            self.fence()
            self.proj_featmaj("wg", self.i_w["wg"], AF.Sigmoid, self.d_g, None,
                              xn2r, xn2e, out_cols=(0, TS), v_ready=v_split)

    def _rsqrt_row(self, vrow, out, tmp, eps):
        def R(fn):
            self.wait("DVE", "dve", self.cnt["dve"])
            return self.op("DVE", "dve", fn, inc=True)

        def A(fn):
            self.wait("ACT", "act", self.cnt["act"])
            return self.op("ACT", "act", fn, inc=True)
        R(lambda e: e.tensor_scalar_add(vrow[:, :], vrow[:, :], eps))
        self.wait("ACT", "dve", self.cnt["dve"])
        va = self.op("ACT", "act", lambda e: e.activation(out=tmp[:, :], in_=vrow[:, :],
                                                          func=AF.Sqrt))
        self.wait("DVE", "act", va)
        R(lambda e: e.reciprocal(out[:, :], tmp[:, :]))
        for i in range(2):
            R(lambda e: e.tensor_tensor(tmp[:, :], out[:, :], out[:, :], ALU.mult))
            R(lambda e: e.tensor_tensor(tmp[:, :], tmp[:, :], vrow[:, :], ALU.mult))
            R(lambda e: e.tensor_scalar(tmp[:, :], tmp[:, :], -0.5, 1.5,
                                        ALU.mult, ALU.add))
            self.wait("DVE", "dve", self.cnt["dve"])
            self.op("DVE", "dve", lambda e: e.tensor_tensor(
                out[:, :], out[:, :], tmp[:, :], ALU.mult), inc=True)

    def proj_featmaj(self, nm, w, act, dr, de, xr, xe, out_cols, v_ready,
                    n_out=D):
        c0, TS_ = out_cols
        wtmp = [self.sb(nm + "wt", [128, 512], F32, self.A + 2048 * i) for i in range(3)]
        wr = [self.sb(nm + "wr", [128, 512], F32R, self.A + 6144 + 2048 * i) for i in range(3)]
        we = [self.sb(nm + "we", [128, 512], F32R, self.A + 12288 + 2048 * i) for i in range(3)]
        rb = [self.sb(nm + "rb", [128, 512], F32R, self.B + 2048 * i) for i in range(2)]
        ebw = self.sb(nm + "ebw", [128, 8 * 512], F32, self.C + 2048)
        n_og = n_out // 512
        self.wait("TE", "dve", v_ready)
        mmrec = {}
        it = 0
        evac_v = 0
        st = {"prev_sub": None, "latest": None, "last_cast": None}
        pending = []

        def mkstep(og, j, va):
            def step():
                oc, t = divmod(j, 2)
                bsl = ebw[:, j * 512:(j + 1) * 512]
                rbb = rb[j % 2]
                row0 = (og * 4 + oc) * 128
                dst_cols = (c0 + t * 512, c0 + t * 512 + 512)
                self.wait("POOL", "act", va)
                if de is None:
                    vo = self.dma("POOL", "ps0",
                                  dr[row0:row0 + 128, dst_cols[0]:dst_cols[1]], bsl)
                    st["latest"] = vo
                    st["last_cast"] = vo
                    return
                if st["prev_sub"] is not None:
                    self.wait("POOL", "dve", st["prev_sub"])
                if st["latest"] is not None:
                    self.qwait("POOL", st["latest"])
                vc = self.dma("POOL", "pc0", rbb[:, :], bsl)
                st["last_cast"] = vc
                self.qwait("DVE", vc)
                vs = self.op("DVE", "dve", lambda e, b=bsl, rbb=rbb: e.tensor_sub(
                    b, b, rbb[:, :].bitcast(F32)))
                st["prev_sub"] = vs
                self.wait("POOL", "dve", vs)
                self.dma("POOL", "ps0",
                         dr[row0:row0 + 128, dst_cols[0]:dst_cols[1]], rbb[:, :])
                vo = self.dma("POOL", "ps0",
                              de[row0:row0 + 128, dst_cols[0]:dst_cols[1]],
                              bsl.bitcast(F32R))
                st["latest"] = vo
            return step

        for og in range(n_og):
            for dc in range(DC):
                r = it % 3
                if it >= 3:
                    self.wait("SP", "mm", mmrec[(it - 3)])
                else:
                    self.wait("SP", "mm", self.cnt["mm"])
                vload = self.dma("SP", f"w{r}", wtmp[r][:, :],
                                 w[dc * 128:(dc + 1) * 128, og * 512:(og + 1) * 512])
                self.qwait("POOL", vload)
                if it >= 3:
                    self.wait("POOL", "mm", mmrec[(it - 3)])
                vcast = self.dma("POOL", f"wc{r}", wr[r][:, :], wtmp[r][:, :])
                self.qwait("DVE", vcast)
                vsub = self.op("DVE", "dve", lambda e, r=r: e.tensor_sub(
                    we[r][:, :], wtmp[r][:, :], wr[r][:, :].bitcast(F32)))
                self.wait("TE", "dve", vsub)
                if og > 0 and dc == 0:
                    self.wait("TE", "act", evac_v)
                for oc in range(4):
                    for t in range(2):
                        pb = self.P[oc * 2 + t]
                        mv_r = xr[:, dc * TS_ + t * 512: dc * TS_ + t * 512 + 512]
                        mv_e = xe[:, dc * TS_ + t * 512: dc * TS_ + t * 512 + 512]
                        stt = (dc == 0)
                        stp = (dc == DC - 1)
                        lr = wr[r][:, oc * 128:(oc + 1) * 128]
                        le = we[r][:, oc * 128:(oc + 1) * 128]
                        self.op("TE", "mm", lambda e, pb=pb, lr=lr, mv=mv_r, s=stt: e.matmul(
                            pb[:, :], lr, mv, start=s, stop=False), inc=False)
                        self.op("TE", "mm", lambda e, pb=pb, lr=lr, mv=mv_e: e.matmul(
                            pb[:, :], lr, mv, start=False, stop=False), inc=False)
                        last = oc == 3 and t == 1
                        self.op("TE", "mm", lambda e, pb=pb, le=le, mv=mv_r, p=stp: e.matmul(
                            pb[:, :], le, mv, start=False, stop=p), inc=last)
                mmrec[it] = self.cnt["mm"]
                it += 1
                if dc % 2 == 1 and pending:
                    pending.pop(0)()
            while pending:
                pending.pop(0)()
            v_grp = self.cnt["mm"]
            self.wait("ACT", "mm", v_grp)
            if st["last_cast"] is not None:
                self.qwait("ACT", st["last_cast"])
            if st["latest"] is not None:
                self.qwait("ACT", st["latest"])
            for j in range(8):
                self.op("ACT", "act", lambda e, pb=self.P[j], j=j, f=act: e.activation(
                    out=ebw[:, j * 512:(j + 1) * 512], in_=pb[:, :], func=f),
                    inc=(j == 7))
            evac_v = self.cnt["act"]
            pending = [mkstep(og, j, evac_v) for j in range(8)]
        while pending:
            pending.pop(0)()

    def proj_tokmaj(self, xr, xe, slab_off, v_ready):
        nm = "wv"
        w = self.i_w["wv"]
        wtmp = [self.sb(nm + "wt", [128, 512], F32, self.A + 2048 * i) for i in range(3)]
        wr = [self.sb(nm + "wr", [128, 512], F32R, self.A + 6144 + 2048 * i) for i in range(3)]
        we = [self.sb(nm + "we", [128, 512], F32R, self.A + 12288 + 2048 * i) for i in range(3)]
        rb = [self.sb(nm + "rb", [128, 512], F32R, self.B + 2048 * i) for i in range(2)]
        ebw = self.sb(nm + "ebw", [128, 8 * 512], F32, self.C + 2048)
        self.wait("TE", "dve", v_ready)
        it = 0
        mmrec = {}
        evac_v = 0
        st = {"prev_sub": None, "latest": None, "last_cast": None}
        pending = []

        def mkstep(dt, tb, va):
            def step():
                bsl = ebw[:, tb * 512:(tb + 1) * 512]
                rbb = rb[tb % 2]
                row0 = slab_off + tb * 128
                self.wait("POOL", "act", va)
                if st["prev_sub"] is not None:
                    self.wait("POOL", "dve", st["prev_sub"])
                if st["latest"] is not None:
                    self.qwait("POOL", st["latest"])
                vc = self.dma("POOL", "pc0", rbb[:, :], bsl)
                st["last_cast"] = vc
                self.qwait("DVE", vc)
                vs = self.op("DVE", "dve", lambda e, b=bsl, rbb=rbb: e.tensor_sub(
                    b, b, rbb[:, :].bitcast(F32)))
                st["prev_sub"] = vs
                self.wait("POOL", "dve", vs)
                self.dma("POOL", "ps0",
                         self.d_vr[row0:row0 + 128, dt * 512:(dt + 1) * 512], rbb[:, :])
                vo = self.dma("POOL", "ps0",
                              self.d_ve[row0:row0 + 128, dt * 512:(dt + 1) * 512],
                              bsl.bitcast(F32R))
                st["latest"] = vo
            return step

        for dt in range(4):
            for dc in range(DC):
                r = it % 3
                if it >= 3:
                    self.wait("SP", "mm", mmrec[it - 3])
                else:
                    self.wait("SP", "mm", self.cnt["mm"])
                vload = self.dma("SP", f"w{r}", wtmp[r][:, :],
                                 w[dc * 128:(dc + 1) * 128, dt * 512:(dt + 1) * 512])
                self.qwait("POOL", vload)
                if it >= 3:
                    self.wait("POOL", "mm", mmrec[it - 3])
                vcast = self.dma("POOL", f"wc{r}", wr[r][:, :], wtmp[r][:, :])
                self.qwait("DVE", vcast)
                vsub = self.op("DVE", "dve", lambda e, r=r: e.tensor_sub(
                    we[r][:, :], wtmp[r][:, :], wr[r][:, :].bitcast(F32)))
                self.wait("TE", "dve", vsub)
                if dt > 0 and dc == 0:
                    self.wait("TE", "act", evac_v)
                for tb in range(8):
                    pb = self.P[tb]
                    st_r = xr[:, dc * TS + tb * 128: dc * TS + tb * 128 + 128]
                    st_e = xe[:, dc * TS + tb * 128: dc * TS + tb * 128 + 128]
                    stt = (dc == 0)
                    stp = (dc == DC - 1)
                    last = tb == 7
                    self.op("TE", "mm", lambda e, pb=pb, l=st_r, m=wr[r], s=stt: e.matmul(
                        pb[:, :], l, m[:, :], start=s, stop=False), inc=False)
                    self.op("TE", "mm", lambda e, pb=pb, l=st_r, m=we[r]: e.matmul(
                        pb[:, :], l, m[:, :], start=False, stop=False), inc=False)
                    self.op("TE", "mm", lambda e, pb=pb, l=st_e, m=wr[r], p=stp: e.matmul(
                        pb[:, :], l, m[:, :], start=False, stop=p), inc=last)
                mmrec[it] = self.cnt["mm"]
                it += 1
                if dc % 2 == 1 and pending:
                    pending.pop(0)()
            while pending:
                pending.pop(0)()
            v_grp = self.cnt["mm"]
            self.wait("ACT", "mm", v_grp)
            if st["last_cast"] is not None:
                self.qwait("ACT", st["last_cast"])
            if st["latest"] is not None:
                self.qwait("ACT", st["latest"])
            for tb in range(8):
                self.op("ACT", "act", lambda e, pb=self.P[tb], tb=tb: e.activation(
                    out=ebw[:, tb * 512:(tb + 1) * 512], in_=pb[:, :], func=AF.Silu),
                    inc=(tb == 7))
            evac_v = self.cnt["act"]
            pending = [mkstep(dt, tb, evac_v) for tb in range(8)]
        while pending:
            pending.pop(0)()

    def ph_ffn(self):
        BIG = self.BIG
        xn = self.sb("fxn", [128, DC * TS], F32, BIG)
        xnr = self.sb("fxnr", [128, DC * TS], F32R, BIG + 65536)
        xne = self.sb("fxne", [128, DC * TS], F32R, BIG)

        src = self.fm(self.d_xn[:, :])
        vx = self.dma("POOL", "xnst", self.fs(xn[:, :], TS), src)
        self.qwait("POOL", vx)
        cast_v = []
        for dc in range(DC):
            vc = self.dma("POOL", "fc", xnr[:, dc * TS:(dc + 1) * TS],
                          xn[:, dc * TS:(dc + 1) * TS])
            cast_v.append(vc)
        self.qwait("DVE", cast_v[-1])
        for dc in range(DC):
            self.op("DVE", "dve", lambda e, dc=dc: e.tensor_sub(
                xne[:, dc * TS:(dc + 1) * TS], xn[:, dc * TS:(dc + 1) * TS],
                xnr[:, dc * TS:(dc + 1) * TS].bitcast(F32)))
        v_split = self.cnt["dve"]

        w1, w3 = self.i_w["w1"], self.i_w["w3"]
        wt1 = [self.sb("f1t", [128, 512], F32, self.A + 2048 * i) for i in range(3)]
        wr1 = [self.sb("f1r", [128, 512], F32R, self.A + 6144 + 2048 * i) for i in range(3)]
        we1 = [self.sb("f1e", [128, 512], F32R, self.A + 12288 + 2048 * i) for i in range(3)]
        ug = self.sb("fug", [128, 8 * 512], F32, self.C)
        rb = [self.sb("fhr", [128, 512], F32R, self.C + 16384 + 2048 * i) for i in range(2)]
        self.wait("TE", "dve", v_split)
        it = 0
        mmrec = {}
        ug_evac_v = 0
        mul_v = 0
        st = {"prev_sub": None, "latest": None, "last_cast": None}
        pending = []

        def mkstep(hg, j, vmul):
            def step():
                hc, t = divmod(j, 2)
                bsl = ug[:, j * 512:(j + 1) * 512]
                rbb = rb[j % 2]
                row0 = (hg * 4 + hc) * 128
                if st["prev_sub"] is not None:
                    self.wait("POOL", "dve", st["prev_sub"])
                self.wait("POOL", "dve", vmul)
                if st["latest"] is not None:
                    self.qwait("POOL", st["latest"])
                vc = self.dma("POOL", "pc0", rbb[:, :], bsl)
                st["last_cast"] = vc
                self.qwait("DVE", vc)
                vs = self.op("DVE", "dve", lambda e, b=bsl, rbb=rbb: e.tensor_sub(
                    b, b, rbb[:, :].bitcast(F32)))
                st["prev_sub"] = vs
                self.wait("POOL", "dve", vs)
                self.dma("POOL", "ps0",
                         self.d_hr[row0:row0 + 128, t * 512:(t + 1) * 512], rbb[:, :])
                vo = self.dma("POOL", "ps0",
                              self.d_he[row0:row0 + 128, t * 512:(t + 1) * 512],
                              bsl.bitcast(F32R))
                st["latest"] = vo
            return step

        for hg in range(16):
            for wi, w in ((0, w1), (1, w3)):
                for dc in range(DC):
                    r = it % 3
                    if it >= 3:
                        self.wait("SP", "mm", mmrec[it - 3])
                    else:
                        self.wait("SP", "mm", self.cnt["mm"])
                    vload = self.dma("SP", f"w{r}", wt1[r][:, :],
                                     w[dc * 128:(dc + 1) * 128, hg * 512:(hg + 1) * 512])
                    self.qwait("POOL", vload)
                    if it >= 3:
                        self.wait("POOL", "mm", mmrec[it - 3])
                    vcast = self.dma("POOL", f"wc{r}", wr1[r][:, :], wt1[r][:, :])
                    self.qwait("DVE", vcast)
                    vsub = self.op("DVE", "dve", lambda e, r=r: e.tensor_sub(
                        we1[r][:, :], wt1[r][:, :], wr1[r][:, :].bitcast(F32)))
                    self.wait("TE", "dve", vsub)
                    if dc == 0 and (hg > 0 or wi > 0):
                        if wi == 0:
                            self.wait("TE", "dve", mul_v)
                        else:
                            self.wait("TE", "act", ug_evac_v)
                    for hc in range(4):
                        for t in range(2):
                            pb = self.P[hc * 2 + t]
                            mv_r = xnr[:, dc * TS + t * 512: dc * TS + t * 512 + 512]
                            mv_e = xne[:, dc * TS + t * 512: dc * TS + t * 512 + 512]
                            lr = wr1[r][:, hc * 128:(hc + 1) * 128]
                            le = we1[r][:, hc * 128:(hc + 1) * 128]
                            stt = (dc == 0)
                            stp = (dc == DC - 1)
                            last = hc == 3 and t == 1
                            self.op("TE", "mm", lambda e, pb=pb, l=lr, m=mv_r, s=stt: e.matmul(
                                pb[:, :], l, m, start=s, stop=False), inc=False)
                            self.op("TE", "mm", lambda e, pb=pb, l=lr, m=mv_e: e.matmul(
                                pb[:, :], l, m, start=False, stop=False), inc=False)
                            self.op("TE", "mm", lambda e, pb=pb, l=le, m=mv_r, p=stp: e.matmul(
                                pb[:, :], l, m, start=False, stop=p), inc=last)
                    mmrec[it] = self.cnt["mm"]
                    it += 1
                    if dc % 2 == 1 and pending:
                        pending.pop(0)()
                v_grp = self.cnt["mm"]
                if wi == 0:
                    while pending:
                        pending.pop(0)()
                    self.wait("ACT", "mm", v_grp)
                    if st["last_cast"] is not None:
                        self.qwait("ACT", st["last_cast"])
                    if st["latest"] is not None:
                        self.qwait("ACT", st["latest"])
                    for j in range(8):
                        self.op("ACT", "act", lambda e, pb=self.P[j], j=j: e.activation(
                            out=ug[:, j * 512:(j + 1) * 512], in_=pb[:, :], func=AF.Silu),
                            inc=(j == 7))
                    ug_evac_v = self.cnt["act"]
                else:
                    self.wait("DVE", "mm", v_grp)
                    self.wait("DVE", "act", ug_evac_v)
                    vmuls = []
                    for j in range(8):
                        self.wait("DVE", "dve", self.cnt["dve"])
                        vm = self.op("DVE", "dve", lambda e, pb=self.P[j], j=j: e.tensor_tensor(
                            ug[:, j * 512:(j + 1) * 512], ug[:, j * 512:(j + 1) * 512],
                            pb[:, :], ALU.mult), inc=True)
                        vmuls.append(vm)
                    mul_v = self.cnt["dve"]
                    pending = [mkstep(hg, j, vmuls[j]) for j in range(8)]
        while pending:
            pending.pop(0)()

    def ph_fwht(self):
        TT = 256
        for tt in range(4):
            self._fwht_tile(tt, TT)

    def _fwht_tile(self, tt, TT):
        BIG = self.BIG
        a = self.sb("wa", [128, HC * TT], F32, BIG)
        bq = self.sb("wb", [128, HC * TT], F32, BIG + 65536)
        bqr = self.sb("wbr", [128, HC * TT], F32R, BIG + 65536)
        hcr = [self.sb("whr", [128, 8 * TT], F32R, self.A + 8192 * i) for i in range(2)]
        hce = [self.sb("whe", [128, 8 * TT], F32R, self.C + 8192 * i) for i in range(2)]

        cols = (tt * TT, (tt + 1) * TT)
        self.qwait("ACT", self.qtotal("ys"))
        self.qwait("ACT", self.qtotal("dbg"))
        self.wait("TE", "act", self.cnt["act"])
        ev = {}
        ld = {}
        for g in range(8):
            r = g % 2
            if g >= 2:
                self.wait("SP", "mm", ld[g - 2])
            else:
                self.wait("SP", "mm", self.cnt["mm"])
            rows = slice(g * 1024, (g + 1) * 1024)
            v1 = self.dma("SP", f"f{r}", self.fs(hcr[r][:, :], TT),
                          self.fm(self.d_hr[rows, cols[0]:cols[1]]))
            v2 = self.dma("SP", f"f{r}", self.fs(hce[r][:, :], TT),
                          self.fm(self.d_he[rows, cols[0]:cols[1]]))
            self.qwait("TE", v2)
            for k in range(8):
                c = g * 8 + k
                pb = self.P[c % 8]
                if c >= 8:
                    self.wait("TE", "act", ev[c - 8])
                self.op("TE", "mm", lambda e, pb=pb, r=r, k=k: e.matmul(
                    pb[:, 0:TT], self.h128r[:, :], hcr[r][:, k * TT:(k + 1) * TT],
                    start=True, stop=False), inc=False)
                v = self.op("TE", "mm", lambda e, pb=pb, r=r, k=k: e.matmul(
                    pb[:, 0:TT], self.h128r[:, :], hce[r][:, k * TT:(k + 1) * TT],
                    start=False, stop=True))
                self.wait("ACT", "mm", v)
                ev[c] = self.op("ACT", "act", lambda e, pb=pb, c=c: e.activation(
                    out=a[:, c * TT:(c + 1) * TT], in_=pb[:, 0:TT], func=AF.Copy,
                    scale=FWHT_SCALE))
            ld[g] = self.cnt["mm"]
        v_y = self.cnt["act"]
        self.wait("DVE", "act", v_y)
        self.wait("POOL", "act", v_y)
        src_, dst_ = a, bq
        for lev in range(6):
            stq = 1 << lev
            span = stq * TT
            L2 = (2 * span) // 3 // 32 * 32
            def mk(which, src=src_, dst=dst_, span=span, L2=L2):
                def f(e):
                    o = dst[:, :].rearrange("p (g two s) -> p g two s", two=2, s=span)
                    i0 = src[:, :].rearrange("p (g two s) -> p g two s", two=2, s=span)
                    if which == "add":
                        return e.tensor_tensor(o[:, :, 0, :], i0[:, :, 0, :],
                                               i0[:, :, 1, :], ALU.add)
                    if which == "sub_pool":
                        return e.tensor_tensor(o[:, :, 1, 0:L2], i0[:, :, 0, 0:L2],
                                               i0[:, :, 1, 0:L2], ALU.subtract)
                    return e.tensor_tensor(o[:, :, 1, L2:span], i0[:, :, 0, L2:span],
                                           i0[:, :, 1, L2:span], ALU.subtract)
                return f
            self.wait("DVE", "dve", self.cnt["dve"])
            self.wait("POOL", "pool", self.cnt["pool"])
            v0 = self.op("DVE", "dve", mk("add"), inc=False)
            v0 = self.op("DVE", "dve", mk("sub_dve"))
            v1 = self.op("POOL", "pool", mk("sub_pool"))
            self.wait("DVE", "pool", v1)
            self.wait("POOL", "dve", v0)
            src_, dst_ = dst_, src_
        v_res_dve = self.cnt["dve"]
        v_res_pool = self.cnt["pool"]
        self.wait("ACT", "dve", v_res_dve)
        self.wait("ACT", "pool", v_res_pool)
        vc = self.op("ACT", "act", lambda e: e.activation(
            out=bqr[:, :], in_=a[:, :], func=AF.Copy))
        self.wait("DVE", "act", vc)
        vs = self.op("DVE", "dve", lambda e: e.tensor_sub(
            a[:, :], a[:, :], bqr[:, :].bitcast(F32)))
        self.wait("POOL", "dve", vs)
        if self.st["debug"] and tt == 0:
            vd = self.dma("POOL", "dbg", self.d_ybf[:, :HC * TT],
                          a[:, :])
            self.qwait("POOL", vd)
        self.dma("POOL", "ys", self.fm(self.d_yr[:, cols[0]:cols[1]]),
                 self.fs(bqr[:, :], TT))
        self.dma("POOL", "ys", self.fm(self.d_ye[:, cols[0]:cols[1]]),
                 self.fs(a[:, :].bitcast(F32R), TT))

    def ph_w2(self):
        w2 = self.i_w["w2"]
        wt = [self.sb("w2t", [128, 1024], F32, self.C + 4096 * i) for i in range(3)]
        wr = [self.sb("w2r", [128, 1024], F32R, self.C + 12288 + 4096 * i) for i in range(3)]
        we = [self.sb("w2e", [128, 1024], F32R, self.B + 4096 * i) for i in range(3)]
        yr = [self.sb("w2yr", [128, 512], F32R, self.A + 2048 * i) for i in range(3)]
        ye = [self.sb("w2ye", [128, 512], F32R, self.A + 6144 + 2048 * i) for i in range(3)]
        g2w = self.sb("w2g", [128, 8 * 512], F32, self.BIG)
        mmrec = {}
        it = 0
        evac_v = 0
        st = {"latest": None}
        pending = []

        def mkstep(t, og, oc, va):
            def step():
                row0 = (og * 8 + oc) * 128
                self.wait("POOL", "act", va)
                vo = self.dma("POOL", "gs0",
                              self.d_g2[row0:row0 + 128, t * 512:(t + 1) * 512],
                              g2w[:, oc * 512:(oc + 1) * 512])
                st["latest"] = vo
            return step

        for t in range(2):
            for og in range(2):
                for hc in range(HC):
                    r = it % 3
                    if it >= 3:
                        self.wait("SP", "mm", mmrec[it - 3])
                    else:
                        self.wait("SP", "mm", self.cnt["mm"])
                    vl = self.dma("SP", f"w{r}", wt[r][:, :],
                                  w2[hc * 128:(hc + 1) * 128, og * 1024:(og + 1) * 1024])
                    vy1 = self.dma("SP", f"w{r}", yr[r][:, :],
                                   self.d_yr[hc * 128:(hc + 1) * 128, t * 512:(t + 1) * 512])
                    vy2 = self.dma("SP", f"w{r}", ye[r][:, :],
                                   self.d_ye[hc * 128:(hc + 1) * 128, t * 512:(t + 1) * 512])
                    self.qwait("POOL", vy2)
                    if it >= 3:
                        self.wait("POOL", "mm", mmrec[it - 3])
                    vcast = self.dma("POOL", f"wc{r}", wr[r][:, :], wt[r][:, :])
                    self.qwait("DVE", vcast)
                    vsub = self.op("DVE", "dve", lambda e, r=r: e.tensor_sub(
                        we[r][:, :], wt[r][:, :], wr[r][:, :].bitcast(F32)))
                    self.wait("TE", "dve", vsub)
                    self.qwait("TE", vy2)
                    if hc == 0 and it > 0:
                        self.wait("TE", "act", evac_v)
                    for oc in range(8):
                        pb = self.P[oc]
                        lr = wr[r][:, oc * 128:(oc + 1) * 128]
                        le = we[r][:, oc * 128:(oc + 1) * 128]
                        stt = (hc == 0)
                        stp = (hc == HC - 1)
                        last = oc == 7
                        self.op("TE", "mm", lambda e, pb=pb, l=lr, m=yr[r], s=stt: e.matmul(
                            pb[:, :], l, m[:, :], start=s, stop=False), inc=False)
                        self.op("TE", "mm", lambda e, pb=pb, l=lr, m=ye[r]: e.matmul(
                            pb[:, :], l, m[:, :], start=False, stop=False), inc=False)
                        self.op("TE", "mm", lambda e, pb=pb, l=le, m=yr[r], p=stp: e.matmul(
                            pb[:, :], l, m[:, :], start=False, stop=p), inc=last)
                    mmrec[it] = self.cnt["mm"]
                    it += 1
                    if it % 8 == 0 and pending:
                        pending.pop(0)()
                while pending:
                    pending.pop(0)()
                v_grp = self.cnt["mm"]
                self.wait("ACT", "mm", v_grp)
                if st["latest"] is not None:
                    self.qwait("ACT", st["latest"])
                for oc in range(8):
                    self.op("ACT", "act", lambda e, pb=self.P[oc], oc=oc: e.activation(
                        out=g2w[:, oc * 512:(oc + 1) * 512], in_=pb[:, :], func=AF.Silu),
                        inc=(oc == 7))
                evac_v = self.cnt["act"]
                pending = [mkstep(t, og, oc, evac_v) for oc in range(8)]
        while pending:
            pending.pop(0)()

    def ph_attn(self, qi):
        BIG = self.BIG
        nkb = NKB[qi]
        qr = self.sb("aqr", [128, DC * 512], F32R, BIG)
        qe = self.sb("aqe", [128, DC * 512], F32R, BIG + 32768)
        stor = self.sb("astr", [128, 16 * 512], F32R, BIG + 65536)
        stoe = self.sb("aste", [128, 16 * 512], F32R, BIG + 98304)
        kbr = [self.sb("akbr", [128, DC * 128], F32R, self.A + 8192 * i) for i in range(2)]
        kbe = [self.sb("akbe", [128, DC * 128], F32R, self.C + 8192 * i) for i in range(2)]
        mbuf = [self.sb("amsk", [128, 512], F32, self.M + 1600 + 2048 * i) for i in range(2)]
        sbnc = [self.sb("asb", [128, 512], F32, self.B + 2048 * i) for i in range(2)]
        srb = [self.sb("asr", [128, 512], F32R, self.B + 4096 + 2048 * i) for i in range(2)]

        cq = qi * 512
        v1 = self.dma("SP", "qq", self.fs(qr[:, :], 512),
                      self.fm(self.d_qr[:, cq:cq + 512]))
        v2 = self.dma("SP", "qq", self.fs(qe[:, :], 512),
                      self.fm(self.d_qe[:, cq:cq + 512]))
        self.qwait("TE", v2)
        kload = {}
        mload = {}
        st_done = {}
        for j, cb in enumerate(KBLIST[qi]):
            r = j % 2
            if j >= 2:
                self.wait("SP", "mm", st_done[j - 2][0])
            else:
                self.wait("SP", "mm", self.cnt["mm"])
            lk1 = self.dma("SP", f"kb{r}", self.fs(kbr[r][:, :], 128),
                           self.fm(self.d_kr[:, cb * 128:(cb + 1) * 128]))
            lk2 = self.dma("SP", f"kb{r}", self.fs(kbe[r][:, :], 128),
                           self.fm(self.d_ke[:, cb * 128:(cb + 1) * 128]))
            mi = (8 if qi else 0) + j if qi == 0 else 8 + j
            mi = j if qi == 0 else 8 + j
            if j >= 2:
                self.wait("ACT", "dve", st_done[j - 2][1])
            else:
                self.wait("ACT", "mm", self.cnt["mm"])
            lm = self.dma("ACT", f"mk{r}", mbuf[r][:, :], self.i_masks[mi, :, :])
            self.qwait("TE", lk2)
            pb = self.P[j % 2]
            if j >= 2:
                self.wait("TE", "dve", st_done[j - 2][1])
            for dc in range(DC):
                lr = kbr[r][:, dc * 128:(dc + 1) * 128]
                le = kbe[r][:, dc * 128:(dc + 1) * 128]
                mr = qr[:, dc * 512:(dc + 1) * 512]
                me = qe[:, dc * 512:(dc + 1) * 512]
                stt = (dc == 0)
                stp = (dc == DC - 1)
                self.op("TE", "mm", lambda e, pb=pb, l=lr, m=mr, s=stt: e.matmul(
                    pb[:, :], l, m, start=s, stop=False), inc=False)
                self.op("TE", "mm", lambda e, pb=pb, l=lr, m=me: e.matmul(
                    pb[:, :], l, m, start=False, stop=False), inc=False)
                self.op("TE", "mm", lambda e, pb=pb, l=le, m=mr, p=stp: e.matmul(
                    pb[:, :], l, m, start=False, stop=p), inc=stp)
            v_s = self.cnt["mm"]
            self.wait("DVE", "mm", v_s)
            self.qwait("DVE", lm)
            b = sbnc[r]
            rbb = srb[r]
            if j >= 2:
                self.qwait("DVE", st_done[j - 2][2])
            vm = self.op("DVE", "dve", lambda e, pb=pb, b=b, m=mbuf[r]: e.tensor_tensor(
                b[:, :], pb[:, :], m[:, :], ALU.mult), inc=True)
            self.wait("POOL", "dve", vm)
            vc = self.dma("POOL", "ogc", stor[:, j * 512:(j + 1) * 512], b[:, :])
            self.qwait("DVE", vc)
            vs = self.op("DVE", "dve", lambda e, b=b, j=j: e.tensor_sub(
                stoe[:, j * 512:(j + 1) * 512], b[:, :],
                stor[:, j * 512:(j + 1) * 512].bitcast(F32)))
            st_done[j] = (v_s, vs, vc)
        v_scores = self.cnt["dve"]

        ogr = self.sb("aogr", [128, DC * 512], F32R, BIG)
        oge = self.sb("aoge", [128, DC * 512], F32R, BIG + 32768)
        vbr = [self.sb("avbr", [128, 4 * 512], F32R, self.A + 8192 * i) for i in range(2)]
        vbe = [self.sb("avbe", [128, 4 * 512], F32R, self.C + 8192 * i) for i in range(2)]
        gb = [self.sb("agb", [128, 512], F32, self.B + 8192 + 2048 * i) for i in range(2)]
        ob = [self.sb("aob", [128, 512], F32, self.B + 2048 * i) for i in range(2)]
        orb = [self.sb("aor", [128, 512], F32R, self.B + 4096 + 2048 * i) for i in range(2)]
        self.wait("SP", "mm", self.cnt["mm"])
        self.wait("ACT", "mm", self.cnt["mm"])
        self.wait("TE", "dve", v_scores)
        gload = {}
        og_ev = 0
        o_ev = [None, None]
        guse = [0, 0]
        ngrp = nkb // 4
        for og in range(4):
            for j, cb in enumerate(KBLIST[qi]):
                r = (j // 4) % 2
                if j % 4 == 0:
                    gidx = og * ngrp + j // 4
                    if gidx >= 2:
                        self.wait("SP", "mm", gload[gidx - 2])
                    else:
                        self.wait("SP", "mm", self.cnt["mm"])
                    grow = cb * 128
                    l1 = self.dma("SP", f"vb{r}", self.fs(vbr[r][:, :], 512),
                                  self.fm(self.d_vr[grow:grow + 512,
                                                    og * 512:(og + 1) * 512]))
                    l2 = self.dma("SP", f"vb{r}", self.fs(vbe[r][:, :], 512),
                                  self.fm(self.d_ve[grow:grow + 512,
                                                    og * 512:(og + 1) * 512]))
                    self.qwait("TE", l2)
                if j == 0 and og > 0:
                    self.wait("TE", "dve", og_ev)
                for oc in range(4):
                    pb = self.P[4 * (og % 2) + oc]
                    lr = vbr[r][:, (j % 4) * 512 + oc * 128:(j % 4) * 512 + oc * 128 + 128]
                    le = vbe[r][:, (j % 4) * 512 + oc * 128:(j % 4) * 512 + oc * 128 + 128]
                    mr = stor[:, j * 512:(j + 1) * 512]
                    me = stoe[:, j * 512:(j + 1) * 512]
                    stt = (j == 0)
                    stp = (j == nkb - 1)
                    last = oc == 3
                    self.op("TE", "mm", lambda e, pb=pb, l=lr, m=mr, s=stt: e.matmul(
                        pb[:, :], l, m, start=s, stop=False), inc=False)
                    self.op("TE", "mm", lambda e, pb=pb, l=le, m=mr: e.matmul(
                        pb[:, :], l, m, start=False, stop=False), inc=False)
                    self.op("TE", "mm", lambda e, pb=pb, l=lr, m=me, p=stp: e.matmul(
                        pb[:, :], l, m, start=False, stop=p), inc=last)
                if j % 4 == 3:
                    gload[og * ngrp + j // 4] = self.cnt["mm"]
            v_og = self.cnt["mm"]
            self.wait("DVE", "mm", v_og)
            for oc in range(4):
                ch = og * 4 + oc
                b = ob[oc % 2]
                rbb = orb[oc % 2]
                row0 = ch * 128
                if ch >= 2:
                    self.wait("ACT", "dve", guse[oc % 2])
                else:
                    self.wait("ACT", "mm", self.cnt["mm"])
                lg = self.dma("ACT", f"gl{oc % 2}", gb[oc % 2][:, :],
                              self.d_g[row0:row0 + 128, cq:cq + 512])
                self.qwait("DVE", lg)
                if ch >= 2:
                    self.qwait("DVE", o_ev[oc % 2])
                vm = self.op("DVE", "dve", lambda e, pb=self.P[4 * (og % 2) + oc], b=b,
                        g=gb[oc % 2]: e.tensor_tensor(b[:, :], pb[:, :], g[:, :],
                                                      ALU.mult), inc=True)
                guse[oc % 2] = vm
                self.wait("POOL", "dve", vm)
                vc = self.dma("POOL", "ogc", ogr[:, ch * 512:(ch + 1) * 512], b[:, :])
                o_ev[oc % 2] = vc
                self.qwait("DVE", vc)
                self.op("DVE", "dve", lambda e, b=b, ch=ch: e.tensor_sub(
                    oge[:, ch * 512:(ch + 1) * 512], b[:, :],
                    ogr[:, ch * 512:(ch + 1) * 512].bitcast(F32)))
            og_ev = self.cnt["dve"]
        v_sv = self.cnt["dve"]

        self.wait("SP", "mm", self.cnt["mm"])
        wp = self.i_w["wp"]
        wt = [self.sb("apt", [128, 1024], F32, self.C + 4096 * i) for i in range(2)]
        wr = [self.sb("apr", [128, 1024], F32R, self.C + 8192 + 4096 * i) for i in range(2)]
        we = [self.sb("ape", [128, 1024], F32R, self.C + 16384 + 4096 * i) for i in range(2)]
        xg = self.sb("axg", [128, DC * 512], F32, BIG + 65536)
        g2t = self.sb("ag2", [128, DC * 512], F32, BIG + 98304)
        obo = [self.sb("aout", [128, 512], F32, self.B + 2048 * i) for i in range(4)]
        self.wait("ACT", "mm", self.cnt["mm"])
        vxg = self.dma("ACT", "xg", self.fs(xg[:, :], 512),
                       self.fm(self.i_x[:, cq:cq + 512]))
        vg2 = self.dma("ACT", "xg", self.fs(g2t[:, :], 512),
                       self.fm(self.d_g2[:, cq:cq + 512]))
        self.wait("TE", "dve", v_sv)
        mmrec = {}
        it = 0
        evac_v = 0
        out_dma = [None, None, None, None]
        for og2 in range(2):
            for dc in range(DC):
                r = it % 2
                if it >= 2:
                    self.wait("SP", "mm", mmrec[it - 2])
                else:
                    self.wait("SP", "mm", self.cnt["mm"])
                vl = self.dma("SP", f"w{r}", wt[r][:, :],
                              wp[dc * 128:(dc + 1) * 128, og2 * 1024:(og2 + 1) * 1024])
                self.qwait("POOL", vl)
                if it >= 2:
                    self.wait("POOL", "mm", mmrec[it - 2])
                vcast = self.dma("POOL", f"wc{r}", wr[r][:, :], wt[r][:, :])
                self.qwait("DVE", vcast)
                vsub = self.op("DVE", "dve", lambda e, r=r: e.tensor_sub(
                    we[r][:, :], wt[r][:, :], wr[r][:, :].bitcast(F32)))
                self.wait("TE", "dve", vsub)
                if dc == 0 and og2 > 0:
                    self.wait("TE", "dve", evac_v)
                for oc in range(8):
                    pb = self.P[oc]
                    lr = wr[r][:, oc * 128:(oc + 1) * 128]
                    le = we[r][:, oc * 128:(oc + 1) * 128]
                    mr = ogr[:, dc * 512:(dc + 1) * 512]
                    me = oge[:, dc * 512:(dc + 1) * 512]
                    stt = (dc == 0)
                    stp = (dc == DC - 1)
                    last = oc == 7
                    self.op("TE", "mm", lambda e, pb=pb, l=lr, m=mr, s=stt: e.matmul(
                        pb[:, :], l, m, start=s, stop=False), inc=False)
                    self.op("TE", "mm", lambda e, pb=pb, l=lr, m=me: e.matmul(
                        pb[:, :], l, m, start=False, stop=False), inc=False)
                    self.op("TE", "mm", lambda e, pb=pb, l=le, m=mr, p=stp: e.matmul(
                        pb[:, :], l, m, start=False, stop=p), inc=last)
                mmrec[it] = self.cnt["mm"]
                it += 1
            v_grp = self.cnt["mm"]
            self.wait("DVE", "mm", v_grp)
            self.qwait("DVE", vg2)
            for oc in range(8):
                ch = og2 * 8 + oc
                b = obo[oc % 4]
                if ch >= 4:
                    self.qwait("DVE", out_dma[oc % 4])
                vmul = self.op("DVE", "dve", lambda e, pb=self.P[oc], b=b, ch=ch: e.tensor_tensor(
                    b[:, :], pb[:, :], g2t[:, ch * 512:(ch + 1) * 512], ALU.mult),
                    inc=True)
                self.wait("DVE", "dve", vmul)
                vm = self.op("DVE", "dve", lambda e, b=b, ch=ch: e.tensor_tensor(
                    b[:, :], b[:, :], xg[:, ch * 512:(ch + 1) * 512], ALU.add))
                self.wait("ACT", "dve", vm)
                row0 = ch * 128
                vo = self.dma("ACT", f"o{oc % 4}",
                              self.o_out[row0:row0 + 128, cq:cq + 512], b[:, :])
                out_dma[oc % 4] = vo
            evac_v = self.cnt["dve"]

    def build(self):
        self.ph0()
        if self.st["p1"]:
            self.fence()
            self.ph1_slab(1)
            self.fence()
            self.ph1_slab(0)
        if self.st["ffn"]:
            self.fence()
            self.ph_ffn()
        if self.st["fwht"]:
            self.fence()
            self.ph_fwht()
        if self.st["w2"]:
            self.fence()
            self.ph_w2()
        if self.st["attn"]:
            self.fence()
            self.ph_attn(0)
            self.fence()
            self.ph_attn(1)
        self.fence()
        for s in ("o0", "o1", "o2", "o3"):
            self.qwait("ACT", self.qtotal(s))
            self.qwait("POOL", self.qtotal(s))
        with self.nc.Block() as block:
            @block.tensor
            def _(e):
                for f in self.ops["TE"]:
                    f(e)

            @block.scalar
            def _(e):
                for f in self.ops["ACT"]:
                    f(e)

            @block.vector
            def _(e):
                for f in self.ops["DVE"]:
                    f(e)

            @block.gpsimd
            def _(e):
                for f in self.ops["POOL"]:
                    f(e)

            @block.sync
            def _(e):
                for f in self.ops["SP"]:
                    f(e)
        self.ctx.close()
        return self.nc


def _hadamard(n):
    h = np.array([[1.0]], dtype=np.float32)
    while h.shape[0] < n:
        h = np.block([[h, h], [h, -h]])
    return h.astype(np.float32)


def _staging(core, x):
    b, h = core // 2, core % 2
    own = np.concatenate([np.arange((2 * m + h) * 128, (2 * m + h) * 128 + 128)
                          for m in range(8)])
    par = np.concatenate([np.arange((2 * m + 1 - h) * 128, (2 * m + 1 - h) * 128 + 128)
                          for m in range(8)])
    perm = np.concatenate([own, par])
    xT = np.ascontiguousarray(x[b][perm, :].T)
    gidx = perm
    masks = np.zeros((24, 128, 512), dtype=np.float32)
    row = 0
    for qi in (0, 1):
        gq = gidx[qi * 512: qi * 512 + 512]
        for cb in KBLIST[qi]:
            gk = gidx[cb * 128:(cb + 1) * 128]
            masks[row] = (gk[:, None] <= gq[None, :]).astype(np.float32)
            row += 1
    return xT, masks, own


def kernel(L0_attn_x, L0_attn_Wq, L0_attn_Wk, L0_attn_Wv, L0_attn_Wg, L0_attn_Wp,
           L0_ffn_W1, L0_ffn_W2, L0_ffn_W3):
    x = np.ascontiguousarray(np.asarray(L0_attn_x, dtype=np.float32))
    ws = {
        "wqT": np.ascontiguousarray(np.asarray(L0_attn_Wq).T),
        "wkT": np.ascontiguousarray(np.asarray(L0_attn_Wk).T),
        "wvT": np.ascontiguousarray(np.asarray(L0_attn_Wv).T),
        "wgT": np.ascontiguousarray(np.asarray(L0_attn_Wg).T),
        "wpT": np.ascontiguousarray(np.asarray(L0_attn_Wp).T),
        "w1T": np.ascontiguousarray(np.asarray(L0_ffn_W1).T),
        "w3T": np.ascontiguousarray(np.asarray(L0_ffn_W3).T),
        "w2T": np.ascontiguousarray(np.asarray(L0_ffn_W2).T),
    }
    had = _hadamard(128)
    in_maps = []
    owns = []
    for c in range(8):
        xT, masks, own = _staging(c, x)
        owns.append(own)
        m = {"xT": xT, "masks": masks, "had128": had}
        m.update(ws)
        in_maps.append(m)

    nc = Builder().build()
    res = run_bass_kernel_spmd(nc, in_maps, core_ids=list(range(8)))
    out = np.empty((4, 2048, 2048), dtype=np.float32)
    for c in range(8):
        b = c // 2
        out[b][owns[c], :] = res.results[c]["outT"].T
    return out


# version 08 (final)
# speedup vs baseline: 1.5049x; 1.0396x over previous
from contextlib import ExitStack

import numpy as np
import concourse.bass as bass
import concourse.mybir as mybir
from concourse.bass_utils import run_bass_kernel_spmd

F32 = mybir.dt.float32
F32R = mybir.dt.float32r
AF = mybir.ActivationFunctionType
ALU = mybir.AluOpType

D = 2048
DC = 16
TS = 1024
HIDDEN = 8192
HC = 64
NKB = {0: 8, 1: 16}
KBLIST = {0: [0, 1, 2, 3, 8, 9, 10, 11], 1: list(range(16))}
LN_EPS = 1e-5
RMS_EPS = float(np.finfo(np.float32).eps)
FWHT_SCALE = float(1.0 / np.sqrt(HIDDEN))

ENGINES = ("TE", "ACT", "DVE", "POOL", "SP")

STAGES = {"p1": True, "ffn": True, "fwht": True, "w2": True, "attn": True,
          "debug": False}


def ceil32(x):
    return (x + 31) & ~31


class Builder:
    def __init__(self, stages=STAGES):
        self.st = dict(stages)
        nc = bass.Bass("TRN2", target_bir_lowering=False)
        self.nc = nc
        self.ops = {e: [] for e in ENGINES}
        self.cnt = {}
        self.sem = {}
        self.ctx = ExitStack()
        for name in ("mm", "act", "dve", "pool", "sp"):
            self.sem[name] = self.ctx.enter_context(nc.semaphore("s_" + name))
            self.cnt[name] = 0
        self.qsem, self.qcnt, self.qeng = {}, {}, {}
        self._declare_dram()
        self._declare_sbuf()
        self._declare_psum()

    def inc(self, name, n=1):
        self.cnt[name] += n
        return self.cnt[name]

    def emit(self, eng, fn):
        self.ops[eng].append(fn)

    def wait(self, eng, sem, val):
        if val <= 0:
            return
        s = self.sem[sem]
        self.emit(eng, lambda e, s=s, v=val: e.wait_ge(s, v))

    def dma(self, eng, stream, out, in_):
        if stream not in self.qsem:
            self.qsem[stream] = self.ctx.enter_context(self.nc.semaphore("q_" + stream))
            self.qcnt[stream] = 0
            self.qeng[stream] = eng
        assert self.qeng[stream] == eng, (stream, eng)
        self.qcnt[stream] += 16
        v = self.qcnt[stream]
        s = self.qsem[stream]
        self.emit(eng, lambda e, o=out, i=in_, s=s: e.dma_start(out=o, in_=i).then_inc(s, 16))
        return (stream, v)

    def qwait(self, eng, tok):
        if tok is None:
            return
        stream, v = tok
        if v <= 0:
            return
        s = self.qsem[stream]
        self.emit(eng, lambda e, s=s, v=v: e.wait_ge(s, v))

    def qtotal(self, stream):
        return (stream, self.qcnt.get(stream, 0))

    @staticmethod
    def fm(ap2d):
        return ap2d.rearrange("(c p) t -> p c t", p=128)

    @staticmethod
    def fs(sb_ap, t):
        return sb_ap.rearrange("p (c t) -> p c t", t=t)

    def op(self, eng, esem, fn, inc=True):
        if inc:
            v = self.inc(esem)
            s = self.sem[esem]
            self.emit(eng, lambda e, f=fn, s=s: f(e).then_inc(s, 1))
            return v
        self.emit(eng, lambda e, f=fn: f(e))
        return self.cnt[esem]

    def fence(self, tag=""):
        vals = {}
        for e in ENGINES:
            for stream, owner in self.qeng.items():
                if owner == e:
                    self.qwait(e, self.qtotal(stream))
            es = {"TE": "mm", "ACT": "act", "DVE": "dve", "POOL": "pool", "SP": "sp"}[e]
            self.wait(e, es, self.cnt[es])
            v = self.inc(es)
            s = self.sem[es]
            self.emit(e, lambda en, s=s: en.nop().then_inc(s, 1))
            vals[e] = (es, v)
        for e in ENGINES:
            for o in ENGINES:
                if o != e:
                    sname, v = vals[o]
                    self.wait(e, sname, v)

    def _declare_dram(self):
        nc = self.nc
        t = nc.dram_tensor
        self.i_x = t("xT", [D, 2 * TS], F32, kind="ExternalInput").ap()
        self.i_w = {}
        for nm in ("wq", "wk", "wv", "wg", "wp"):
            self.i_w[nm] = t(nm + "T", [D, D], F32, kind="ExternalInput").ap()
        self.i_w["w1"] = t("w1T", [D, HIDDEN], F32, kind="ExternalInput").ap()
        self.i_w["w3"] = t("w3T", [D, HIDDEN], F32, kind="ExternalInput").ap()
        self.i_w["w2"] = t("w2T", [HIDDEN, D], F32, kind="ExternalInput").ap()
        self.i_masks = t("masks", [24, 128, 512], F32, kind="ExternalInput").ap()
        self.i_h128 = t("had128", [128, 128], F32, kind="ExternalInput").ap()

        kind = "ExternalOutput" if self.st["debug"] else "Internal"
        if self.st["debug"]:
            self.d_x2r = t("x2r_d", [D, TS], F32R, kind=kind).ap()
            self.d_x2e = t("x2e_d", [D, TS], F32R, kind=kind).ap()
            self.d_y1dbg = t("y1dbg_d", [128, HC * 256], F32, kind=kind).ap()
            self.d_ybf = t("ybf_d", [128, HC * 256], F32, kind=kind).ap()
        self.d_kr = t("kr_d", [D, 2 * TS], F32R, kind=kind).ap()
        self.d_ke = t("ke_d", [D, 2 * TS], F32R, kind=kind).ap()
        self.d_vr = t("vr_d", [2 * TS, D], F32R, kind=kind).ap()
        self.d_ve = t("ve_d", [2 * TS, D], F32R, kind=kind).ap()
        self.d_qr = t("qr_d", [D, TS], F32R, kind=kind).ap()
        self.d_qe = t("qe_d", [D, TS], F32R, kind=kind).ap()
        self.d_g = t("g_d", [D, TS], F32, kind=kind).ap()
        self.d_xn = t("xn_d", [D, TS], F32, kind=kind).ap()
        self.d_hr = t("hr_d", [HIDDEN, TS], F32R, kind=kind).ap()
        self.d_he = t("he_d", [HIDDEN, TS], F32R, kind=kind).ap()
        self.d_yr = t("yr_d", [HIDDEN, TS], F32R, kind=kind).ap()
        self.d_ye = t("ye_d", [HIDDEN, TS], F32R, kind=kind).ap()
        self.d_g2 = t("g2_d", [D, TS], F32, kind=kind).ap()
        self.o_out = t("outT", [D, TS], F32, kind="ExternalOutput").ap()

    def _declare_sbuf(self):
        nc = self.nc
        at = nc.alloc_sbuf_tensor_at
        base0 = nc.sbuf_base
        M = ceil32(base0)
        self.M = M
        self.A = M + 18432
        self.B = M + 38912
        self.C = M + 53248
        self.BIG = M + 77824
        end = self.BIG + 131072
        nc.alloc_sbuf_tensor("manual_map_reserved",
                             [128, end - base0], mybir.dt.uint8)
        self.ones_col = at("ones_col", [128, 1], F32, offset=M).ap()
        self.ones_row = at("ones_row", [1, 128], F32, offset=M + 32).ap()
        self.h128r = at("h128r", [128, 128], F32R, offset=M + 544).ap()
        self.h128f = at("h128f", [128, 128], F32, offset=M + 1056).ap()
        self.stat = [at(f"stat{i}", [1, 1024], F32, offset=M + 1600 + 4096 * i).ap()
                     for i in range(4)]
        self._tens = {}

    def sb(self, name, shape, dtype, off):
        key = (name, tuple(shape), dtype, off)
        if key not in self._tens:
            self._tens[key] = self.nc.alloc_sbuf_tensor_at(
                f"{name}_{off}", shape, dtype, offset=off).ap()
        return self._tens[key]

    def _declare_psum(self):
        self.P = [self.ctx.enter_context(
            self.nc.psum_tensor(f"pb{i}", [128, 512], F32)) for i in range(8)]

    def ph0(self):
        self.op("DVE", "dve", lambda e: e.memset(self.ones_col[:, :], 1.0), inc=False)
        self.op("DVE", "dve", lambda e: e.memset(self.ones_row[:, :], 1.0))
        v = self.dma("POOL", "ini", self.h128f[:, :], self.i_h128[:, :])
        self.qwait("POOL", v)
        vc = self.dma("POOL", "ini", self.h128r[:, :], self.h128f[:, :])
        self.qwait("TE", vc)
        self.wait("TE", "dve", self.cnt["dve"])

    def ph1_slab(self, s):
        c0 = s * TS
        own = (s == 0)
        BIG = self.BIG
        xn = self.sb("xn", [128, DC * TS], F32, BIG)
        xn2r = self.sb("xn2r", [128, DC * TS], F32R, BIG + 65536)
        xn2e = self.sb("xn2e", [128, DC * TS], F32R, BIG)
        sq = [self.sb("sqb", [128, 1024], F32, self.B + 4096 * i) for i in range(2)]
        bnc = [self.sb("lnb", [128, 1024], F32, self.B + 8192 + 4096 * i) for i in range(2)]

        src = self.fm(self.i_x[:, c0:c0 + TS])
        vx = self.dma("ACT", "xs", self.fs(xn[:, :], TS), src)

        self.qwait("TE", vx)
        self.qwait("ACT", vx)
        mm_done = []
        sq_done = [0, 0]
        for dc in range(DC):
            xc = xn[:, dc * TS:(dc + 1) * TS]
            sqb = sq[dc % 2]
            if dc >= 2:
                self.wait("ACT", "mm", sq_done[dc % 2])
            va = self.op("ACT", "act", lambda e, i=xc, o=sqb: e.activation(
                out=o[:, :], in_=i, func=AF.Square))
            for t in range(2):
                last = (dc == DC - 1)
                self.op("TE", "mm", lambda e, i=xc, t=t, dc=dc, last=last: e.matmul(
                    self.P[t][0:1, :], self.ones_col[:, :], i[:, t * 512:(t + 1) * 512],
                    start=(dc == 0), stop=last), inc=last)
            self.wait("TE", "act", va)
            for t in range(2):
                last = (dc == DC - 1)
                v = self.op("TE", "mm", lambda e, i=sqb, t=t, dc=dc, last=last: e.matmul(
                    self.P[2 + t][0:1, :], self.ones_col[:, :], i[:, t * 512:(t + 1) * 512],
                    start=(dc == 0), stop=last), inc=True)
                sq_done[dc % 2] = v
        v_stats = self.cnt["mm"]

        st0, st1, st2, st3 = self.stat
        self.wait("DVE", "mm", v_stats)

        def R(fn):
            self.wait("DVE", "dve", self.cnt["dve"])
            return self.op("DVE", "dve", fn, inc=True)
        for t in range(2):
            R(lambda e, t=t: e.tensor_scalar_mul(st0[:, t * 512:(t + 1) * 512],
                                                 self.P[t][0:1, :], 1.0 / D))
            R(lambda e, t=t: e.tensor_scalar_mul(st1[:, t * 512:(t + 1) * 512],
                                                 self.P[2 + t][0:1, :], 1.0 / D))
        R(lambda e: e.tensor_tensor(st2[:, :], st0[:, :], st0[:, :], ALU.mult))
        R(lambda e: e.tensor_sub(st1[:, :], st1[:, :], st2[:, :]))
        self._rsqrt_row(st1, st2, st3, LN_EPS)
        self.wait("DVE", "dve", self.cnt["dve"])
        v_rows = self.op("DVE", "dve", lambda e: e.tensor_tensor(
            st3[:, :], st0[:, :], st2[:, :], ALU.mult))

        self.wait("TE", "dve", v_rows)
        for t in range(2):
            self.op("TE", "mm", lambda e, t=t: e.matmul(
                self.P[4 + t][:, :], self.ones_row[:, :], st2[:, t * 512:(t + 1) * 512],
                start=True, stop=True), inc=False)
            self.op("TE", "mm", lambda e, t=t: e.matmul(
                self.P[6 + t][:, :], self.ones_row[:, :], st3[:, t * 512:(t + 1) * 512],
                start=True, stop=True), inc=(t == 1))
        v_bcast = self.cnt["mm"]

        self.wait("DVE", "mm", v_bcast)
        xn_chunk_done = []
        for dc in range(DC):
            xc = xn[:, dc * TS:(dc + 1) * TS]
            b = bnc[dc % 2]
            for t in range(2):
                R(lambda e, i=xc, b=b, t=t: e.tensor_tensor(
                    b[:, t * 512:(t + 1) * 512], i[:, t * 512:(t + 1) * 512],
                    self.P[4 + t][:, :], ALU.mult))
            self.wait("DVE", "dve", self.cnt["dve"])
            v = self.op("DVE", "dve", lambda e, i=xc, b=b: e.tensor_tensor(
                i[:, 0:512], b[:, 0:512], self.P[6][:, :], ALU.subtract), inc=True)
            self.wait("DVE", "dve", self.cnt["dve"])
            v = self.op("DVE", "dve", lambda e, i=xc, b=b: e.tensor_tensor(
                i[:, 512:1024], b[:, 512:1024], self.P[7][:, :], ALU.subtract))
            xn_chunk_done.append(v)
        if own:
            self.wait("POOL", "dve", xn_chunk_done[-1])
            self.v_xn_dma = self.dma("POOL", "xnst", self.fm(self.d_xn[:, :]),
                                     self.fs(xn[:, :], TS))

        sq_done = [0, 0]
        for dc in range(DC):
            xc = xn[:, dc * TS:(dc + 1) * TS]
            sqb = sq[dc % 2]
            self.wait("ACT", "dve", xn_chunk_done[dc])
            if dc >= 2:
                self.wait("ACT", "mm", sq_done[dc % 2])
            va = self.op("ACT", "act", lambda e, i=xc, o=sqb: e.activation(
                out=o[:, :], in_=i, func=AF.Square))
            self.wait("TE", "act", va)
            for t in range(2):
                last = (dc == DC - 1)
                v = self.op("TE", "mm", lambda e, i=sqb, t=t, dc=dc, last=last: e.matmul(
                    self.P[t][0:1, :], self.ones_col[:, :], i[:, t * 512:(t + 1) * 512],
                    start=(dc == 0), stop=last), inc=True)
                sq_done[dc % 2] = v
        v_stats2 = self.cnt["mm"]
        self.wait("DVE", "mm", v_stats2)
        for t in range(2):
            R(lambda e, t=t: e.tensor_scalar_mul(st0[:, t * 512:(t + 1) * 512],
                                                 self.P[t][0:1, :], 1.0 / D))
        self._rsqrt_row(st0, st2, st3, RMS_EPS)
        v_rows2 = self.cnt["dve"]
        self.wait("TE", "dve", v_rows2)
        for t in range(2):
            self.op("TE", "mm", lambda e, t=t: e.matmul(
                self.P[4 + t][:, :], self.ones_row[:, :], st2[:, t * 512:(t + 1) * 512],
                start=True, stop=True), inc=(t == 1))
        v_bc2 = self.cnt["mm"]

        self.wait("DVE", "mm", v_bc2)
        for dc in range(DC):
            xc = xn[:, dc * TS:(dc + 1) * TS]
            b = bnc[dc % 2]
            if dc >= 2:
                self.qwait("DVE", self._xn2cast[dc % 2])
            for t in range(2):
                R(lambda e, i=xc, b=b, t=t: e.tensor_tensor(
                    b[:, t * 512:(t + 1) * 512], i[:, t * 512:(t + 1) * 512],
                    self.P[4 + t][:, :], ALU.mult))
            vb = self.cnt["dve"]
            self.wait("POOL", "dve", vb)
            if own and dc == 0:
                pass
            vc = self.dma("POOL", "xc", xn2r[:, dc * TS:(dc + 1) * TS], b[:, :])
            if not hasattr(self, "_xn2cast"):
                self._xn2cast = [None, None]
            self._xn2cast[dc % 2] = vc
            self.qwait("DVE", vc)
            if own:
                self.qwait("DVE", self.v_xn_dma)
            self.op("DVE", "dve", lambda e, b=b, dc=dc: e.tensor_sub(
                xn2e[:, dc * TS:(dc + 1) * TS], b[:, :],
                xn2r[:, dc * TS:(dc + 1) * TS].bitcast(F32)))
        v_split = self.cnt["dve"]

        if self.st["debug"] and own:
            self.wait("POOL", "dve", v_split)
            self.dma("POOL", "dbg", self.fm(self.d_x2r[:, :]), self.fs(xn2r[:, :], TS))
            self.dma("POOL", "dbg", self.fm(self.d_x2e[:, :]), self.fs(xn2e[:, :], TS))

        self.fence()
        self._pj = {"it": 0, "mmrec": {}, "unit": 0, "evac_v": 0,
                    "st": {"prev_sub": None, "latest": None, "last_cast": None},
                    "pending": []}
        self.proj_featmaj("wk", self.i_w["wk"], AF.Silu, self.d_kr, self.d_ke,
                          xn2r, xn2e, out_cols=(c0, TS), v_ready=v_split)
        self.proj_tokmaj(xn2r, xn2e, slab_off=c0, v_ready=v_split,
                         flush=not own)
        if own:
            self.proj_featmaj("wq", self.i_w["wq"], AF.Silu, self.d_qr, self.d_qe,
                              xn2r, xn2e, out_cols=(0, TS), v_ready=v_split)
            self.proj_featmaj("wg", self.i_w["wg"], AF.Sigmoid, self.d_g, None,
                              xn2r, xn2e, out_cols=(0, TS), v_ready=v_split,
                              flush=True)

    def _rsqrt_row(self, vrow, out, tmp, eps):
        def R(fn):
            self.wait("DVE", "dve", self.cnt["dve"])
            return self.op("DVE", "dve", fn, inc=True)

        def A(fn):
            self.wait("ACT", "act", self.cnt["act"])
            return self.op("ACT", "act", fn, inc=True)
        R(lambda e: e.tensor_scalar_add(vrow[:, :], vrow[:, :], eps))
        self.wait("ACT", "dve", self.cnt["dve"])
        va = self.op("ACT", "act", lambda e: e.activation(out=tmp[:, :], in_=vrow[:, :],
                                                          func=AF.Sqrt))
        self.wait("DVE", "act", va)
        R(lambda e: e.reciprocal(out[:, :], tmp[:, :]))
        for i in range(2):
            R(lambda e: e.tensor_tensor(tmp[:, :], out[:, :], out[:, :], ALU.mult))
            R(lambda e: e.tensor_tensor(tmp[:, :], tmp[:, :], vrow[:, :], ALU.mult))
            R(lambda e: e.tensor_scalar(tmp[:, :], tmp[:, :], -0.5, 1.5,
                                        ALU.mult, ALU.add))
            self.wait("DVE", "dve", self.cnt["dve"])
            self.op("DVE", "dve", lambda e: e.tensor_tensor(
                out[:, :], out[:, :], tmp[:, :], ALU.mult), inc=True)

    def proj_featmaj(self, nm, w, act, dr, de, xr, xe, out_cols, v_ready,
                    n_out=D, flush=False):
        c0, TS_ = out_cols
        wtmp = [self.sb(nm + "wt", [128, 512], F32, self.A + 2048 * i) for i in range(3)]
        wr = [self.sb(nm + "wr", [128, 512], F32R, self.A + 6144 + 2048 * i) for i in range(3)]
        we = [self.sb(nm + "we", [128, 512], F32R, self.A + 12288 + 2048 * i) for i in range(3)]
        rb = [self.sb(nm + "rb", [128, 512], F32R, self.B + 2048 * i) for i in range(2)]
        ebw = self.sb(nm + "ebw", [128, 8 * 512], F32, self.C + 2048)
        n_og = n_out // 512
        self.wait("TE", "dve", v_ready)
        pj = self._pj
        mmrec = pj["mmrec"]
        st = pj["st"]
        pending = pj["pending"]

        def mkstep(og, j, va):
            def step():
                oc, t = divmod(j, 2)
                bsl = ebw[:, j * 512:(j + 1) * 512]
                rbb = rb[j % 2]
                row0 = (og * 4 + oc) * 128
                dst_cols = (c0 + t * 512, c0 + t * 512 + 512)
                self.wait("POOL", "act", va)
                if de is None:
                    vo = self.dma("POOL", "ps0",
                                  dr[row0:row0 + 128, dst_cols[0]:dst_cols[1]], bsl)
                    st["latest"] = vo
                    st["last_cast"] = vo
                    return
                if st["prev_sub"] is not None:
                    self.wait("POOL", "dve", st["prev_sub"])
                if st["latest"] is not None:
                    self.qwait("POOL", st["latest"])
                vc = self.dma("POOL", "pc0", rbb[:, :], bsl)
                st["last_cast"] = vc
                self.qwait("DVE", vc)
                vs = self.op("DVE", "dve", lambda e, b=bsl, rbb=rbb: e.tensor_sub(
                    b, b, rbb[:, :].bitcast(F32)))
                st["prev_sub"] = vs
                self.wait("POOL", "dve", vs)
                self.dma("POOL", "ps0",
                         dr[row0:row0 + 128, dst_cols[0]:dst_cols[1]], rbb[:, :])
                vo = self.dma("POOL", "ps0",
                              de[row0:row0 + 128, dst_cols[0]:dst_cols[1]],
                              bsl.bitcast(F32R))
                st["latest"] = vo
            return step

        for og in range(n_og):
            for dc in range(DC):
                it = pj["it"]
                r = it % 3
                if it >= 3:
                    self.wait("SP", "mm", mmrec[(it - 3)])
                else:
                    self.wait("SP", "mm", self.cnt["mm"])
                vload = self.dma("SP", f"w{r}", wtmp[r][:, :],
                                 w[dc * 128:(dc + 1) * 128, og * 512:(og + 1) * 512])
                self.qwait("POOL", vload)
                if it >= 3:
                    self.wait("POOL", "mm", mmrec[(it - 3)])
                vcast = self.dma("POOL", f"wc{r}", wr[r][:, :], wtmp[r][:, :])
                self.qwait("DVE", vcast)
                vsub = self.op("DVE", "dve", lambda e, r=r: e.tensor_sub(
                    we[r][:, :], wtmp[r][:, :], wr[r][:, :].bitcast(F32)))
                self.wait("TE", "dve", vsub)
                if pj["unit"] > 0 and dc == 0:
                    self.wait("TE", "act", pj["evac_v"])
                for oc in range(4):
                    for t in range(2):
                        pb = self.P[oc * 2 + t]
                        mv_r = xr[:, dc * TS_ + t * 512: dc * TS_ + t * 512 + 512]
                        mv_e = xe[:, dc * TS_ + t * 512: dc * TS_ + t * 512 + 512]
                        stt = (dc == 0)
                        stp = (dc == DC - 1)
                        lr = wr[r][:, oc * 128:(oc + 1) * 128]
                        le = we[r][:, oc * 128:(oc + 1) * 128]
                        self.op("TE", "mm", lambda e, pb=pb, lr=lr, mv=mv_r, s=stt: e.matmul(
                            pb[:, :], lr, mv, start=s, stop=False), inc=False)
                        self.op("TE", "mm", lambda e, pb=pb, lr=lr, mv=mv_e: e.matmul(
                            pb[:, :], lr, mv, start=False, stop=False), inc=False)
                        last = oc == 3 and t == 1
                        self.op("TE", "mm", lambda e, pb=pb, le=le, mv=mv_r, p=stp: e.matmul(
                            pb[:, :], le, mv, start=False, stop=p), inc=last)
                mmrec[it] = self.cnt["mm"]
                pj["it"] = it + 1
                if dc % 2 == 1 and pending:
                    pending.pop(0)()
            while pending:
                pending.pop(0)()
            v_grp = self.cnt["mm"]
            self.wait("ACT", "mm", v_grp)
            if st["last_cast"] is not None:
                self.qwait("ACT", st["last_cast"])
            if st["latest"] is not None:
                self.qwait("ACT", st["latest"])
            for j in range(8):
                self.op("ACT", "act", lambda e, pb=self.P[j], j=j, f=act: e.activation(
                    out=ebw[:, j * 512:(j + 1) * 512], in_=pb[:, :], func=f),
                    inc=(j == 7))
            pj["evac_v"] = self.cnt["act"]
            pj["unit"] += 1
            pending.extend(mkstep(og, j, pj["evac_v"]) for j in range(8))
        if flush:
            while pending:
                pending.pop(0)()

    def proj_tokmaj(self, xr, xe, slab_off, v_ready, flush=False):
        nm = "wv"
        w = self.i_w["wv"]
        wtmp = [self.sb(nm + "wt", [128, 512], F32, self.A + 2048 * i) for i in range(3)]
        wr = [self.sb(nm + "wr", [128, 512], F32R, self.A + 6144 + 2048 * i) for i in range(3)]
        we = [self.sb(nm + "we", [128, 512], F32R, self.A + 12288 + 2048 * i) for i in range(3)]
        rb = [self.sb(nm + "rb", [128, 512], F32R, self.B + 2048 * i) for i in range(2)]
        ebw = self.sb(nm + "ebw", [128, 8 * 512], F32, self.C + 2048)
        self.wait("TE", "dve", v_ready)
        pj = self._pj
        mmrec = pj["mmrec"]
        st = pj["st"]
        pending = pj["pending"]

        def mkstep(dt, tb, va):
            def step():
                bsl = ebw[:, tb * 512:(tb + 1) * 512]
                rbb = rb[tb % 2]
                row0 = slab_off + tb * 128
                self.wait("POOL", "act", va)
                if st["prev_sub"] is not None:
                    self.wait("POOL", "dve", st["prev_sub"])
                if st["latest"] is not None:
                    self.qwait("POOL", st["latest"])
                vc = self.dma("POOL", "pc0", rbb[:, :], bsl)
                st["last_cast"] = vc
                self.qwait("DVE", vc)
                vs = self.op("DVE", "dve", lambda e, b=bsl, rbb=rbb: e.tensor_sub(
                    b, b, rbb[:, :].bitcast(F32)))
                st["prev_sub"] = vs
                self.wait("POOL", "dve", vs)
                self.dma("POOL", "ps0",
                         self.d_vr[row0:row0 + 128, dt * 512:(dt + 1) * 512], rbb[:, :])
                vo = self.dma("POOL", "ps0",
                              self.d_ve[row0:row0 + 128, dt * 512:(dt + 1) * 512],
                              bsl.bitcast(F32R))
                st["latest"] = vo
            return step

        for dt in range(4):
            for dc in range(DC):
                it = pj["it"]
                r = it % 3
                if it >= 3:
                    self.wait("SP", "mm", mmrec[it - 3])
                else:
                    self.wait("SP", "mm", self.cnt["mm"])
                vload = self.dma("SP", f"w{r}", wtmp[r][:, :],
                                 w[dc * 128:(dc + 1) * 128, dt * 512:(dt + 1) * 512])
                self.qwait("POOL", vload)
                if it >= 3:
                    self.wait("POOL", "mm", mmrec[it - 3])
                vcast = self.dma("POOL", f"wc{r}", wr[r][:, :], wtmp[r][:, :])
                self.qwait("DVE", vcast)
                vsub = self.op("DVE", "dve", lambda e, r=r: e.tensor_sub(
                    we[r][:, :], wtmp[r][:, :], wr[r][:, :].bitcast(F32)))
                self.wait("TE", "dve", vsub)
                if pj["unit"] > 0 and dc == 0:
                    self.wait("TE", "act", pj["evac_v"])
                for tb in range(8):
                    pb = self.P[tb]
                    st_r = xr[:, dc * TS + tb * 128: dc * TS + tb * 128 + 128]
                    st_e = xe[:, dc * TS + tb * 128: dc * TS + tb * 128 + 128]
                    stt = (dc == 0)
                    stp = (dc == DC - 1)
                    last = tb == 7
                    self.op("TE", "mm", lambda e, pb=pb, l=st_r, m=wr[r], s=stt: e.matmul(
                        pb[:, :], l, m[:, :], start=s, stop=False), inc=False)
                    self.op("TE", "mm", lambda e, pb=pb, l=st_r, m=we[r]: e.matmul(
                        pb[:, :], l, m[:, :], start=False, stop=False), inc=False)
                    self.op("TE", "mm", lambda e, pb=pb, l=st_e, m=wr[r], p=stp: e.matmul(
                        pb[:, :], l, m[:, :], start=False, stop=p), inc=last)
                mmrec[it] = self.cnt["mm"]
                pj["it"] = it + 1
                if dc % 2 == 1 and pending:
                    pending.pop(0)()
            while pending:
                pending.pop(0)()
            v_grp = self.cnt["mm"]
            self.wait("ACT", "mm", v_grp)
            if st["last_cast"] is not None:
                self.qwait("ACT", st["last_cast"])
            if st["latest"] is not None:
                self.qwait("ACT", st["latest"])
            for tb in range(8):
                self.op("ACT", "act", lambda e, pb=self.P[tb], tb=tb: e.activation(
                    out=ebw[:, tb * 512:(tb + 1) * 512], in_=pb[:, :], func=AF.Silu),
                    inc=(tb == 7))
            pj["evac_v"] = self.cnt["act"]
            pj["unit"] += 1
            pending.extend(mkstep(dt, tb, pj["evac_v"]) for tb in range(8))
        if flush:
            while pending:
                pending.pop(0)()

    def ph_ffn(self):
        BIG = self.BIG
        xn = self.sb("fxn", [128, DC * TS], F32, BIG)
        xnr = self.sb("fxnr", [128, DC * TS], F32R, BIG + 65536)
        xne = self.sb("fxne", [128, DC * TS], F32R, BIG)

        src = self.fm(self.d_xn[:, :])
        vx = self.dma("POOL", "xnst", self.fs(xn[:, :], TS), src)
        self.qwait("POOL", vx)
        cast_v = []
        for dc in range(DC):
            vc = self.dma("POOL", "fc", xnr[:, dc * TS:(dc + 1) * TS],
                          xn[:, dc * TS:(dc + 1) * TS])
            cast_v.append(vc)
        self.qwait("DVE", cast_v[-1])
        for dc in range(DC):
            self.op("DVE", "dve", lambda e, dc=dc: e.tensor_sub(
                xne[:, dc * TS:(dc + 1) * TS], xn[:, dc * TS:(dc + 1) * TS],
                xnr[:, dc * TS:(dc + 1) * TS].bitcast(F32)))
        v_split = self.cnt["dve"]

        w1, w3 = self.i_w["w1"], self.i_w["w3"]
        wt1 = [self.sb("f1t", [128, 512], F32, self.A + 2048 * i) for i in range(3)]
        wr1 = [self.sb("f1r", [128, 512], F32R, self.A + 6144 + 2048 * i) for i in range(3)]
        we1 = [self.sb("f1e", [128, 512], F32R, self.A + 12288 + 2048 * i) for i in range(3)]
        ug = self.sb("fug", [128, 8 * 512], F32, self.C)
        rb = [self.sb("fhr", [128, 512], F32R, self.C + 16384 + 2048 * i) for i in range(2)]
        self.wait("TE", "dve", v_split)
        it = 0
        mmrec = {}
        ug_evac_v = 0
        mul_v = 0
        st = {"prev_sub": None, "latest": None, "last_cast": None}
        pending = []

        def mkstep(hg, j, vmul):
            def step():
                hc, t = divmod(j, 2)
                bsl = ug[:, j * 512:(j + 1) * 512]
                rbb = rb[j % 2]
                row0 = (hg * 4 + hc) * 128
                if st["prev_sub"] is not None:
                    self.wait("POOL", "dve", st["prev_sub"])
                self.wait("POOL", "dve", vmul)
                if st["latest"] is not None:
                    self.qwait("POOL", st["latest"])
                vc = self.dma("POOL", "pc0", rbb[:, :], bsl)
                st["last_cast"] = vc
                self.qwait("DVE", vc)
                vs = self.op("DVE", "dve", lambda e, b=bsl, rbb=rbb: e.tensor_sub(
                    b, b, rbb[:, :].bitcast(F32)))
                st["prev_sub"] = vs
                self.wait("POOL", "dve", vs)
                self.dma("POOL", "ps0",
                         self.d_hr[row0:row0 + 128, t * 512:(t + 1) * 512], rbb[:, :])
                vo = self.dma("POOL", "ps0",
                              self.d_he[row0:row0 + 128, t * 512:(t + 1) * 512],
                              bsl.bitcast(F32R))
                st["latest"] = vo
            return step

        for hg in range(16):
            for wi, w in ((0, w1), (1, w3)):
                for dc in range(DC):
                    r = it % 3
                    if it >= 3:
                        self.wait("SP", "mm", mmrec[it - 3])
                    else:
                        self.wait("SP", "mm", self.cnt["mm"])
                    vload = self.dma("SP", f"w{r}", wt1[r][:, :],
                                     w[dc * 128:(dc + 1) * 128, hg * 512:(hg + 1) * 512])
                    self.qwait("POOL", vload)
                    if it >= 3:
                        self.wait("POOL", "mm", mmrec[it - 3])
                    vcast = self.dma("POOL", f"wc{r}", wr1[r][:, :], wt1[r][:, :])
                    self.qwait("DVE", vcast)
                    vsub = self.op("DVE", "dve", lambda e, r=r: e.tensor_sub(
                        we1[r][:, :], wt1[r][:, :], wr1[r][:, :].bitcast(F32)))
                    self.wait("TE", "dve", vsub)
                    if dc == 0 and (hg > 0 or wi > 0):
                        if wi == 0:
                            self.wait("TE", "dve", mul_v)
                        else:
                            self.wait("TE", "act", ug_evac_v)
                    for hc in range(4):
                        for t in range(2):
                            pb = self.P[hc * 2 + t]
                            mv_r = xnr[:, dc * TS + t * 512: dc * TS + t * 512 + 512]
                            mv_e = xne[:, dc * TS + t * 512: dc * TS + t * 512 + 512]
                            lr = wr1[r][:, hc * 128:(hc + 1) * 128]
                            le = we1[r][:, hc * 128:(hc + 1) * 128]
                            stt = (dc == 0)
                            stp = (dc == DC - 1)
                            last = hc == 3 and t == 1
                            self.op("TE", "mm", lambda e, pb=pb, l=lr, m=mv_r, s=stt: e.matmul(
                                pb[:, :], l, m, start=s, stop=False), inc=False)
                            self.op("TE", "mm", lambda e, pb=pb, l=lr, m=mv_e: e.matmul(
                                pb[:, :], l, m, start=False, stop=False), inc=False)
                            self.op("TE", "mm", lambda e, pb=pb, l=le, m=mv_r, p=stp: e.matmul(
                                pb[:, :], l, m, start=False, stop=p), inc=last)
                    mmrec[it] = self.cnt["mm"]
                    it += 1
                    if dc % 2 == 1 and pending:
                        pending.pop(0)()
                v_grp = self.cnt["mm"]
                if wi == 0:
                    while pending:
                        pending.pop(0)()
                    self.wait("ACT", "mm", v_grp)
                    if st["last_cast"] is not None:
                        self.qwait("ACT", st["last_cast"])
                    if st["latest"] is not None:
                        self.qwait("ACT", st["latest"])
                    for j in range(8):
                        self.op("ACT", "act", lambda e, pb=self.P[j], j=j: e.activation(
                            out=ug[:, j * 512:(j + 1) * 512], in_=pb[:, :], func=AF.Silu),
                            inc=(j == 7))
                    ug_evac_v = self.cnt["act"]
                else:
                    self.wait("DVE", "mm", v_grp)
                    self.wait("DVE", "act", ug_evac_v)
                    vmuls = []
                    for j in range(8):
                        self.wait("DVE", "dve", self.cnt["dve"])
                        vm = self.op("DVE", "dve", lambda e, pb=self.P[j], j=j: e.tensor_tensor(
                            ug[:, j * 512:(j + 1) * 512], ug[:, j * 512:(j + 1) * 512],
                            pb[:, :], ALU.mult), inc=True)
                        vmuls.append(vm)
                    mul_v = self.cnt["dve"]
                    pending = [mkstep(hg, j, vmuls[j]) for j in range(8)]
        while pending:
            pending.pop(0)()

    def ph_fwht(self):
        TT = 256
        for tt in range(4):
            self._fwht_tile(tt, TT)

    def _fwht_tile(self, tt, TT):
        BIG = self.BIG
        a = self.sb("wa", [128, HC * TT], F32, BIG)
        bq = self.sb("wb", [128, HC * TT], F32, BIG + 65536)
        bqr = self.sb("wbr", [128, HC * TT], F32R, BIG + 65536)
        hcr = [self.sb("whr", [128, 8 * TT], F32R, self.A + 8192 * i) for i in range(2)]
        hce = [self.sb("whe", [128, 8 * TT], F32R, self.C + 8192 * i) for i in range(2)]

        cols = (tt * TT, (tt + 1) * TT)
        self.qwait("ACT", self.qtotal("ys"))
        self.qwait("ACT", self.qtotal("dbg"))
        self.wait("TE", "act", self.cnt["act"])
        ev = {}
        ld = {}
        for g in range(8):
            r = g % 2
            if g >= 2:
                self.wait("SP", "mm", ld[g - 2])
            else:
                self.wait("SP", "mm", self.cnt["mm"])
            rows = slice(g * 1024, (g + 1) * 1024)
            v1 = self.dma("SP", f"f{r}", self.fs(hcr[r][:, :], TT),
                          self.fm(self.d_hr[rows, cols[0]:cols[1]]))
            v2 = self.dma("SP", f"f{r}", self.fs(hce[r][:, :], TT),
                          self.fm(self.d_he[rows, cols[0]:cols[1]]))
            self.qwait("TE", v2)
            for k in range(8):
                c = g * 8 + k
                pb = self.P[c % 8]
                if c >= 8:
                    self.wait("TE", "act", ev[c - 8])
                self.op("TE", "mm", lambda e, pb=pb, r=r, k=k: e.matmul(
                    pb[:, 0:TT], self.h128r[:, :], hcr[r][:, k * TT:(k + 1) * TT],
                    start=True, stop=False), inc=False)
                v = self.op("TE", "mm", lambda e, pb=pb, r=r, k=k: e.matmul(
                    pb[:, 0:TT], self.h128r[:, :], hce[r][:, k * TT:(k + 1) * TT],
                    start=False, stop=True))
                self.wait("ACT", "mm", v)
                ev[c] = self.op("ACT", "act", lambda e, pb=pb, c=c: e.activation(
                    out=a[:, c * TT:(c + 1) * TT], in_=pb[:, 0:TT], func=AF.Copy,
                    scale=FWHT_SCALE))
            ld[g] = self.cnt["mm"]
        v_y = self.cnt["act"]
        self.wait("DVE", "act", v_y)
        self.wait("POOL", "act", v_y)
        src_, dst_ = a, bq
        for lev in range(6):
            stq = 1 << lev
            span = stq * TT
            L2 = (2 * span) // 3 // 32 * 32
            def mk(which, src=src_, dst=dst_, span=span, L2=L2):
                def f(e):
                    o = dst[:, :].rearrange("p (g two s) -> p g two s", two=2, s=span)
                    i0 = src[:, :].rearrange("p (g two s) -> p g two s", two=2, s=span)
                    if which == "add":
                        return e.tensor_tensor(o[:, :, 0, :], i0[:, :, 0, :],
                                               i0[:, :, 1, :], ALU.add)
                    if which == "sub_pool":
                        return e.tensor_tensor(o[:, :, 1, 0:L2], i0[:, :, 0, 0:L2],
                                               i0[:, :, 1, 0:L2], ALU.subtract)
                    return e.tensor_tensor(o[:, :, 1, L2:span], i0[:, :, 0, L2:span],
                                           i0[:, :, 1, L2:span], ALU.subtract)
                return f
            self.wait("DVE", "dve", self.cnt["dve"])
            self.wait("POOL", "pool", self.cnt["pool"])
            v0 = self.op("DVE", "dve", mk("add"), inc=False)
            v0 = self.op("DVE", "dve", mk("sub_dve"))
            v1 = self.op("POOL", "pool", mk("sub_pool"))
            self.wait("DVE", "pool", v1)
            self.wait("POOL", "dve", v0)
            src_, dst_ = dst_, src_
        v_res_dve = self.cnt["dve"]
        v_res_pool = self.cnt["pool"]
        self.wait("ACT", "dve", v_res_dve)
        self.wait("ACT", "pool", v_res_pool)
        vc = self.op("ACT", "act", lambda e: e.activation(
            out=bqr[:, :], in_=a[:, :], func=AF.Copy))
        self.wait("DVE", "act", vc)
        vs = self.op("DVE", "dve", lambda e: e.tensor_sub(
            a[:, :], a[:, :], bqr[:, :].bitcast(F32)))
        self.wait("POOL", "dve", vs)
        if self.st["debug"] and tt == 0:
            vd = self.dma("POOL", "dbg", self.d_ybf[:, :HC * TT],
                          a[:, :])
            self.qwait("POOL", vd)
        self.dma("POOL", "ys", self.fm(self.d_yr[:, cols[0]:cols[1]]),
                 self.fs(bqr[:, :], TT))
        self.dma("POOL", "ys", self.fm(self.d_ye[:, cols[0]:cols[1]]),
                 self.fs(a[:, :].bitcast(F32R), TT))

    def ph_w2(self):
        w2 = self.i_w["w2"]
        wt = [self.sb("w2t", [128, 1024], F32, self.C + 4096 * i) for i in range(3)]
        wr = [self.sb("w2r", [128, 1024], F32R, self.C + 12288 + 4096 * i) for i in range(3)]
        we = [self.sb("w2e", [128, 1024], F32R, self.B + 4096 * i) for i in range(3)]
        yr = [self.sb("w2yr", [128, 512], F32R, self.A + 2048 * i) for i in range(3)]
        ye = [self.sb("w2ye", [128, 512], F32R, self.A + 6144 + 2048 * i) for i in range(3)]
        g2w = self.sb("w2g", [128, 8 * 512], F32, self.BIG)
        mmrec = {}
        it = 0
        evac_v = 0
        st = {"latest": None}
        pending = []

        def mkstep(t, og, oc, va):
            def step():
                row0 = (og * 8 + oc) * 128
                self.wait("POOL", "act", va)
                vo = self.dma("POOL", "gs0",
                              self.d_g2[row0:row0 + 128, t * 512:(t + 1) * 512],
                              g2w[:, oc * 512:(oc + 1) * 512])
                st["latest"] = vo
            return step

        for t in range(2):
            for og in range(2):
                for hc in range(HC):
                    r = it % 3
                    if it >= 3:
                        self.wait("SP", "mm", mmrec[it - 3])
                    else:
                        self.wait("SP", "mm", self.cnt["mm"])
                    vl = self.dma("SP", f"w{r}", wt[r][:, :],
                                  w2[hc * 128:(hc + 1) * 128, og * 1024:(og + 1) * 1024])
                    vy1 = self.dma("SP", f"w{r}", yr[r][:, :],
                                   self.d_yr[hc * 128:(hc + 1) * 128, t * 512:(t + 1) * 512])
                    vy2 = self.dma("SP", f"w{r}", ye[r][:, :],
                                   self.d_ye[hc * 128:(hc + 1) * 128, t * 512:(t + 1) * 512])
                    self.qwait("POOL", vy2)
                    if it >= 3:
                        self.wait("POOL", "mm", mmrec[it - 3])
                    vcast = self.dma("POOL", f"wc{r}", wr[r][:, :], wt[r][:, :])
                    self.qwait("DVE", vcast)
                    vsub = self.op("DVE", "dve", lambda e, r=r: e.tensor_sub(
                        we[r][:, :], wt[r][:, :], wr[r][:, :].bitcast(F32)))
                    self.wait("TE", "dve", vsub)
                    self.qwait("TE", vy2)
                    if hc == 0 and it > 0:
                        self.wait("TE", "act", evac_v)
                    for oc in range(8):
                        pb = self.P[oc]
                        lr = wr[r][:, oc * 128:(oc + 1) * 128]
                        le = we[r][:, oc * 128:(oc + 1) * 128]
                        stt = (hc == 0)
                        stp = (hc == HC - 1)
                        last = oc == 7
                        self.op("TE", "mm", lambda e, pb=pb, l=lr, m=yr[r], s=stt: e.matmul(
                            pb[:, :], l, m[:, :], start=s, stop=False), inc=False)
                        self.op("TE", "mm", lambda e, pb=pb, l=lr, m=ye[r]: e.matmul(
                            pb[:, :], l, m[:, :], start=False, stop=False), inc=False)
                        self.op("TE", "mm", lambda e, pb=pb, l=le, m=yr[r], p=stp: e.matmul(
                            pb[:, :], l, m[:, :], start=False, stop=p), inc=last)
                    mmrec[it] = self.cnt["mm"]
                    it += 1
                    if it % 8 == 0 and pending:
                        pending.pop(0)()
                while pending:
                    pending.pop(0)()
                v_grp = self.cnt["mm"]
                self.wait("ACT", "mm", v_grp)
                if st["latest"] is not None:
                    self.qwait("ACT", st["latest"])
                for oc in range(8):
                    self.op("ACT", "act", lambda e, pb=self.P[oc], oc=oc: e.activation(
                        out=g2w[:, oc * 512:(oc + 1) * 512], in_=pb[:, :], func=AF.Silu),
                        inc=(oc == 7))
                evac_v = self.cnt["act"]
                pending = [mkstep(t, og, oc, evac_v) for oc in range(8)]
        while pending:
            pending.pop(0)()

    def ph_attn(self, qi):
        BIG = self.BIG
        nkb = NKB[qi]
        qr = self.sb("aqr", [128, DC * 512], F32R, BIG)
        qe = self.sb("aqe", [128, DC * 512], F32R, BIG + 32768)
        stor = self.sb("astr", [128, 16 * 512], F32R, BIG + 65536)
        stoe = self.sb("aste", [128, 16 * 512], F32R, BIG + 98304)
        kbr = [self.sb("akbr", [128, DC * 128], F32R, self.A + 8192 * i) for i in range(2)]
        kbe = [self.sb("akbe", [128, DC * 128], F32R, self.C + 8192 * i) for i in range(2)]
        mbuf = [self.sb("amsk", [128, 512], F32, self.M + 1600 + 2048 * i) for i in range(2)]
        sbnc = [self.sb("asb", [128, 512], F32, self.B + 2048 * i) for i in range(2)]
        srb = [self.sb("asr", [128, 512], F32R, self.B + 4096 + 2048 * i) for i in range(2)]

        cq = qi * 512
        v1 = self.dma("SP", "qq", self.fs(qr[:, :], 512),
                      self.fm(self.d_qr[:, cq:cq + 512]))
        v2 = self.dma("SP", "qq", self.fs(qe[:, :], 512),
                      self.fm(self.d_qe[:, cq:cq + 512]))
        self.qwait("TE", v2)
        kload = {}
        mload = {}
        st_done = {}
        for j, cb in enumerate(KBLIST[qi]):
            r = j % 2
            if j >= 2:
                self.wait("SP", "mm", st_done[j - 2][0])
            else:
                self.wait("SP", "mm", self.cnt["mm"])
            lk1 = self.dma("SP", f"kb{r}", self.fs(kbr[r][:, :], 128),
                           self.fm(self.d_kr[:, cb * 128:(cb + 1) * 128]))
            lk2 = self.dma("SP", f"kb{r}", self.fs(kbe[r][:, :], 128),
                           self.fm(self.d_ke[:, cb * 128:(cb + 1) * 128]))
            mi = (8 if qi else 0) + j if qi == 0 else 8 + j
            mi = j if qi == 0 else 8 + j
            if j >= 2:
                self.wait("ACT", "dve", st_done[j - 2][1])
            else:
                self.wait("ACT", "mm", self.cnt["mm"])
            lm = self.dma("ACT", f"mk{r}", mbuf[r][:, :], self.i_masks[mi, :, :])
            self.qwait("TE", lk2)
            pb = self.P[j % 2]
            if j >= 2:
                self.wait("TE", "dve", st_done[j - 2][1])
            for dc in range(DC):
                lr = kbr[r][:, dc * 128:(dc + 1) * 128]
                le = kbe[r][:, dc * 128:(dc + 1) * 128]
                mr = qr[:, dc * 512:(dc + 1) * 512]
                me = qe[:, dc * 512:(dc + 1) * 512]
                stt = (dc == 0)
                stp = (dc == DC - 1)
                self.op("TE", "mm", lambda e, pb=pb, l=lr, m=mr, s=stt: e.matmul(
                    pb[:, :], l, m, start=s, stop=False), inc=False)
                self.op("TE", "mm", lambda e, pb=pb, l=lr, m=me: e.matmul(
                    pb[:, :], l, m, start=False, stop=False), inc=False)
                self.op("TE", "mm", lambda e, pb=pb, l=le, m=mr, p=stp: e.matmul(
                    pb[:, :], l, m, start=False, stop=p), inc=stp)
            v_s = self.cnt["mm"]
            self.wait("DVE", "mm", v_s)
            self.qwait("DVE", lm)
            b = sbnc[r]
            rbb = srb[r]
            if j >= 2:
                self.qwait("DVE", st_done[j - 2][2])
            vm = self.op("DVE", "dve", lambda e, pb=pb, b=b, m=mbuf[r]: e.tensor_tensor(
                b[:, :], pb[:, :], m[:, :], ALU.mult), inc=True)
            self.wait("POOL", "dve", vm)
            vc = self.dma("POOL", "ogc", stor[:, j * 512:(j + 1) * 512], b[:, :])
            self.qwait("DVE", vc)
            vs = self.op("DVE", "dve", lambda e, b=b, j=j: e.tensor_sub(
                stoe[:, j * 512:(j + 1) * 512], b[:, :],
                stor[:, j * 512:(j + 1) * 512].bitcast(F32)))
            st_done[j] = (v_s, vs, vc)
        v_scores = self.cnt["dve"]

        ogr = self.sb("aogr", [128, DC * 512], F32R, BIG)
        oge = self.sb("aoge", [128, DC * 512], F32R, BIG + 32768)
        vbr = [self.sb("avbr", [128, 4 * 512], F32R, self.A + 8192 * i) for i in range(2)]
        vbe = [self.sb("avbe", [128, 4 * 512], F32R, self.C + 8192 * i) for i in range(2)]
        gb = [self.sb("agb", [128, 512], F32, self.B + 8192 + 2048 * i) for i in range(2)]
        ob = [self.sb("aob", [128, 512], F32, self.B + 2048 * i) for i in range(2)]
        orb = [self.sb("aor", [128, 512], F32R, self.B + 4096 + 2048 * i) for i in range(2)]
        self.wait("SP", "mm", self.cnt["mm"])
        self.wait("ACT", "mm", self.cnt["mm"])
        self.wait("TE", "dve", v_scores)
        gload = {}
        og_ev = 0
        o_ev = [None, None]
        guse = [0, 0]
        ngrp = nkb // 4
        for og in range(4):
            for j, cb in enumerate(KBLIST[qi]):
                r = (j // 4) % 2
                if j % 4 == 0:
                    gidx = og * ngrp + j // 4
                    if gidx >= 2:
                        self.wait("SP", "mm", gload[gidx - 2])
                    else:
                        self.wait("SP", "mm", self.cnt["mm"])
                    grow = cb * 128
                    l1 = self.dma("SP", f"vb{r}", self.fs(vbr[r][:, :], 512),
                                  self.fm(self.d_vr[grow:grow + 512,
                                                    og * 512:(og + 1) * 512]))
                    l2 = self.dma("SP", f"vb{r}", self.fs(vbe[r][:, :], 512),
                                  self.fm(self.d_ve[grow:grow + 512,
                                                    og * 512:(og + 1) * 512]))
                    self.qwait("TE", l2)
                if j == 0 and og > 0:
                    self.wait("TE", "dve", og_ev)
                for oc in range(4):
                    pb = self.P[4 * (og % 2) + oc]
                    lr = vbr[r][:, (j % 4) * 512 + oc * 128:(j % 4) * 512 + oc * 128 + 128]
                    le = vbe[r][:, (j % 4) * 512 + oc * 128:(j % 4) * 512 + oc * 128 + 128]
                    mr = stor[:, j * 512:(j + 1) * 512]
                    me = stoe[:, j * 512:(j + 1) * 512]
                    stt = (j == 0)
                    stp = (j == nkb - 1)
                    last = oc == 3
                    self.op("TE", "mm", lambda e, pb=pb, l=lr, m=mr, s=stt: e.matmul(
                        pb[:, :], l, m, start=s, stop=False), inc=False)
                    self.op("TE", "mm", lambda e, pb=pb, l=le, m=mr: e.matmul(
                        pb[:, :], l, m, start=False, stop=False), inc=False)
                    self.op("TE", "mm", lambda e, pb=pb, l=lr, m=me, p=stp: e.matmul(
                        pb[:, :], l, m, start=False, stop=p), inc=last)
                if j % 4 == 3:
                    gload[og * ngrp + j // 4] = self.cnt["mm"]
            v_og = self.cnt["mm"]
            self.wait("DVE", "mm", v_og)
            for oc in range(4):
                ch = og * 4 + oc
                b = ob[oc % 2]
                rbb = orb[oc % 2]
                row0 = ch * 128
                if ch >= 2:
                    self.wait("ACT", "dve", guse[oc % 2])
                else:
                    self.wait("ACT", "mm", self.cnt["mm"])
                lg = self.dma("ACT", f"gl{oc % 2}", gb[oc % 2][:, :],
                              self.d_g[row0:row0 + 128, cq:cq + 512])
                self.qwait("DVE", lg)
                if ch >= 2:
                    self.qwait("DVE", o_ev[oc % 2])
                vm = self.op("DVE", "dve", lambda e, pb=self.P[4 * (og % 2) + oc], b=b,
                        g=gb[oc % 2]: e.tensor_tensor(b[:, :], pb[:, :], g[:, :],
                                                      ALU.mult), inc=True)
                guse[oc % 2] = vm
                self.wait("POOL", "dve", vm)
                vc = self.dma("POOL", "ogc", ogr[:, ch * 512:(ch + 1) * 512], b[:, :])
                o_ev[oc % 2] = vc
                self.qwait("DVE", vc)
                self.op("DVE", "dve", lambda e, b=b, ch=ch: e.tensor_sub(
                    oge[:, ch * 512:(ch + 1) * 512], b[:, :],
                    ogr[:, ch * 512:(ch + 1) * 512].bitcast(F32)))
            og_ev = self.cnt["dve"]
        v_sv = self.cnt["dve"]

        self.wait("SP", "mm", self.cnt["mm"])
        wp = self.i_w["wp"]
        wt = [self.sb("apt", [128, 1024], F32, self.C + 4096 * i) for i in range(2)]
        wr = [self.sb("apr", [128, 1024], F32R, self.C + 8192 + 4096 * i) for i in range(2)]
        we = [self.sb("ape", [128, 1024], F32R, self.C + 16384 + 4096 * i) for i in range(2)]
        xg = self.sb("axg", [128, DC * 512], F32, BIG + 65536)
        g2t = self.sb("ag2", [128, DC * 512], F32, BIG + 98304)
        obo = [self.sb("aout", [128, 512], F32, self.B + 2048 * i) for i in range(4)]
        self.wait("ACT", "mm", self.cnt["mm"])
        vxg = self.dma("ACT", "xg", self.fs(xg[:, :], 512),
                       self.fm(self.i_x[:, cq:cq + 512]))
        vg2 = self.dma("ACT", "xg", self.fs(g2t[:, :], 512),
                       self.fm(self.d_g2[:, cq:cq + 512]))
        self.wait("TE", "dve", v_sv)
        mmrec = {}
        it = 0
        evac_v = 0
        out_dma = [None, None, None, None]
        for og2 in range(2):
            for dc in range(DC):
                r = it % 2
                if it >= 2:
                    self.wait("SP", "mm", mmrec[it - 2])
                else:
                    self.wait("SP", "mm", self.cnt["mm"])
                vl = self.dma("SP", f"w{r}", wt[r][:, :],
                              wp[dc * 128:(dc + 1) * 128, og2 * 1024:(og2 + 1) * 1024])
                self.qwait("POOL", vl)
                if it >= 2:
                    self.wait("POOL", "mm", mmrec[it - 2])
                vcast = self.dma("POOL", f"wc{r}", wr[r][:, :], wt[r][:, :])
                self.qwait("DVE", vcast)
                vsub = self.op("DVE", "dve", lambda e, r=r: e.tensor_sub(
                    we[r][:, :], wt[r][:, :], wr[r][:, :].bitcast(F32)))
                self.wait("TE", "dve", vsub)
                if dc == 0 and og2 > 0:
                    self.wait("TE", "dve", evac_v)
                for oc in range(8):
                    pb = self.P[oc]
                    lr = wr[r][:, oc * 128:(oc + 1) * 128]
                    le = we[r][:, oc * 128:(oc + 1) * 128]
                    mr = ogr[:, dc * 512:(dc + 1) * 512]
                    me = oge[:, dc * 512:(dc + 1) * 512]
                    stt = (dc == 0)
                    stp = (dc == DC - 1)
                    last = oc == 7
                    self.op("TE", "mm", lambda e, pb=pb, l=lr, m=mr, s=stt: e.matmul(
                        pb[:, :], l, m, start=s, stop=False), inc=False)
                    self.op("TE", "mm", lambda e, pb=pb, l=lr, m=me: e.matmul(
                        pb[:, :], l, m, start=False, stop=False), inc=False)
                    self.op("TE", "mm", lambda e, pb=pb, l=le, m=mr, p=stp: e.matmul(
                        pb[:, :], l, m, start=False, stop=p), inc=last)
                mmrec[it] = self.cnt["mm"]
                it += 1
            v_grp = self.cnt["mm"]
            self.wait("DVE", "mm", v_grp)
            self.qwait("DVE", vg2)
            for oc in range(8):
                ch = og2 * 8 + oc
                b = obo[oc % 4]
                if ch >= 4:
                    self.qwait("DVE", out_dma[oc % 4])
                vmul = self.op("DVE", "dve", lambda e, pb=self.P[oc], b=b, ch=ch: e.tensor_tensor(
                    b[:, :], pb[:, :], g2t[:, ch * 512:(ch + 1) * 512], ALU.mult),
                    inc=True)
                self.wait("DVE", "dve", vmul)
                vm = self.op("DVE", "dve", lambda e, b=b, ch=ch: e.tensor_tensor(
                    b[:, :], b[:, :], xg[:, ch * 512:(ch + 1) * 512], ALU.add))
                self.wait("ACT", "dve", vm)
                row0 = ch * 128
                vo = self.dma("ACT", f"o{oc % 4}",
                              self.o_out[row0:row0 + 128, cq:cq + 512], b[:, :])
                out_dma[oc % 4] = vo
            evac_v = self.cnt["dve"]

    def build(self):
        self.ph0()
        if self.st["p1"]:
            self.fence()
            self.ph1_slab(1)
            self.fence()
            self.ph1_slab(0)
        if self.st["ffn"]:
            self.fence()
            self.ph_ffn()
        if self.st["fwht"]:
            self.fence()
            self.ph_fwht()
        if self.st["w2"]:
            self.fence()
            self.ph_w2()
        if self.st["attn"]:
            self.fence()
            self.ph_attn(0)
            self.fence()
            self.ph_attn(1)
        self.fence()
        for s in ("o0", "o1", "o2", "o3"):
            self.qwait("ACT", self.qtotal(s))
            self.qwait("POOL", self.qtotal(s))
        with self.nc.Block() as block:
            @block.tensor
            def _(e):
                for f in self.ops["TE"]:
                    f(e)

            @block.scalar
            def _(e):
                for f in self.ops["ACT"]:
                    f(e)

            @block.vector
            def _(e):
                for f in self.ops["DVE"]:
                    f(e)

            @block.gpsimd
            def _(e):
                for f in self.ops["POOL"]:
                    f(e)

            @block.sync
            def _(e):
                for f in self.ops["SP"]:
                    f(e)
        self.ctx.close()
        return self.nc


def _hadamard(n):
    h = np.array([[1.0]], dtype=np.float32)
    while h.shape[0] < n:
        h = np.block([[h, h], [h, -h]])
    return h.astype(np.float32)


def _staging(core, x):
    b, h = core // 2, core % 2
    own = np.concatenate([np.arange((2 * m + h) * 128, (2 * m + h) * 128 + 128)
                          for m in range(8)])
    par = np.concatenate([np.arange((2 * m + 1 - h) * 128, (2 * m + 1 - h) * 128 + 128)
                          for m in range(8)])
    perm = np.concatenate([own, par])
    xT = np.ascontiguousarray(x[b][perm, :].T)
    gidx = perm
    masks = np.zeros((24, 128, 512), dtype=np.float32)
    row = 0
    for qi in (0, 1):
        gq = gidx[qi * 512: qi * 512 + 512]
        for cb in KBLIST[qi]:
            gk = gidx[cb * 128:(cb + 1) * 128]
            masks[row] = (gk[:, None] <= gq[None, :]).astype(np.float32)
            row += 1
    return xT, masks, own


def kernel(L0_attn_x, L0_attn_Wq, L0_attn_Wk, L0_attn_Wv, L0_attn_Wg, L0_attn_Wp,
           L0_ffn_W1, L0_ffn_W2, L0_ffn_W3):
    x = np.ascontiguousarray(np.asarray(L0_attn_x, dtype=np.float32))
    ws = {
        "wqT": np.ascontiguousarray(np.asarray(L0_attn_Wq).T),
        "wkT": np.ascontiguousarray(np.asarray(L0_attn_Wk).T),
        "wvT": np.ascontiguousarray(np.asarray(L0_attn_Wv).T),
        "wgT": np.ascontiguousarray(np.asarray(L0_attn_Wg).T),
        "wpT": np.ascontiguousarray(np.asarray(L0_attn_Wp).T),
        "w1T": np.ascontiguousarray(np.asarray(L0_ffn_W1).T),
        "w3T": np.ascontiguousarray(np.asarray(L0_ffn_W3).T),
        "w2T": np.ascontiguousarray(np.asarray(L0_ffn_W2).T),
    }
    had = _hadamard(128)
    in_maps = []
    owns = []
    for c in range(8):
        xT, masks, own = _staging(c, x)
        owns.append(own)
        m = {"xT": xT, "masks": masks, "had128": had}
        m.update(ws)
        in_maps.append(m)

    nc = Builder().build()
    res = run_bass_kernel_spmd(nc, in_maps, core_ids=list(range(8)))
    out = np.empty((4, 2048, 2048), dtype=np.float32)
    for c in range(8):
        b = c // 2
        out[b][owns[c], :] = res.results[c]["outT"].T
    return out
